# version 08
# baseline (speedup 1.0000x reference)
import numpy as np
import ml_dtypes
from contextlib import ExitStack

import concourse.bass as bass
import concourse.mybir as mybir
from concourse.bass_utils import run_bass_kernel_spmd

F32 = mybir.dt.float32
BF16 = mybir.dt.bfloat16
AF = mybir.ActivationFunctionType
ALU = mybir.AluOpType
AX = mybir.AxisListType.X

T = 8192
D = 1024
NT = T // 128
DC = D // 128
NSA = 8
NSB = 16
NEG = -1e30
CLIP = 128.0 - 1e-5

ENGINES = ("sync", "tensor", "vector", "scalar", "gpsimd")

QPOOL = {"sync": 6, "gpsimd": 5, "scalar": 4}
QSEMS = {e: [f"q_{e}{i}" for i in range(n)] for e, n in QPOOL.items()}


class Sched:
    def __init__(self, sems, counts, waited, qrot):
        self.sems = sems
        self.cnt = counts
        self.waited = waited
        self.qrot = qrot
        self.ops = {e: [] for e in ENGINES}

    def op(self, eng, fn, waits=(), inc=None, inc_by=1, asynchronous=False,
           drain=False):
        self.ops[eng].append((fn, [w for w in waits if w is not None],
                              (inc, inc_by) if inc else None, drain))
        if inc:
            self.cnt[inc] += inc_by
            return (eng, inc, self.cnt[inc], asynchronous)
        return None

    def dma(self, eng, fn, waits=(), sem=None):
        pool = QSEMS[eng]
        s = pool[self.qrot[eng] % len(pool)]
        self.qrot[eng] += 1
        ws = list(waits)
        if self.cnt[s]:
            ws.append((eng, s, self.cnt[s], True))
        return self.op(eng, fn, ws, inc=s, inc_by=16, asynchronous=True)

    def queue_drained(self, eng):
        return [(eng, s, self.cnt[s], True) for s in QSEMS[eng] if self.cnt[s]]

    def emit(self, block):
        for name in ENGINES:
            ops = self.ops[name]
            if not ops:
                continue
            waited = self.waited.setdefault(name, {})
            sems = self.sems

            def body(eng, ops=ops, waited=waited, name=name):
                for fn, waits, inc, drain in ops:
                    for (peng, sname, val, is_async) in waits:
                        if peng == name and not is_async and name != "tensor":
                            continue
                        if waited.get(sname, -1) >= val:
                            continue
                        eng.wait_ge(sems[sname], val)
                        waited[sname] = val
                    inst = fn(eng)
                    if drain:
                        inst = eng.drain()
                    if inc is not None:
                        inst.then_inc(sems[inc[0]], inc[1])

            getattr(block, name)(body)


def build_program():
    nc = bass.Bass("TRN2", target_bir_lowering=False)

    x_h = nc.dram_tensor("x", [T, D], F32, kind="ExternalInput").ap()
    wqT_h = nc.dram_tensor("wqT", [D, D], F32, kind="ExternalInput").ap()
    wkT_h = nc.dram_tensor("wkT", [D, D], F32, kind="ExternalInput").ap()
    wvT_h = nc.dram_tensor("wvT", [D, D], F32, kind="ExternalInput").ap()
    woT_h = nc.dram_tensor("woT", [D, D], F32, kind="ExternalInput").ap()
    wf_h = nc.dram_tensor("wf", [D, D], F32, kind="ExternalInput").ap()
    gam_h = nc.dram_tensor("gam", [1, D], F32, kind="ExternalInput").ap()
    bet_h = nc.dram_tensor("bet", [1, D], F32, kind="ExternalInput").ap()
    identb_h = nc.dram_tensor("identb", [128, 128], BF16, kind="ExternalInput").ap()
    mask_h = nc.dram_tensor("mask", [128, 128], F32, kind="ExternalInput").ap()
    out_h = nc.dram_tensor("out", [T, D], F32, kind="ExternalOutput").ap()

    Qs = nc.dram_tensor("Qs", [D, T], BF16, kind="Internal").ap()
    Ks = nc.dram_tensor("Ks", [D, T], BF16, kind="Internal").ap()
    Vs = nc.dram_tensor("Vs", [T, D], BF16, kind="Internal").ap()
    X1s = nc.dram_tensor("X1s", [T, D], F32, kind="Internal").ap()
    hbounce = nc.dram_tensor("hbounce", [1, 128], F32, kind="Internal").ap()
    smd = nc.dram_tensor("smd", [1, 8], F32, kind="Internal").ap()
    cc_in = nc.dram_tensor("cc_in", [1, 512], F32, kind="Internal")
    cc_out = nc.dram_tensor("cc_out", [1, 512], F32, kind="Internal")

    with ExitStack() as P:
        def sb(name, shape, dtype, stack):
            return stack.enter_context(nc.sbuf_tensor(name, shape, dtype))

        identb = sb("identb_s", [128, 128], BF16, P)
        mask = sb("mask_s", [128, 128], F32, P)
        gam_bc = sb("gam_bc", [128, D], F32, P)
        bet_bc = sb("bet_bc", [128, D], F32, P)
        w_tern = sb("w_tern", [128, DC * D], BF16, P)
        woT_b = sb("woT_b", [128, DC * D], BF16, P)
        scal = sb("scal", [128, 8], F32, P)
        sm = sb("sm", [1, 8], F32, P)
        cc_sb = sb("cc_sb", [1, 512], F32, P)
        redrow = sb("redrow", [1, 128], F32, P)
        epsc = sb("epsc", [128, 2], F32, P)

        ps_a = P.enter_context(nc.psum_tensor("ps_a", [128, 1024], F32))
        ps_b = P.enter_context(nc.psum_tensor("ps_b", [128, 1024], F32))
        ps_x = P.enter_context(nc.psum_tensor("ps_x", [128, 1024], F32))
        ps_t0 = P.enter_context(nc.psum_tensor("ps_t0", [128, 1024], BF16))
        ps_t1 = P.enter_context(nc.psum_tensor("ps_t1", [128, 1024], BF16))
        ps_t = (ps_t0, ps_t1)

        semnames = (["pe", "dv", "ac", "cc", "bar"]
                    + [s for pool in QSEMS.values() for s in pool])
        sems = {n: P.enter_context(nc.semaphore(n)) for n in semnames}
        counts = {n: 0 for n in semnames}
        waited = {}
        qrot = {e: 0 for e in QPOOL}

        with ExitStack() as PA:
            wqT_b = sb("wqT_b", [128, DC * D], BF16, PA)
            wkT_b = sb("wkT_b", [128, DC * D], BF16, PA)
            wvT_b = sb("wvT_b", [128, DC * D], BF16, PA)
            stg = sb("stg", [128, 2 * D], F32, PA)
            x_in = sb("x_in", [128, 2 * D], F32, PA)
            sq_scr = sb("sq_scr", [128, 2 * D], F32, PA)
            stats = sb("stats", [128, 32], F32, PA)
            xn_tm = sb("xn_tm", [128, 2 * D], BF16, PA)
            xn_fm = sb("xn_fm", [128, DC * D], BF16, PA)
            pstg = sb("pstg", [128, 2 * D], BF16, PA)
            abss = sb("abss", [128, 9], F32, PA)
            zs = sb("zs", [128, D], F32, PA)
            z2 = sb("z2", [128, D], F32, PA)

            S = Sched(sems, counts, waited, qrot)

            t_id = S.dma("sync", lambda e: e.dma_start(out=identb[:], in_=identb_h))
            S.dma("sync", lambda e: e.dma_start(out=mask[:], in_=mask_h))
            S.dma("sync", lambda e: e.dma_start(
                out=gam_bc[:], in_=gam_h.partition_broadcast(128)))
            S.dma("sync", lambda e: e.dma_start(
                out=bet_bc[:], in_=bet_h.partition_broadcast(128)))
            S.op("vector", lambda e: e.memset(epsc[:, 0:1], 1e-5))
            t_eps = S.op("vector", lambda e: e.memset(epsc[:, 1:2], 1e-8),
                         inc="dv", drain=True)

            stg_free = [None, None]
            for wi, (src, dst, scale) in enumerate([
                    (wqT_h, wqT_b, 1.0), (wkT_h, wkT_b, 1.0 / 32.0),
                    (wvT_h, wvT_b, 1.0), (woT_h, woT_b, 1.0)]):
                for c in range(DC):
                    slot = (wi * DC + c) % 2
                    tl = S.dma("sync", lambda e, src=src, c=c, slot=slot: e.dma_start(
                        out=stg[:, slot * D:(slot + 1) * D],
                        in_=src[c * 128:(c + 1) * 128, :]),
                        waits=[stg_free[slot]])
                    stg_free[slot] = S.op(
                        "vector", lambda e, dst=dst, c=c, slot=slot, scale=scale:
                        e.tensor_scalar(out=dst[:, c * D:(c + 1) * D],
                                        in0=stg[:, slot * D:(slot + 1) * D],
                                        scalar1=scale, scalar2=None, op0=ALU.mult),
                        waits=[tl], inc="dv", drain=True)
            t_wcast = stg_free[1]

            for c in range(DC):
                slot = c % 2
                tl = S.dma("sync", lambda e, c=c, slot=slot: e.dma_start(
                    out=stg[:, slot * D:(slot + 1) * D],
                    in_=wf_h[c * 128:(c + 1) * 128, :]), waits=[stg_free[slot]])
                stg_free[slot] = S.op("vector", lambda e, c=c, slot=slot: e.tensor_reduce(
                    out=abss[:, c:c + 1], in_=stg[:, slot * D:(slot + 1) * D],
                    axis=AX, op=ALU.add, apply_absolute_value=True),
                    waits=[tl], inc="dv", drain=True)
            t_rs = S.op("vector", lambda e: e.tensor_reduce(
                out=abss[:, 8:9], in_=abss[:, 0:8], axis=AX, op=ALU.add),
                waits=[stg_free[0], stg_free[1]], inc="dv", drain=True)
            t_hb = S.dma("gpsimd", lambda e: e.dma_start(
                out=hbounce, in_=abss[:, 8:9], allow_slow_non_contiguous=True),
                waits=[t_rs], sem="st")
            t_rr = S.dma("sync", lambda e: e.dma_start(out=redrow[:], in_=hbounce),
                         waits=[t_hb])
            S.op("vector", lambda e: e.tensor_reduce(
                out=sm[0:1, 4:5], in_=redrow[:], axis=AX, op=ALU.add),
                waits=[t_rr], drain=True)
            S.op("vector", lambda e: e.tensor_scalar(
                out=sm[0:1, 1:2], in0=sm[0:1, 4:5], scalar1=1.0 / (D * D),
                scalar2=None, op0=ALU.mult), drain=True)
            t_thr = S.op("vector", lambda e: e.tensor_scalar(
                out=sm[0:1, 0:1], in0=sm[0:1, 1:2], scalar1=1e-5, scalar2=0.5,
                op0=ALU.add, op1=ALU.mult), inc="dv", drain=True)
            t_smd = S.dma("gpsimd", lambda e: e.dma_start(out=smd[0:1, 0:2],
                                                          in_=sm[0:1, 0:2]),
                          waits=[t_thr], sem="st")
            t_thrb = S.dma("sync", lambda e: e.dma_start(
                out=scal[:, 0:2], in_=smd[0:1, 0:2].partition_broadcast(128),
                allow_slow_non_contiguous=True), waits=[t_smd])
            t_nthr = S.op("vector", lambda e: e.tensor_scalar(
                out=scal[:, 6:7], in0=scal[:, 0:1], scalar1=-1.0, scalar2=None,
                op0=ALU.mult), waits=[t_thrb], inc="dv", drain=True)
            for c in range(DC):
                slot = c % 2
                tl = S.dma("sync", lambda e, c=c, slot=slot: e.dma_start(
                    out=stg[:, slot * D:(slot + 1) * D],
                    in_=wf_h[c * 128:(c + 1) * 128, :]), waits=[stg_free[slot]])
                S.op("vector", lambda e, slot=slot: e.tensor_scalar(
                    out=zs[:], in0=stg[:, slot * D:(slot + 1) * D],
                    scalar1=scal[:, 0:1], scalar2=None, op0=ALU.is_ge),
                    waits=[tl, t_nthr])
                S.op("vector", lambda e, slot=slot: e.tensor_scalar(
                    out=z2[:], in0=stg[:, slot * D:(slot + 1) * D],
                    scalar1=scal[:, 6:7], scalar2=None, op0=ALU.is_le),
                    drain=True)
                stg_free[slot] = S.op("vector", lambda e, c=c: e.tensor_tensor(
                    out=w_tern[:, c * D:(c + 1) * D], in0=zs[:], in1=z2[:],
                    op=ALU.subtract), inc="dv", drain=True)

            x_free = [None, None]
            xntm_free = [None, None]
            sq_free = [None, None]
            psAB_free = {0: None, 1: None}
            pstg_free = [None, None]
            xnfm_rd = None
            pr_ct = 0
            for s in range(NSA):
                xnfm_toks = []
                for tl_ in range(8):
                    gt = s * 8 + tl_
                    pb = gt % 2
                    base = pb * 16
                    tx = S.dma("sync", lambda e, gt=gt, pb=pb: e.dma_start(
                        out=x_in[:, pb * D:(pb + 1) * D],
                        in_=x_h[gt * 128:(gt + 1) * 128, :]), waits=[x_free[pb]])
                    S.op("vector", lambda e, pb=pb, base=base: e.tensor_reduce(
                        out=stats[:, base:base + 1], in_=x_in[:, pb * D:(pb + 1) * D],
                        axis=AX, op=ALU.add), waits=[tx], drain=True)
                    tsq = S.op("scalar", lambda e, pb=pb, base=base: e.activation(
                        out=sq_scr[:, pb * D:(pb + 1) * D],
                        in_=x_in[:, pb * D:(pb + 1) * D], func=AF.Square),
                        waits=[tx, sq_free[pb]], inc="ac", drain=True)
                    S.op("vector", lambda e, base=base: e.tensor_scalar(
                        out=stats[:, base + 2:base + 3], in0=stats[:, base:base + 1],
                        scalar1=1.0 / D, scalar2=None, op0=ALU.mult), drain=True)
                    S.op("vector", lambda e, base=base: e.tensor_tensor(
                        out=stats[:, base + 3:base + 4], in0=stats[:, base + 2:base + 3],
                        in1=stats[:, base + 2:base + 3], op=ALU.mult), drain=True)
                    sq_free[pb] = S.op("vector", lambda e, pb=pb, base=base: e.tensor_reduce(
                        out=stats[:, base + 1:base + 2],
                        in_=sq_scr[:, pb * D:(pb + 1) * D],
                        axis=AX, op=ALU.add), waits=[tsq], drain=True)
                    tvar = S.op("vector", lambda e, base=base: e.scalar_tensor_tensor(
                        out=stats[:, base + 4:base + 5], in0=stats[:, base + 1:base + 2],
                        scalar=1.0 / D, in1=stats[:, base + 3:base + 4],
                        op0=ALU.mult, op1=ALU.subtract), inc="dv", drain=True)
                    S.op("scalar", lambda e, base=base: e.activation(
                        out=stats[:, base + 5:base + 6], in_=stats[:, base + 4:base + 5],
                        func=AF.Ln, bias=epsc[:, 0:1], scale=1.0),
                        waits=[tvar, t_eps], drain=True)
                    trstd = S.op("scalar", lambda e, base=base: e.activation(
                        out=stats[:, base + 6:base + 7], in_=stats[:, base + 5:base + 6],
                        func=AF.Exp, bias=0.0, scale=-0.5), inc="ac", drain=True)
                    tneg = S.op("vector", lambda e, base=base: e.scalar_tensor_tensor(
                        out=stats[:, base + 7:base + 8], in0=stats[:, base + 2:base + 3],
                        scalar=-1.0, in1=stats[:, base + 6:base + 7],
                        op0=ALU.mult, op1=ALU.mult), waits=[trstd], inc="dv", drain=True)
                    txn = S.op("scalar", lambda e, pb=pb, base=base: e.activation(
                        out=xn_tm[:, pb * D:(pb + 1) * D],
                        in_=x_in[:, pb * D:(pb + 1) * D], func=AF.Identity,
                        bias=stats[:, base + 7:base + 8],
                        scale=stats[:, base + 6:base + 7]),
                        waits=[tneg, xntm_free[pb]], inc="ac", drain=True)
                    x_free[pb] = txn
                    ev_last = None
                    for c in range(DC):
                        ev = S.dma("sync", lambda e, pb=pb, c=c, tl_=tl_: e.dma_start(
                            out=xn_fm[:, c * D + tl_ * 128: c * D + (tl_ + 1) * 128],
                            in_=xn_tm[:, pb * D + c * 128: pb * D + (c + 1) * 128],
                            transpose=True),
                            waits=[txn, xnfm_rd], sem="tr")
                        ev_last = ev
                    xntm_free[pb] = ev_last
                    xnfm_toks.append(ev_last)
                tok0 = s * 1024
                for (wb, fm, is_v) in ((wvT_b, False, True), (wkT_b, True, False),
                                       (wqT_b, True, False)):
                    dstT = Qs if wb is wqT_b else Ks
                    for oc in range(DC):
                        pg = pr_ct % 2
                        pr_ct += 1
                        psu = (ps_a, ps_b)[pg]
                        mm_last = None
                        for ic in range(DC):
                            for h in range(2):
                                if fm:
                                    def mmf(e, wb=wb, psu=psu, ic=ic, oc=oc, h=h,
                                            st=(ic == 0), sp=(ic == DC - 1)):
                                        return e.matmul(
                                            out=psu[:, h * 512:(h + 1) * 512],
                                            lhsT=wb[:, ic * D + oc * 128:
                                                    ic * D + oc * 128 + 128],
                                            rhs=xn_fm[:, ic * D + h * 512:
                                                      ic * D + h * 512 + 512],
                                            start=st, stop=sp)
                                else:
                                    def mmf(e, wb=wb, psu=psu, ic=ic, oc=oc, h=h,
                                            st=(ic == 0), sp=(ic == DC - 1)):
                                        return e.matmul(
                                            out=psu[:, h * 512:(h + 1) * 512],
                                            lhsT=xn_fm[:, ic * D + oc * 128:
                                                       ic * D + oc * 128 + 128],
                                            rhs=wb[:, ic * D + h * 512:
                                                   ic * D + h * 512 + 512],
                                            start=st, stop=sp)
                                w = ([psAB_free[pg], t_wcast] + xnfm_toks
                                     if (ic == 0 and h == 0) else [])
                                mm_last = S.op("tensor", mmf, waits=w,
                                               inc="pe" if (ic == DC - 1 and h == 1)
                                               else None,
                                               drain=(ic == DC - 1 and h == 1))
                        xnfm_rd = mm_last
                        tev = S.op("vector", lambda e, psu=psu, pg=pg: e.tensor_copy(
                            out=pstg[:, pg * D:(pg + 1) * D], in_=psu[:]),
                            waits=[mm_last, pstg_free[pg]], inc="dv", drain=True)
                        psAB_free[pg] = tev
                        if is_v:
                            tst = S.dma("gpsimd",
                                        lambda e, oc=oc, tok0=tok0, pg=pg: e.dma_start(
                                            out=Vs[tok0 + oc * 128:
                                                   tok0 + (oc + 1) * 128, :],
                                            in_=pstg[:, pg * D:(pg + 1) * D]),
                                        waits=[tev], sem="st")
                        else:
                            tst = S.dma("gpsimd",
                                        lambda e, dstT=dstT, oc=oc, tok0=tok0, pg=pg:
                                        e.dma_start(
                                            out=dstT[oc * 128:(oc + 1) * 128,
                                                     tok0:tok0 + 1024],
                                            in_=pstg[:, pg * D:(pg + 1) * D]),
                                        waits=[tev], sem="st")
                        pstg_free[pg] = tst

            S.op("tensor", lambda e: e.nop(), inc="bar", drain=True)
            S.op("vector", lambda e: e.nop(), inc="bar", drain=True)
            S.op("scalar", lambda e: e.nop(), inc="bar", drain=True)
            S.op("gpsimd", lambda e: e.nop(),
                 waits=S.queue_drained("gpsimd"), inc="bar")
            bar_a = counts["bar"]

            blockA = PA.enter_context(nc.Block("phaseA"))
            S.emit(blockA)

        with ExitStack() as PB:
            rows = sb("rows", [128, 5 * T], BF16, PB)
            q_fm = sb("q_fm", [128, DC * 512], BF16, PB)
            kst = sb("kst", [128, 2 * DC * 512], BF16, PB)
            vst = sb("vst", [128, 2 * 4 * 1024], BF16, PB)
            attT = sb("attT", [128, 8 * 128], BF16, PB)
            ctx_bf = sb("ctx_bf", [128, 1024], BF16, PB)
            ctx_fm = sb("ctx_fm", [128, 1024], BF16, PB)
            x_res = sb("x_res", [128, 2 * D], F32, PB)
            x1_st = sb("x1_st", [128, 2 * D], F32, PB)
            bstat = sb("bstat", [128, 16], F32, PB)

            S = Sched(sems, counts, waited, qrot)
            barw = [("sync", "bar", bar_a, True)]

            rows_ids = [0, 1, 2, 3, 4]
            rows_tok = {i: None for i in range(5)}
            kst_free = [None, None]
            vst_free = [None, None]
            qfm_free = None
            attT_free = [None] * 8
            ctxAB_free = {0: None, 1: None}
            psx_half = [None, None]
            xres_free = [None, None]
            x1st_free = [None, None]
            ctxbf_free = None
            ctxfm_free = None
            exp_tok = {}
            negm_free = [None] * 4
            prev_mm = None
            sc_ct = 0

            for sj in range(NSB):
                sb_ids = rows_ids[0:4]
                spare = rows_ids[4]
                tq = S.dma("sync", lambda e, sj=sj: e.dma_start(
                    out=q_fm[:].rearrange("p (c n) -> p c n", n=512),
                    in_=Qs.rearrange("(c p) m -> p c m", p=128)[:, :,
                                                               sj * 512:(sj + 1) * 512]),
                    waits=barw + [qfm_free])
                ev_by_qt = [None] * 4
                last_scores = None
                for kb in range(sj + 1):
                    kbuf = kb % 2
                    tk = S.dma("sync", lambda e, kb=kb, kbuf=kbuf: e.dma_start(
                        out=kst[:, kbuf * DC * 512:(kbuf + 1) * DC * 512]
                            .rearrange("p (c n) -> p c n", n=512),
                        in_=Ks.rearrange("(c p) m -> p c m", p=128)[:, :,
                                                                    kb * 512:(kb + 1) * 512]),
                        waits=barw + [kst_free[kbuf]])
                    kst_rd = None
                    for qt in range(4):
                        g = 4 * sj + qt
                        L = (g + 1) * 128
                        n = min(512, L - kb * 512)
                        if n <= 0:
                            continue
                        half = sc_ct % 2
                        sc_ct += 1
                        mm_last = None
                        for ic in range(DC):
                            def mmf(e, ic=ic, qt=qt, kbuf=kbuf, half=half, n=n,
                                    st=(ic == 0), sp=(ic == DC - 1)):
                                return e.matmul(
                                    out=ps_x[:, half * 512: half * 512 + n],
                                    lhsT=q_fm[:, ic * 512 + qt * 128:
                                              ic * 512 + (qt + 1) * 128],
                                    rhs=kst[:, kbuf * DC * 512 + ic * 512:
                                            kbuf * DC * 512 + ic * 512 + n],
                                    start=st, stop=sp)
                            w = [tk, tq, psx_half[half]] if ic == 0 else []
                            mm_last = S.op("tensor", mmf, waits=w,
                                           inc="pe" if ic == DC - 1 else None,
                                           drain=(ic == DC - 1))
                        last_scores = mm_last
                        kst_rd = mm_last
                        rb = sb_ids[qt]
                        col0 = kb * 512
                        dcol = (L - 128) - col0
                        has_diag = 0 <= dcol < 512
                        if has_diag:
                            if dcol > 0:
                                S.op("vector",
                                     lambda e, rb=rb, col0=col0, dcol=dcol, half=half:
                                     e.tensor_copy(
                                         out=rows[:, rb * T + col0:
                                                  rb * T + col0 + dcol],
                                         in_=ps_x[:, half * 512: half * 512 + dcol]),
                                     waits=[mm_last, rows_tok[rb]])
                            tevl = S.op("vector",
                                        lambda e, rb=rb, col0=col0, dcol=dcol, half=half:
                                        e.scalar_tensor_tensor(
                                            out=rows[:, rb * T + col0 + dcol:
                                                     rb * T + col0 + dcol + 128],
                                            in0=ps_x[:, half * 512 + dcol:
                                                     half * 512 + dcol + 128],
                                            scalar=0.0, in1=mask[:],
                                            op0=ALU.add, op1=ALU.add),
                                        waits=[mm_last, rows_tok[rb]],
                                        inc="dv", drain=True)
                        else:
                            tevl = S.op("vector",
                                        lambda e, rb=rb, col0=col0, n=n, half=half:
                                        e.tensor_copy(
                                            out=rows[:, rb * T + col0:
                                                     rb * T + col0 + n],
                                            in_=ps_x[:, half * 512: half * 512 + n]),
                                        waits=[mm_last, rows_tok[rb]],
                                        inc="dv", drain=True)
                        psx_half[half] = tevl
                        ev_by_qt[qt] = tevl
                    kst_free[kbuf] = kst_rd
                qfm_free = last_scores
                att_ids = [None] * 4
                for qt in range(4):
                    g = 4 * sj + qt
                    L = (g + 1) * 128
                    rb = sb_ids[qt]
                    ab = spare if qt == 0 else sb_ids[qt - 1]
                    att_ids[qt] = ab
                    tmax = S.op("vector", lambda e, rb=rb, L=L, qt=qt: e.tensor_reduce(
                        out=bstat[:, qt:qt + 1], in_=rows[:, rb * T: rb * T + L],
                        axis=AX, op=ALU.max, negate=True),
                        waits=[ev_by_qt[qt], negm_free[qt]], inc="dv", drain=True)
                    texp = S.op("scalar", lambda e, rb=rb, ab=ab, L=L, qt=qt: e.activation(
                        out=rows[:, ab * T: ab * T + L],
                        in_=rows[:, rb * T: rb * T + L],
                        func=AF.Exp, bias=bstat[:, qt:qt + 1], scale=1.0),
                        waits=barw + [tmax, rows_tok[ab]], inc="ac", drain=True)
                    exp_tok[g] = texp
                    negm_free[qt] = texp
                    rows_tok[rb] = texp
                    S.op("vector", lambda e, ab=ab, L=L, qt=qt: e.tensor_reduce(
                        out=bstat[:, 4 + qt:5 + qt], in_=rows[:, ab * T: ab * T + L],
                        axis=AX, op=ALU.add), waits=[texp], drain=True)
                    S.op("vector", lambda e, qt=qt: e.reciprocal(
                        out=bstat[:, 8 + qt:9 + qt], in_=bstat[:, 4 + qt:5 + qt]),
                        inc="dv", drain=True)
                for pr in range(2):
                    qts = (2 * pr, 2 * pr + 1)
                    nkv_hi = 4 * sj + qts[1] + 1
                    nch = (nkv_hi + 3) // 4
                    pv_last = {qt: None for qt in qts}
                    for kb in range(nch):
                        vbuf = kb % 2
                        tv = S.dma("gpsimd", lambda e, kb=kb, vbuf=vbuf: e.dma_start(
                            out=vst[:, vbuf * 4096:(vbuf + 1) * 4096]
                                .rearrange("p (t n) -> p t n", n=1024),
                            in_=Vs.rearrange("(t p) n -> p t n", p=128)[:,
                                                                        kb * 4:(kb + 1) * 4, :]),
                            waits=barw + [vst_free[vbuf]], sem="st")
                        valid = []
                        for vt in range(4):
                            kv = kb * 4 + vt
                            if kv >= nkv_hi:
                                break
                            for qt in qts:
                                if kv < 4 * sj + qt + 1:
                                    valid.append((vt, kv, qt))
                        evs = []
                        for idx, (vt, kv, qt) in enumerate(valid):
                            g = 4 * sj + qt
                            ab = att_ids[qt]
                            tEv = S.dma("scalar",
                                        lambda e, ab=ab, kv=kv, idx=idx: e.dma_start(
                                            out=attT[:, idx * 128:(idx + 1) * 128],
                                            in_=rows[:, ab * T + kv * 128:
                                                     ab * T + (kv + 1) * 128],
                                            transpose=True),
                                        waits=[exp_tok[g], attT_free[idx]], sem="tr")
                            evs.append(tEv)
                        chunk_last = None
                        for idx, (vt, kv, qt) in enumerate(valid):
                            g = 4 * sj + qt
                            psu = (ps_a, ps_b)[qt % 2]
                            first = (kv == 0)
                            lastkv = (kv == g)
                            mm2 = None
                            for h in range(2):
                                def mmf(e, psu=psu, idx=idx, vbuf=vbuf, vt=vt, h=h,
                                        st=first, sp=lastkv):
                                    return e.matmul(
                                        out=psu[:, h * 512:(h + 1) * 512],
                                        lhsT=attT[:, idx * 128:(idx + 1) * 128],
                                        rhs=vst[:, vbuf * 4096 + vt * 1024 + h * 512:
                                                vbuf * 4096 + vt * 1024 + h * 512 + 512],
                                        start=st, stop=sp)
                                w = ([evs[-1], tv] +
                                     ([ctxAB_free[qt % 2]] if first else []))
                                mm2 = S.op("tensor", mmf, waits=w if h == 0 else [],
                                           inc="pe" if h == 1 else None,
                                           drain=(h == 1 and lastkv))
                            attT_free[idx] = mm2
                            pv_last[qt] = mm2
                            chunk_last = mm2
                        vst_free[vbuf] = chunk_last
                        prev_mm = chunk_last
                    for qt in qts:
                        g = 4 * sj + qt
                        psu = (ps_a, ps_b)[qt % 2]
                        S.op("vector", lambda e, psu=psu, qt=qt: e.tensor_scalar(
                            out=ctx_bf[:, 0:512], in0=psu[:, 0:512],
                            scalar1=bstat[:, 8 + qt:9 + qt], scalar2=None, op0=ALU.mult),
                            waits=[pv_last[qt], ctxbf_free])
                        tc2 = S.op("vector", lambda e, psu=psu, qt=qt: e.tensor_scalar(
                            out=ctx_bf[:, 512:1024], in0=psu[:, 512:1024],
                            scalar1=bstat[:, 8 + qt:9 + qt], scalar2=None, op0=ALU.mult),
                            inc="dv", drain=True)
                        ctxAB_free[qt % 2] = tc2
                        rows_tok[att_ids[qt]] = pv_last[qt]
                        evl = None
                        for c in range(DC):
                            ev = S.dma("scalar", lambda e, c=c: e.dma_start(
                                out=ctx_fm[:, c * 128:(c + 1) * 128],
                                in_=ctx_bf[:, c * 128:(c + 1) * 128],
                                transpose=True),
                                waits=[tc2, ctxfm_free if c == 0 else None],
                                sem="tr")
                            evl = ev
                        ctxbf_free = evl
                        xb = qt % 2
                        txr = S.dma("sync", lambda e, g=g, xb=xb: e.dma_start(
                            out=x_res[:, xb * D:(xb + 1) * D],
                            in_=x_h[g * 128:(g + 1) * 128, :]),
                            waits=barw + [xres_free[xb]])
                        mmo = None
                        for ic in range(DC):
                            for h in range(2):
                                def mmf(e, ic=ic, h=h, st=(ic == 0), sp=(ic == DC - 1)):
                                    return e.matmul(
                                        out=ps_x[:, h * 512:(h + 1) * 512],
                                        lhsT=ctx_fm[:, ic * 128:(ic + 1) * 128],
                                        rhs=woT_b[:, ic * D + h * 512:
                                                  ic * D + h * 512 + 512],
                                        start=st, stop=sp)
                                w = ([evl, psx_half[0], psx_half[1]]
                                     if ic == 0 and h == 0 else [])
                                mmo = S.op("tensor", mmf, waits=w,
                                           inc="pe" if (ic == DC - 1 and h == 1)
                                           else None,
                                           drain=(ic == DC - 1 and h == 1))
                        ctxfm_free = mmo
                        prev_mm = mmo
                        tadd = S.op("vector", lambda e, xb=xb: e.tensor_tensor(
                            out=x1_st[:, xb * D:(xb + 1) * D], in0=ps_x[:],
                            in1=x_res[:, xb * D:(xb + 1) * D], op=ALU.add),
                            waits=[mmo, txr, x1st_free[xb]], inc="dv", drain=True)
                        psx_half[0] = tadd
                        psx_half[1] = tadd
                        xres_free[xb] = tadd
                        x1st_free[xb] = S.dma("gpsimd",
                                              lambda e, g=g, xb=xb: e.dma_start(
                                                  out=X1s[g * 128:(g + 1) * 128, :],
                                                  in_=x1_st[:, xb * D:(xb + 1) * D]),
                                              waits=[tadd], sem="st")
                rows_ids = [rows_ids[4], rows_ids[0], rows_ids[1],
                            rows_ids[2], rows_ids[3]]

            S.op("tensor", lambda e: e.nop(), inc="bar", drain=True)
            S.op("vector", lambda e: e.nop(), inc="bar", drain=True)
            S.op("scalar", lambda e: e.nop(), inc="bar", drain=True)
            S.op("gpsimd", lambda e: e.nop(),
                 waits=S.queue_drained("gpsimd"), inc="bar")
            bar_b = counts["bar"]

            blockB = PB.enter_context(nc.Block("phaseB"))
            S.emit(blockB)

        with ExitStack() as PC:
            xn_res = sb("xn_res", [128, NT * D], BF16, PC)
            x1_in = sb("x1_in", [128, 2 * D], F32, PC)
            xh = sb("xh", [128, D], F32, PC)
            sq2 = sb("sq2", [128, 2 * D], F32, PC)
            cstat = sb("cstat", [128, 32], F32, PC)
            absm = sb("absm", [128, 72], F32, PC)
            xq_bf = sb("xq_bf", [128, D], BF16, PC)
            xq_fm = sb("xq_fm", [128, D], BF16, PC)
            y_out = sb("y_out", [128, 2 * D], F32, PC)

            S = Sched(sems, counts, waited, qrot)
            barw = [("sync", "bar", bar_b, True)]

            S.op("vector", lambda e: e.memset(cc_sb[:], 0.0), waits=barw,
                 inc="dv", drain=True)

            x1_free = [None, None]
            xh_free = None
            sq2_free = [None, None]
            xn_toks = []
            for t in range(NT):
                pb = t % 2
                base = (t % 2) * 16
                tx = S.dma("sync", lambda e, t=t, pb=pb: e.dma_start(
                    out=x1_in[:, pb * D:(pb + 1) * D],
                    in_=X1s[t * 128:(t + 1) * 128, :]), waits=barw + [x1_free[pb]])
                S.op("vector", lambda e, pb=pb, base=base: e.tensor_reduce(
                    out=cstat[:, base:base + 1], in_=x1_in[:, pb * D:(pb + 1) * D],
                    axis=AX, op=ALU.add), waits=[tx], drain=True)
                tsq = S.op("scalar", lambda e, pb=pb: e.activation(
                    out=sq2[:, pb * D:(pb + 1) * D],
                    in_=x1_in[:, pb * D:(pb + 1) * D], func=AF.Square),
                    waits=barw + [tx, sq2_free[pb]], inc="ac", drain=True)
                S.op("vector", lambda e, base=base: e.tensor_scalar(
                    out=cstat[:, base + 2:base + 3], in0=cstat[:, base:base + 1],
                    scalar1=1.0 / D, scalar2=None, op0=ALU.mult), drain=True)
                S.op("vector", lambda e, base=base: e.tensor_tensor(
                    out=cstat[:, base + 3:base + 4], in0=cstat[:, base + 2:base + 3],
                    in1=cstat[:, base + 2:base + 3], op=ALU.mult), drain=True)
                S.op("vector", lambda e, pb=pb, base=base: e.tensor_reduce(
                    out=cstat[:, base + 1:base + 2], in_=sq2[:, pb * D:(pb + 1) * D],
                    axis=AX, op=ALU.add), waits=[tsq], drain=True)
                tvar = S.op("vector", lambda e, base=base: e.scalar_tensor_tensor(
                    out=cstat[:, base + 4:base + 5], in0=cstat[:, base + 1:base + 2],
                    scalar=1.0 / D, in1=cstat[:, base + 3:base + 4],
                    op0=ALU.mult, op1=ALU.subtract), inc="dv", drain=True)
                S.op("scalar", lambda e, base=base: e.activation(
                    out=cstat[:, base + 5:base + 6], in_=cstat[:, base + 4:base + 5],
                    func=AF.Ln, bias=epsc[:, 1:2], scale=1.001),
                    waits=[tvar], drain=True)
                ts_ = S.op("scalar", lambda e, base=base: e.activation(
                    out=cstat[:, base + 6:base + 7], in_=cstat[:, base + 5:base + 6],
                    func=AF.Exp, bias=0.0, scale=-0.5), inc="ac", drain=True)
                tneg = S.op("vector", lambda e, base=base: e.scalar_tensor_tensor(
                    out=cstat[:, base + 7:base + 8], in0=cstat[:, base + 2:base + 3],
                    scalar=-1.0, in1=cstat[:, base + 6:base + 7],
                    op0=ALU.mult, op1=ALU.mult), waits=[ts_], inc="dv", drain=True)
                txh = S.op("scalar", lambda e, pb=pb, base=base: e.activation(
                    out=xh[:], in_=x1_in[:, pb * D:(pb + 1) * D], func=AF.Identity,
                    bias=cstat[:, base + 7:base + 8],
                    scale=cstat[:, base + 6:base + 7]),
                    waits=[tneg, xh_free], inc="ac", drain=True)
                x1_free[pb] = txh
                S.op("vector", lambda e, pb=pb: e.tensor_tensor(
                    out=sq2[:, pb * D:(pb + 1) * D], in0=xh[:], in1=gam_bc[:],
                    op=ALU.mult), waits=[txh], drain=True)
                taff = S.op("vector", lambda e, t=t, pb=pb: e.tensor_tensor(
                    out=xn_res[:, t * D:(t + 1) * D], in0=sq2[:, pb * D:(pb + 1) * D],
                    in1=bet_bc[:], op=ALU.add), inc="dv", drain=True)
                sq2_free[pb] = taff
                xh_free = taff
                xn_toks.append(taff)
                S.op("vector", lambda e, t=t: e.tensor_reduce(
                    out=absm[:, t:t + 1], in_=xn_res[:, t * D:(t + 1) * D],
                    axis=AX, op=ALU.max, apply_absolute_value=True), drain=True)

            tam = S.op("vector", lambda e: e.tensor_reduce(
                out=absm[:, 64:65], in_=absm[:, 0:64], axis=AX, op=ALU.max),
                inc="dv", drain=True)
            t_hb2 = S.dma("gpsimd", lambda e: e.dma_start(
                out=hbounce, in_=absm[:, 64:65], allow_slow_non_contiguous=True),
                waits=[tam], sem="st")
            t_rr2 = S.dma("sync", lambda e: e.dma_start(out=redrow[:], in_=hbounce),
                          waits=[t_hb2])
            tred = S.op("vector", lambda e: e.tensor_reduce(
                out=cc_sb[0:1, 0:1], in_=redrow[:], axis=AX, op=ALU.max),
                waits=[t_rr2], inc="dv", drain=True)
            tcc1 = S.dma("gpsimd", lambda e: e.dma_start(out=cc_in[:, :], in_=cc_sb[:]),
                         waits=[tred], sem="st")
            tcol = S.op("gpsimd", lambda e: e.collective_compute(
                "AllReduce", ALU.max, replica_groups=[list(range(8))],
                ins=[cc_in.ap().opt()], outs=[cc_out.ap().opt()]),
                waits=[tcc1], inc="cc", asynchronous=True)
            t_gbc = S.dma("sync", lambda e: e.dma_start(
                out=scal[:, 2:3], in_=cc_out.ap()[0:1, 0:1].partition_broadcast(128),
                allow_slow_non_contiguous=True), waits=[tcol])
            S.op("vector", lambda e: e.reciprocal(out=scal[:, 5:6], in_=scal[:, 2:3]),
                 waits=[t_gbc], drain=True)
            ts1 = S.op("vector", lambda e: e.tensor_scalar(
                out=scal[:, 3:4], in0=scal[:, 5:6], scalar1=128.0, scalar2=None,
                op0=ALU.mult), inc="dv", drain=True)
            ts2 = S.op("vector", lambda e: e.scalar_tensor_tensor(
                out=scal[:, 4:5], in0=scal[:, 2:3], scalar=1.0 / 128.0,
                in1=scal[:, 1:2], op0=ALU.mult, op1=ALU.mult), inc="dv", drain=True)

            xq_free = None
            xqfm_rd = None
            yout_free = [None, None]
            psC_free = {0: None, 1: None}
            for t in range(NT):
                pb = t % 2
                tx = S.dma("sync", lambda e, t=t, pb=pb: e.dma_start(
                    out=x1_in[:, pb * D:(pb + 1) * D],
                    in_=X1s[t * 128:(t + 1) * 128, :]), waits=[x1_free[pb]])
                tq1 = S.op("vector", lambda e, t=t: e.tensor_scalar(
                    out=xq_bf[:], in0=xn_res[:, t * D:(t + 1) * D],
                    scalar1=scal[:, 3:4], scalar2=CLIP, op0=ALU.mult, op1=ALU.min),
                    waits=[ts1, xn_toks[t], xq_free])
                tq2 = S.op("vector", lambda e: e.tensor_scalar(
                    out=xq_bf[:], in0=xq_bf[:], scalar1=-CLIP, scalar2=None,
                    op0=ALU.max), inc="dv", drain=True)
                evl = None
                for c in range(DC):
                    ev = S.dma("scalar", lambda e, c=c: e.dma_start(
                        out=xq_fm[:, c * 128:(c + 1) * 128],
                        in_=xq_bf[:, c * 128:(c + 1) * 128],
                        transpose=True),
                        waits=[tq2, xqfm_rd if c == 0 else None], sem="tr")
                    evl = ev
                xq_free = evl
                pg = t % 2
                psu = (ps_a, ps_b)[pg]
                mml = None
                for ic in range(DC):
                    for h in range(2):
                        def mmf(e, psu=psu, ic=ic, h=h, st=(ic == 0), sp=(ic == DC - 1)):
                            return e.matmul(
                                out=psu[:, h * 512:(h + 1) * 512],
                                lhsT=xq_fm[:, ic * 128:(ic + 1) * 128],
                                rhs=w_tern[:, ic * D + h * 512: ic * D + h * 512 + 512],
                                start=st, stop=sp)
                        w = [evl, psC_free[pg]] if ic == 0 and h == 0 else []
                        mml = S.op("tensor", mmf, waits=w,
                                   inc="pe" if (ic == DC - 1 and h == 1) else None,
                                   drain=(ic == DC - 1 and h == 1))
                xqfm_rd = mml
                tep = S.op("vector", lambda e, psu=psu, pb=pb: e.scalar_tensor_tensor(
                    out=y_out[:, pb * D:(pb + 1) * D], in0=psu[:],
                    scalar=scal[:, 4:5], in1=x1_in[:, pb * D:(pb + 1) * D],
                    op0=ALU.mult, op1=ALU.add),
                    waits=[mml, tx, ts2, yout_free[pb]], inc="dv", drain=True)
                psC_free[pg] = tep
                x1_free[pb] = tep
                yout_free[pb] = S.dma("gpsimd", lambda e, t=t, pb=pb: e.dma_start(
                    out=out_h[t * 128:(t + 1) * 128, :],
                    in_=y_out[:, pb * D:(pb + 1) * D]), waits=[tep], sem="st")

            S.op("gpsimd", lambda e: e.nop(), waits=S.queue_drained("gpsimd"))

            blockC = PC.enter_context(nc.Block("phaseC"))
            S.emit(blockC)

    return nc


_CONSTS = None


def _consts():
    global _CONSTS
    if _CONSTS is None:
        m = np.zeros((128, 128), dtype=np.float32)
        iu = np.triu_indices(128, k=1)
        m[iu] = NEG
        _CONSTS = (m, np.eye(128, dtype=ml_dtypes.bfloat16))
    return _CONSTS


def kernel(L0_attn_in_features, L0_attn_w_q, L0_attn_w_k, L0_attn_w_v,
           L0_attn_w_o, L0_ffn_weight, L0_ffn_ln_gamma, L0_ffn_ln_beta):
    x = np.ascontiguousarray(np.asarray(L0_attn_in_features, dtype=np.float32))
    wqT = np.ascontiguousarray(np.asarray(L0_attn_w_q, dtype=np.float32).T)
    wkT = np.ascontiguousarray(np.asarray(L0_attn_w_k, dtype=np.float32).T)
    wvT = np.ascontiguousarray(np.asarray(L0_attn_w_v, dtype=np.float32).T)
    woT = np.ascontiguousarray(np.asarray(L0_attn_w_o, dtype=np.float32).T)
    wf = np.ascontiguousarray(np.asarray(L0_ffn_weight, dtype=np.float32))
    gam = np.ascontiguousarray(np.asarray(L0_ffn_ln_gamma, dtype=np.float32).reshape(1, D))
    bet = np.ascontiguousarray(np.asarray(L0_ffn_ln_beta, dtype=np.float32).reshape(1, D))
    mask, identb = _consts()

    nc = build_program()
    in_maps = []
    for c in range(8):
        in_maps.append({
            "x": np.ascontiguousarray(x[c]),
            "wqT": wqT, "wkT": wkT, "wvT": wvT, "woT": woT, "wf": wf,
            "gam": gam, "bet": bet, "identb": identb, "mask": mask,
        })
    res = run_bass_kernel_spmd(nc, in_maps, core_ids=list(range(8)))
    return np.stack([res.results[c]["out"] for c in range(8)], axis=0)


# version 10
# speedup vs baseline: 1.4169x; 1.4169x over previous
import numpy as np
import ml_dtypes
from contextlib import ExitStack

import concourse.bass as bass
import concourse.mybir as mybir
from concourse.bass_utils import run_bass_kernel_spmd

F32 = mybir.dt.float32
BF16 = mybir.dt.bfloat16
AF = mybir.ActivationFunctionType
ALU = mybir.AluOpType
AX = mybir.AxisListType.X

T = 8192
D = 1024
NT = T // 128
DC = D // 128
NSA = 8
NSB = 32
NEG = -1e30
CLIP = 128.0 - 1e-5

ENGINES = ("sync", "tensor", "vector", "scalar", "gpsimd")

QPOOL = {"sync": 6, "gpsimd": 5, "scalar": 4}
QSEMS = {e: [f"q_{e}{i}" for i in range(n)] for e, n in QPOOL.items()}


class Sched:
    def __init__(self, sems, counts, waited, qrot):
        self.sems = sems
        self.cnt = counts
        self.waited = waited
        self.qrot = qrot
        self.ops = {e: [] for e in ENGINES}

    def op(self, eng, fn, waits=(), inc=None, inc_by=1, asynchronous=False,
           drain=False):
        self.ops[eng].append((fn, [w for w in waits if w is not None],
                              (inc, inc_by) if inc else None, drain))
        if inc:
            self.cnt[inc] += inc_by
            return (eng, inc, self.cnt[inc], asynchronous)
        return None

    def dma(self, eng, fn, waits=(), sem=None):
        pool = QSEMS[eng]
        s = pool[self.qrot[eng] % len(pool)]
        self.qrot[eng] += 1
        ws = list(waits)
        if self.cnt[s]:
            ws.append((eng, s, self.cnt[s], True))
        return self.op(eng, fn, ws, inc=s, inc_by=16, asynchronous=True)

    def queue_drained(self, eng):
        return [(eng, s, self.cnt[s], True) for s in QSEMS[eng] if self.cnt[s]]

    def emit(self, block):
        for name in ENGINES:
            ops = self.ops[name]
            if not ops:
                continue
            waited = self.waited.setdefault(name, {})
            sems = self.sems

            def body(eng, ops=ops, waited=waited, name=name):
                for fn, waits, inc, drain in ops:
                    for (peng, sname, val, is_async) in waits:
                        if peng == name and not is_async and name != "tensor":
                            continue
                        if waited.get(sname, -1) >= val:
                            continue
                        eng.wait_ge(sems[sname], val)
                        waited[sname] = val
                    inst = fn(eng)
                    if drain:
                        inst = eng.drain()
                    if inc is not None:
                        inst.then_inc(sems[inc[0]], inc[1])

            getattr(block, name)(body)


def build_program():
    nc = bass.Bass("TRN2", target_bir_lowering=False)

    x_h = nc.dram_tensor("x", [T, D], F32, kind="ExternalInput").ap()
    wqT_h = nc.dram_tensor("wqT", [D, D], F32, kind="ExternalInput").ap()
    wkT_h = nc.dram_tensor("wkT", [D, D], F32, kind="ExternalInput").ap()
    wvT_h = nc.dram_tensor("wvT", [D, D], F32, kind="ExternalInput").ap()
    woT_h = nc.dram_tensor("woT", [D, D], F32, kind="ExternalInput").ap()
    wf_h = nc.dram_tensor("wf", [D, D], F32, kind="ExternalInput").ap()
    gam_h = nc.dram_tensor("gam", [1, D], F32, kind="ExternalInput").ap()
    bet_h = nc.dram_tensor("bet", [1, D], F32, kind="ExternalInput").ap()
    maskT_h = nc.dram_tensor("maskT", [128, 128], BF16, kind="ExternalInput").ap()
    ones8_h = nc.dram_tensor("ones8", [128, 8], BF16, kind="ExternalInput").ap()
    out_h = nc.dram_tensor("out", [T, D], F32, kind="ExternalOutput").ap()

    Qs = nc.dram_tensor("Qs", [D, T], BF16, kind="Internal").ap()
    Ks = nc.dram_tensor("Ks", [D, T], BF16, kind="Internal").ap()
    Vs = nc.dram_tensor("Vs", [T, D], BF16, kind="Internal").ap()
    X1s = nc.dram_tensor("X1s", [T, D], F32, kind="Internal").ap()
    hbounce = nc.dram_tensor("hbounce", [1, 128], F32, kind="Internal").ap()
    smd = nc.dram_tensor("smd", [1, 8], F32, kind="Internal").ap()
    cc_in = nc.dram_tensor("cc_in", [1, 512], F32, kind="Internal")
    cc_out = nc.dram_tensor("cc_out", [1, 512], F32, kind="Internal")

    with ExitStack() as P:
        def sb(name, shape, dtype, stack):
            return stack.enter_context(nc.sbuf_tensor(name, shape, dtype))

        maskT = sb("maskT_s", [128, 128], BF16, P)
        ones8 = sb("ones8_s", [128, 8], BF16, P)
        gam_bc = sb("gam_bc", [128, D], F32, P)
        bet_bc = sb("bet_bc", [128, D], F32, P)
        w_tern = sb("w_tern", [128, DC * D], BF16, P)
        scal = sb("scal", [128, 8], F32, P)
        sm = sb("sm", [1, 8], F32, P)
        cc_sb = sb("cc_sb", [1, 512], F32, P)
        redrow = sb("redrow", [1, 128], F32, P)
        epsc = sb("epsc", [128, 2], F32, P)

        ps_a = P.enter_context(nc.psum_tensor("ps_a", [128, 1024], F32))
        ps_b = P.enter_context(nc.psum_tensor("ps_b", [128, 1024], F32))
        ps_s0 = P.enter_context(nc.psum_tensor("ps_s0", [128, 256], F32))
        ps_s1 = P.enter_context(nc.psum_tensor("ps_s1", [128, 256], F32))
        ps_s = (ps_s0, ps_s1)
        ps_d0 = P.enter_context(nc.psum_tensor("ps_d0", [128, 8], F32))
        ps_d1 = P.enter_context(nc.psum_tensor("ps_d1", [128, 8], F32))
        ps_dd = (ps_d0, ps_d1)

        semnames = (["pe", "dv", "ac", "cc", "bar"]
                    + [s for pool in QSEMS.values() for s in pool])
        sems = {n: P.enter_context(nc.semaphore(n)) for n in semnames}
        counts = {n: 0 for n in semnames}
        waited = {}
        qrot = {e: 0 for e in QPOOL}

        with ExitStack() as PA:
            wqT_b = sb("wqT_b", [128, DC * D], BF16, PA)
            wkT_b = sb("wkT_b", [128, DC * D], BF16, PA)
            wvT_b = sb("wvT_b", [128, DC * D], BF16, PA)
            woT_b = sb("woT_b", [128, DC * D], BF16, PA)
            V_fm = sb("V_fm", [128, DC * D], BF16, PA)
            stg = sb("stg", [128, 2 * D], F32, PA)
            x_in = sb("x_in", [128, 2 * D], F32, PA)
            sq_scr = sb("sq_scr", [128, 2 * D], F32, PA)
            stats = sb("stats", [128, 32], F32, PA)
            xn_tm = sb("xn_tm", [128, 2 * D], BF16, PA)
            xn_fm = sb("xn_fm", [128, DC * D], BF16, PA)
            pstg = sb("pstg", [128, 2 * D], BF16, PA)
            abss = sb("abss", [128, 9], F32, PA)
            zs = sb("zs", [128, D], F32, PA)
            z2 = sb("z2", [128, D], F32, PA)

            S = Sched(sems, counts, waited, qrot)

            t_mask = S.dma("sync", lambda e: e.dma_start(out=maskT[:], in_=maskT_h))
            t_ones = S.dma("sync", lambda e: e.dma_start(out=ones8[:], in_=ones8_h))
            S.dma("sync", lambda e: e.dma_start(
                out=gam_bc[:], in_=gam_h.partition_broadcast(128)))
            S.dma("sync", lambda e: e.dma_start(
                out=bet_bc[:], in_=bet_h.partition_broadcast(128)))
            S.op("vector", lambda e: e.memset(epsc[:, 0:1], 1e-5))
            t_eps = S.op("vector", lambda e: e.memset(epsc[:, 1:2], 1e-8),
                         inc="dv", drain=True)

            stg_free = [None, None]
            for wi, (src, dst, scale) in enumerate([
                    (wqT_h, wqT_b, 1.0), (wkT_h, wkT_b, 1.0 / 32.0),
                    (wvT_h, wvT_b, 1.0), (woT_h, woT_b, 1.0)]):
                for c in range(DC):
                    slot = (wi * DC + c) % 2
                    tl = S.dma("sync", lambda e, src=src, c=c, slot=slot: e.dma_start(
                        out=stg[:, slot * D:(slot + 1) * D],
                        in_=src[c * 128:(c + 1) * 128, :]),
                        waits=[stg_free[slot]])
                    stg_free[slot] = S.op(
                        "vector", lambda e, dst=dst, c=c, slot=slot, scale=scale:
                        e.tensor_scalar(out=dst[:, c * D:(c + 1) * D],
                                        in0=stg[:, slot * D:(slot + 1) * D],
                                        scalar1=scale, scalar2=None, op0=ALU.mult),
                        waits=[tl], inc="dv", drain=True)
            t_wcast = stg_free[1]

            for c in range(DC):
                slot = c % 2
                tl = S.dma("sync", lambda e, c=c, slot=slot: e.dma_start(
                    out=stg[:, slot * D:(slot + 1) * D],
                    in_=wf_h[c * 128:(c + 1) * 128, :]), waits=[stg_free[slot]])
                stg_free[slot] = S.op("vector", lambda e, c=c, slot=slot: e.tensor_reduce(
                    out=abss[:, c:c + 1], in_=stg[:, slot * D:(slot + 1) * D],
                    axis=AX, op=ALU.add, apply_absolute_value=True),
                    waits=[tl], inc="dv", drain=True)
            t_rs = S.op("vector", lambda e: e.tensor_reduce(
                out=abss[:, 8:9], in_=abss[:, 0:8], axis=AX, op=ALU.add),
                waits=[stg_free[0], stg_free[1]], inc="dv", drain=True)
            t_hb = S.dma("gpsimd", lambda e: e.dma_start(
                out=hbounce, in_=abss[:, 8:9], allow_slow_non_contiguous=True),
                waits=[t_rs], sem="st")
            t_rr = S.dma("sync", lambda e: e.dma_start(out=redrow[:], in_=hbounce),
                         waits=[t_hb])
            S.op("vector", lambda e: e.tensor_reduce(
                out=sm[0:1, 4:5], in_=redrow[:], axis=AX, op=ALU.add),
                waits=[t_rr], drain=True)
            S.op("vector", lambda e: e.tensor_scalar(
                out=sm[0:1, 1:2], in0=sm[0:1, 4:5], scalar1=1.0 / (D * D),
                scalar2=None, op0=ALU.mult), drain=True)
            t_thr = S.op("vector", lambda e: e.tensor_scalar(
                out=sm[0:1, 0:1], in0=sm[0:1, 1:2], scalar1=1e-5, scalar2=0.5,
                op0=ALU.add, op1=ALU.mult), inc="dv", drain=True)
            t_smd = S.dma("gpsimd", lambda e: e.dma_start(out=smd[0:1, 0:2],
                                                          in_=sm[0:1, 0:2]),
                          waits=[t_thr], sem="st")
            t_thrb = S.dma("sync", lambda e: e.dma_start(
                out=scal[:, 0:2], in_=smd[0:1, 0:2].partition_broadcast(128),
                allow_slow_non_contiguous=True), waits=[t_smd])
            t_nthr = S.op("vector", lambda e: e.tensor_scalar(
                out=scal[:, 6:7], in0=scal[:, 0:1], scalar1=-1.0, scalar2=None,
                op0=ALU.mult), waits=[t_thrb], inc="dv", drain=True)
            for c in range(DC):
                slot = c % 2
                tl = S.dma("sync", lambda e, c=c, slot=slot: e.dma_start(
                    out=stg[:, slot * D:(slot + 1) * D],
                    in_=wf_h[c * 128:(c + 1) * 128, :]), waits=[stg_free[slot]])
                S.op("vector", lambda e, slot=slot: e.tensor_scalar(
                    out=zs[:], in0=stg[:, slot * D:(slot + 1) * D],
                    scalar1=scal[:, 0:1], scalar2=None, op0=ALU.is_ge),
                    waits=[tl, t_nthr])
                S.op("vector", lambda e, slot=slot: e.tensor_scalar(
                    out=z2[:], in0=stg[:, slot * D:(slot + 1) * D],
                    scalar1=scal[:, 6:7], scalar2=None, op0=ALU.is_le),
                    drain=True)
                stg_free[slot] = S.op("vector", lambda e, c=c: e.tensor_tensor(
                    out=w_tern[:, c * D:(c + 1) * D], in0=zs[:], in1=z2[:],
                    op=ALU.subtract), inc="dv", drain=True)

            x_free = [None, None]
            xntm_free = [None, None]
            sq_free = [None, None]
            psAB_free = {0: None, 1: None}
            pstg_free = [None, None]
            xnfm_rd = None
            vfm_rd = None
            pr_ct = 0
            for s in range(NSA):
                xnfm_toks = []
                for tl_ in range(8):
                    gt = s * 8 + tl_
                    pb = gt % 2
                    base = pb * 16
                    tx = S.dma("sync", lambda e, gt=gt, pb=pb: e.dma_start(
                        out=x_in[:, pb * D:(pb + 1) * D],
                        in_=x_h[gt * 128:(gt + 1) * 128, :]), waits=[x_free[pb]])
                    S.op("vector", lambda e, pb=pb, base=base: e.tensor_reduce(
                        out=stats[:, base:base + 1], in_=x_in[:, pb * D:(pb + 1) * D],
                        axis=AX, op=ALU.add), waits=[tx], drain=True)
                    tsq = S.op("scalar", lambda e, pb=pb, base=base: e.activation(
                        out=sq_scr[:, pb * D:(pb + 1) * D],
                        in_=x_in[:, pb * D:(pb + 1) * D], func=AF.Square),
                        waits=[tx, sq_free[pb]], inc="ac", drain=True)
                    S.op("vector", lambda e, base=base: e.tensor_scalar(
                        out=stats[:, base + 2:base + 3], in0=stats[:, base:base + 1],
                        scalar1=1.0 / D, scalar2=None, op0=ALU.mult), drain=True)
                    S.op("vector", lambda e, base=base: e.tensor_tensor(
                        out=stats[:, base + 3:base + 4], in0=stats[:, base + 2:base + 3],
                        in1=stats[:, base + 2:base + 3], op=ALU.mult), drain=True)
                    sq_free[pb] = S.op("vector", lambda e, pb=pb, base=base: e.tensor_reduce(
                        out=stats[:, base + 1:base + 2],
                        in_=sq_scr[:, pb * D:(pb + 1) * D],
                        axis=AX, op=ALU.add), waits=[tsq], drain=True)
                    tvar = S.op("vector", lambda e, base=base: e.scalar_tensor_tensor(
                        out=stats[:, base + 4:base + 5], in0=stats[:, base + 1:base + 2],
                        scalar=1.0 / D, in1=stats[:, base + 3:base + 4],
                        op0=ALU.mult, op1=ALU.subtract), inc="dv", drain=True)
                    S.op("scalar", lambda e, base=base: e.activation(
                        out=stats[:, base + 5:base + 6], in_=stats[:, base + 4:base + 5],
                        func=AF.Ln, bias=epsc[:, 0:1], scale=1.0),
                        waits=[tvar, t_eps], drain=True)
                    trstd = S.op("scalar", lambda e, base=base: e.activation(
                        out=stats[:, base + 6:base + 7], in_=stats[:, base + 5:base + 6],
                        func=AF.Exp, bias=0.0, scale=-0.5), inc="ac", drain=True)
                    tneg = S.op("vector", lambda e, base=base: e.scalar_tensor_tensor(
                        out=stats[:, base + 7:base + 8], in0=stats[:, base + 2:base + 3],
                        scalar=-1.0, in1=stats[:, base + 6:base + 7],
                        op0=ALU.mult, op1=ALU.mult), waits=[trstd], inc="dv", drain=True)
                    txn = S.op("scalar", lambda e, pb=pb, base=base: e.activation(
                        out=xn_tm[:, pb * D:(pb + 1) * D],
                        in_=x_in[:, pb * D:(pb + 1) * D], func=AF.Identity,
                        bias=stats[:, base + 7:base + 8],
                        scale=stats[:, base + 6:base + 7]),
                        waits=[tneg, xntm_free[pb]], inc="ac", drain=True)
                    x_free[pb] = txn
                    ev_last = None
                    for c in range(DC):
                        ev = S.dma("sync", lambda e, pb=pb, c=c, tl_=tl_: e.dma_start(
                            out=xn_fm[:, c * D + tl_ * 128: c * D + (tl_ + 1) * 128],
                            in_=xn_tm[:, pb * D + c * 128: pb * D + (c + 1) * 128],
                            transpose=True),
                            waits=[txn, xnfm_rd], sem="tr")
                        ev_last = ev
                    xntm_free[pb] = ev_last
                    xnfm_toks.append(ev_last)
                tok0 = s * 1024
                vfm_toks = []
                for oc in range(DC):
                    pg = pr_ct % 2
                    pr_ct += 1
                    psu = (ps_a, ps_b)[pg]
                    mm_last = None
                    for ic in range(DC):
                        for h in range(2):
                            def mmf(e, psu=psu, ic=ic, oc=oc, h=h,
                                    st=(ic == 0), sp=(ic == DC - 1)):
                                return e.matmul(
                                    out=psu[:, h * 512:(h + 1) * 512],
                                    lhsT=wvT_b[:, ic * D + oc * 128:
                                               ic * D + oc * 128 + 128],
                                    rhs=xn_fm[:, ic * D + h * 512:
                                              ic * D + h * 512 + 512],
                                    start=st, stop=sp)
                            w = ([psAB_free[pg], t_wcast] + xnfm_toks
                                 if (ic == 0 and h == 0) else [])
                            mm_last = S.op("tensor", mmf, waits=w,
                                           inc="pe" if (ic == DC - 1 and h == 1)
                                           else None,
                                           drain=(ic == DC - 1 and h == 1))
                    xnfm_rd = mm_last
                    tev = S.op("vector", lambda e, psu=psu, oc=oc: e.tensor_copy(
                        out=V_fm[:, oc * D:(oc + 1) * D], in_=psu[:]),
                        waits=[mm_last, vfm_rd], inc="dv", drain=True)
                    psAB_free[pg] = tev
                    vfm_toks.append(tev)
                for tt in range(8):
                    pg = pr_ct % 2
                    pr_ct += 1
                    psu = (ps_a, ps_b)[pg]
                    mm_last = None
                    for oc in range(DC):
                        for h in range(2):
                            def mmf(e, psu=psu, oc=oc, tt=tt, h=h,
                                    st=(oc == 0), sp=(oc == DC - 1)):
                                return e.matmul(
                                    out=psu[:, h * 512:(h + 1) * 512],
                                    lhsT=V_fm[:, oc * D + tt * 128:
                                              oc * D + tt * 128 + 128],
                                    rhs=woT_b[:, oc * D + h * 512:
                                              oc * D + h * 512 + 512],
                                    start=st, stop=sp)
                            w = ([psAB_free[pg]] + vfm_toks
                                 if (oc == 0 and h == 0) else [])
                            mm_last = S.op("tensor", mmf, waits=w,
                                           inc="pe" if (oc == DC - 1 and h == 1)
                                           else None,
                                           drain=(oc == DC - 1 and h == 1))
                    vfm_rd = mm_last
                    tev = S.op("vector", lambda e, psu=psu, pg=pg: e.tensor_copy(
                        out=pstg[:, pg * D:(pg + 1) * D], in_=psu[:]),
                        waits=[mm_last, pstg_free[pg]], inc="dv", drain=True)
                    psAB_free[pg] = tev
                    tst = S.dma("gpsimd",
                                lambda e, tt=tt, tok0=tok0, pg=pg: e.dma_start(
                                    out=Vs[tok0 + tt * 128:
                                           tok0 + (tt + 1) * 128, :],
                                    in_=pstg[:, pg * D:(pg + 1) * D]),
                                waits=[tev])
                    pstg_free[pg] = tst
                for (wb, dstT) in ((wkT_b, Ks), (wqT_b, Qs)):
                    for oc in range(DC):
                        pg = pr_ct % 2
                        pr_ct += 1
                        psu = (ps_a, ps_b)[pg]
                        mm_last = None
                        for ic in range(DC):
                            for h in range(2):
                                def mmf(e, wb=wb, psu=psu, ic=ic, oc=oc, h=h,
                                        st=(ic == 0), sp=(ic == DC - 1)):
                                    return e.matmul(
                                        out=psu[:, h * 512:(h + 1) * 512],
                                        lhsT=wb[:, ic * D + oc * 128:
                                                ic * D + oc * 128 + 128],
                                        rhs=xn_fm[:, ic * D + h * 512:
                                                  ic * D + h * 512 + 512],
                                        start=st, stop=sp)
                                w = ([psAB_free[pg], t_wcast] + xnfm_toks
                                     if (ic == 0 and h == 0) else [])
                                mm_last = S.op("tensor", mmf, waits=w,
                                               inc="pe" if (ic == DC - 1 and h == 1)
                                               else None,
                                               drain=(ic == DC - 1 and h == 1))
                        xnfm_rd = mm_last
                        tev = S.op("vector", lambda e, psu=psu, pg=pg: e.tensor_copy(
                            out=pstg[:, pg * D:(pg + 1) * D], in_=psu[:]),
                            waits=[mm_last, pstg_free[pg]], inc="dv", drain=True)
                        psAB_free[pg] = tev
                        tst = S.dma("gpsimd",
                                    lambda e, dstT=dstT, oc=oc, tok0=tok0, pg=pg:
                                    e.dma_start(
                                        out=dstT[oc * 128:(oc + 1) * 128,
                                                 tok0:tok0 + 1024],
                                        in_=pstg[:, pg * D:(pg + 1) * D]),
                                    waits=[tev])
                        pstg_free[pg] = tst

            S.op("tensor", lambda e: e.nop(), inc="bar", drain=True)
            S.op("vector", lambda e: e.nop(), inc="bar", drain=True)
            S.op("scalar", lambda e: e.nop(), inc="bar", drain=True)
            S.op("gpsimd", lambda e: e.nop(),
                 waits=S.queue_drained("gpsimd"), inc="bar")
            bar_a = counts["bar"]
            psAB_last = dict(psAB_free)

            blockA = PA.enter_context(nc.Block("phaseA"))
            S.emit(blockA)

        with ExitStack() as PB:
            qbuf = sb("qbuf", [128, 2 * DC * 256], BF16, PB)
            kst = sb("kst", [128, 2 * DC * 512], BF16, PB)
            vst = sb("vst", [128, 2 * 4 * D], BF16, PB)
            pt = sb("pt", [128, 4 * 256], BF16, PB)
            xres = sb("xres", [128, 2 * D], F32, PB)
            x1st = sb("x1st", [128, 2 * D], F32, PB)
            den = sb("den", [128, 32], F32, PB)

            S = Sched(sems, counts, waited, qrot)
            barw = [("sync", "bar", bar_a, True)]

            qb_free = [None, None]
            kst_free = [None, None]
            vst_free = [None, None]
            pt_free = [None] * 4
            psS_free = [None, None]
            psC_free = {0: psAB_last[0], 1: psAB_last[1]}
            psD_free = [None, None]
            xres_free = [None, None]
            x1_free = [None, None]
            sc = 0
            ptn = 0
            for jq in range(NSB):
                qs_ = jq % 2
                tq = S.dma("sync", lambda e, jq=jq, qs_=qs_: e.dma_start(
                    out=qbuf[:, qs_ * DC * 256:(qs_ + 1) * DC * 256]
                        .rearrange("p (c n) -> p c n", n=256),
                    in_=Qs.rearrange("(c p) m -> p c m", p=128)[
                        :, :, jq * 256:(jq + 1) * 256]),
                    waits=barw + [qb_free[qs_]])
                txr = [None, None]
                for qt in (0, 1):
                    g = 2 * jq + qt
                    txr[qt] = S.dma("sync", lambda e, g=g, qt=qt: e.dma_start(
                        out=xres[:, qt * D:(qt + 1) * D],
                        in_=x_h[g * 128:(g + 1) * 128, :]),
                        waits=barw + [xres_free[qt]])
                nkb = 2 * jq + 2
                nch = (nkb + 3) // 4
                pv_last = {0: None, 1: None}
                dn_last = {0: None, 1: None}
                pend = None
                last_sc = None

                def issue_pv(info):
                    rd = None
                    for (qt, lo) in info["cols"]:
                        kg = info["kg"]
                        psu = (ps_a, ps_b)[qt]
                        first = (kg == 0)
                        last = (kg == 2 * jq + qt)
                        mm2 = None
                        for h in range(2):
                            def mmf(e, psu=psu, lo=lo, ksv=info["ks"],
                                    kb=info["kb"], h=h, st=first, sp=last):
                                return e.matmul(
                                    out=psu[:, h * 512:(h + 1) * 512],
                                    lhsT=pt[:, lo:lo + 128],
                                    rhs=vst[:, ksv * 4096 + kb * 1024 + h * 512:
                                            ksv * 4096 + kb * 1024 + h * 512 + 512],
                                    start=st, stop=sp)
                            w_ = (([info["ttok"], info["tv"]]
                                   + ([psC_free[qt]] if first else []))
                                  if h == 0 else [])
                            mm2 = S.op("tensor", mmf, waits=w_,
                                       inc="pe" if h == 1 else None,
                                       drain=(h == 1 and last))
                        pv_last[qt] = mm2
                        vst_free[info["ks"]] = mm2

                        def mmd(e, lo=lo, qt=qt, st=first, sp=last):
                            return e.matmul(out=ps_dd[qt][:, 0:8],
                                            lhsT=pt[:, lo:lo + 128],
                                            rhs=ones8[:], start=st, stop=sp)
                        tdm = S.op("tensor", mmd,
                                   waits=([t_ones, psD_free[qt]]
                                          if first else []),
                                   inc="pe" if last else None, drain=last)
                        dn_last[qt] = tdm
                        rd = tdm
                    pt_free[info["p"]] = rd

                for ch in range(nch):
                    ks_ = ch % 2
                    tk = S.dma("sync", lambda e, ch=ch, ks_=ks_: e.dma_start(
                        out=kst[:, ks_ * DC * 512:(ks_ + 1) * DC * 512]
                            .rearrange("p (c n) -> p c n", n=512),
                        in_=Ks.rearrange("(c p) m -> p c m", p=128)[
                            :, :, ch * 512:(ch + 1) * 512]),
                        waits=barw + [kst_free[ks_]])
                    tv = S.dma("gpsimd", lambda e, ch=ch, ks_=ks_: e.dma_start(
                        out=vst[:, ks_ * 4096:(ks_ + 1) * 4096]
                            .rearrange("p (t n) -> p t n", n=1024),
                        in_=Vs.rearrange("(t p) n -> p t n", p=128)[
                            :, ch * 4:(ch + 1) * 4, :]),
                        waits=barw + [vst_free[ks_]])
                    for kb in range(4):
                        kg = ch * 4 + kb
                        if kg >= nkb:
                            break
                        narrow = (kg == nkb - 1)
                        c0, w = (128, 128) if narrow else (0, 256)
                        sb_ = sc % 2
                        sc += 1
                        mml = None
                        for dc in range(DC):
                            def mmf(e, dc=dc, sb_=sb_, ks_=ks_, kb=kb, qs_=qs_,
                                    c0=c0, w=w, st=(dc == 0), sp=(dc == DC - 1)):
                                return e.matmul(
                                    out=ps_s[sb_][:, 0:w],
                                    lhsT=kst[:, ks_ * DC * 512 + dc * 512 + kb * 128:
                                             ks_ * DC * 512 + dc * 512 + kb * 128 + 128],
                                    rhs=qbuf[:, qs_ * DC * 256 + dc * 256 + c0:
                                             qs_ * DC * 256 + dc * 256 + c0 + w],
                                    start=st, stop=sp)
                            w_ = [tk, tq, psS_free[sb_]] if dc == 0 else []
                            mml = S.op("tensor", mmf, waits=w_,
                                       inc="pe" if dc == DC - 1 else None,
                                       drain=(dc == DC - 1))
                        last_sc = mml
                        kst_free[ks_] = mml
                        p_ = ptn % 4
                        ptn += 1
                        texp = S.op("scalar", lambda e, p_=p_, sb_=sb_, w=w:
                                    e.activation(
                                        out=pt[:, p_ * 256:p_ * 256 + w],
                                        in_=ps_s[sb_][:, 0:w],
                                        func=AF.Exp, bias=0.0, scale=1.0),
                                    waits=[mml, pt_free[p_]], inc="ac", drain=True)
                        psS_free[sb_] = texp
                        ttok = texp
                        if kg >= nkb - 2:
                            mc0 = p_ * 256
                            ttok = S.op("vector", lambda e, mc0=mc0:
                                        e.tensor_tensor(
                                            out=pt[:, mc0:mc0 + 128],
                                            in0=pt[:, mc0:mc0 + 128],
                                            in1=maskT[:], op=ALU.mult),
                                        waits=[texp, t_mask], inc="dv", drain=True)
                        if narrow:
                            cols = [(1, p_ * 256)]
                        else:
                            cols = [(0, p_ * 256), (1, p_ * 256 + 128)]
                        cur = {"kg": kg, "kb": kb, "ks": ks_, "p": p_,
                               "ttok": ttok, "tv": tv, "cols": cols}
                        if pend is not None:
                            issue_pv(pend)
                        pend = cur
                if pend is not None:
                    issue_pv(pend)
                    pend = None
                qb_free[qs_] = last_sc

                S.op("vector", lambda e: e.tensor_copy(
                    out=den[:, 0:8], in_=ps_d0[:, 0:8]),
                    waits=[dn_last[0], dn_last[1]], drain=True)
                tde = S.op("vector", lambda e: e.tensor_copy(
                    out=den[:, 8:16], in_=ps_d1[:, 0:8]), inc="dv", drain=True)
                trc = S.op("vector", lambda e: e.reciprocal(
                    out=den[:, 16:32], in_=den[:, 0:16]), inc="dv", drain=True)
                psD_free[0] = tde
                psD_free[1] = tde
                for qt in (0, 1):
                    g = 2 * jq + qt
                    tep = S.op("vector", lambda e, qt=qt: e.scalar_tensor_tensor(
                        out=x1st[:, qt * D:(qt + 1) * D],
                        in0=(ps_a, ps_b)[qt][:],
                        scalar=den[:, 16 + qt * 8:17 + qt * 8],
                        in1=xres[:, qt * D:(qt + 1) * D],
                        op0=ALU.mult, op1=ALU.add),
                        waits=[pv_last[qt], trc, txr[qt], x1_free[qt]],
                        inc="dv", drain=True)
                    psC_free[qt] = tep
                    xres_free[qt] = tep
                    x1_free[qt] = S.dma("gpsimd", lambda e, g=g, qt=qt: e.dma_start(
                        out=X1s[g * 128:(g + 1) * 128, :],
                        in_=x1st[:, qt * D:(qt + 1) * D]), waits=[tep])

            S.op("tensor", lambda e: e.nop(), inc="bar", drain=True)
            S.op("vector", lambda e: e.nop(), inc="bar", drain=True)
            S.op("scalar", lambda e: e.nop(), inc="bar", drain=True)
            S.op("gpsimd", lambda e: e.nop(),
                 waits=S.queue_drained("gpsimd"), inc="bar")
            bar_b = counts["bar"]

            blockB = PB.enter_context(nc.Block("phaseB"))
            S.emit(blockB)

        with ExitStack() as PC:
            xn_res = sb("xn_res", [128, NT * D], BF16, PC)
            x1_in = sb("x1_in", [128, 2 * D], F32, PC)
            xh = sb("xh", [128, D], F32, PC)
            sq2 = sb("sq2", [128, 2 * D], F32, PC)
            cstat = sb("cstat", [128, 32], F32, PC)
            absm = sb("absm", [128, 72], F32, PC)
            xq_bf = sb("xq_bf", [128, D], BF16, PC)
            xq_fm = sb("xq_fm", [128, D], BF16, PC)
            y_out = sb("y_out", [128, 2 * D], F32, PC)

            S = Sched(sems, counts, waited, qrot)
            barw = [("sync", "bar", bar_b, True)]

            S.op("vector", lambda e: e.memset(cc_sb[:], 0.0), waits=barw,
                 inc="dv", drain=True)

            x1_free = [None, None]
            xh_free = None
            sq2_free = [None, None]
            xn_toks = []
            for t in range(NT):
                pb = t % 2
                base = (t % 2) * 16
                tx = S.dma("sync", lambda e, t=t, pb=pb: e.dma_start(
                    out=x1_in[:, pb * D:(pb + 1) * D],
                    in_=X1s[t * 128:(t + 1) * 128, :]), waits=barw + [x1_free[pb]])
                S.op("vector", lambda e, pb=pb, base=base: e.tensor_reduce(
                    out=cstat[:, base:base + 1], in_=x1_in[:, pb * D:(pb + 1) * D],
                    axis=AX, op=ALU.add), waits=[tx], drain=True)
                tsq = S.op("scalar", lambda e, pb=pb: e.activation(
                    out=sq2[:, pb * D:(pb + 1) * D],
                    in_=x1_in[:, pb * D:(pb + 1) * D], func=AF.Square),
                    waits=barw + [tx, sq2_free[pb]], inc="ac", drain=True)
                S.op("vector", lambda e, base=base: e.tensor_scalar(
                    out=cstat[:, base + 2:base + 3], in0=cstat[:, base:base + 1],
                    scalar1=1.0 / D, scalar2=None, op0=ALU.mult), drain=True)
                S.op("vector", lambda e, base=base: e.tensor_tensor(
                    out=cstat[:, base + 3:base + 4], in0=cstat[:, base + 2:base + 3],
                    in1=cstat[:, base + 2:base + 3], op=ALU.mult), drain=True)
                S.op("vector", lambda e, pb=pb, base=base: e.tensor_reduce(
                    out=cstat[:, base + 1:base + 2], in_=sq2[:, pb * D:(pb + 1) * D],
                    axis=AX, op=ALU.add), waits=[tsq], drain=True)
                tvar = S.op("vector", lambda e, base=base: e.scalar_tensor_tensor(
                    out=cstat[:, base + 4:base + 5], in0=cstat[:, base + 1:base + 2],
                    scalar=1.0 / D, in1=cstat[:, base + 3:base + 4],
                    op0=ALU.mult, op1=ALU.subtract), inc="dv", drain=True)
                S.op("scalar", lambda e, base=base: e.activation(
                    out=cstat[:, base + 5:base + 6], in_=cstat[:, base + 4:base + 5],
                    func=AF.Ln, bias=epsc[:, 1:2], scale=1.001),
                    waits=[tvar], drain=True)
                ts_ = S.op("scalar", lambda e, base=base: e.activation(
                    out=cstat[:, base + 6:base + 7], in_=cstat[:, base + 5:base + 6],
                    func=AF.Exp, bias=0.0, scale=-0.5), inc="ac", drain=True)
                tneg = S.op("vector", lambda e, base=base: e.scalar_tensor_tensor(
                    out=cstat[:, base + 7:base + 8], in0=cstat[:, base + 2:base + 3],
                    scalar=-1.0, in1=cstat[:, base + 6:base + 7],
                    op0=ALU.mult, op1=ALU.mult), waits=[ts_], inc="dv", drain=True)
                txh = S.op("scalar", lambda e, pb=pb, base=base: e.activation(
                    out=xh[:], in_=x1_in[:, pb * D:(pb + 1) * D], func=AF.Identity,
                    bias=cstat[:, base + 7:base + 8],
                    scale=cstat[:, base + 6:base + 7]),
                    waits=[tneg, xh_free], inc="ac", drain=True)
                x1_free[pb] = txh
                S.op("vector", lambda e, pb=pb: e.tensor_tensor(
                    out=sq2[:, pb * D:(pb + 1) * D], in0=xh[:], in1=gam_bc[:],
                    op=ALU.mult), waits=[txh], drain=True)
                taff = S.op("vector", lambda e, t=t, pb=pb: e.tensor_tensor(
                    out=xn_res[:, t * D:(t + 1) * D], in0=sq2[:, pb * D:(pb + 1) * D],
                    in1=bet_bc[:], op=ALU.add), inc="dv", drain=True)
                sq2_free[pb] = taff
                xh_free = taff
                xn_toks.append(taff)
                S.op("vector", lambda e, t=t: e.tensor_reduce(
                    out=absm[:, t:t + 1], in_=xn_res[:, t * D:(t + 1) * D],
                    axis=AX, op=ALU.max, apply_absolute_value=True), drain=True)

            tam = S.op("vector", lambda e: e.tensor_reduce(
                out=absm[:, 64:65], in_=absm[:, 0:64], axis=AX, op=ALU.max),
                inc="dv", drain=True)
            t_hb2 = S.dma("gpsimd", lambda e: e.dma_start(
                out=hbounce, in_=absm[:, 64:65], allow_slow_non_contiguous=True),
                waits=[tam], sem="st")
            t_rr2 = S.dma("sync", lambda e: e.dma_start(out=redrow[:], in_=hbounce),
                          waits=[t_hb2])
            tred = S.op("vector", lambda e: e.tensor_reduce(
                out=cc_sb[0:1, 0:1], in_=redrow[:], axis=AX, op=ALU.max),
                waits=[t_rr2], inc="dv", drain=True)
            tcc1 = S.dma("gpsimd", lambda e: e.dma_start(out=cc_in[:, :], in_=cc_sb[:]),
                         waits=[tred], sem="st")
            tcol = S.op("gpsimd", lambda e: e.collective_compute(
                "AllReduce", ALU.max, replica_groups=[list(range(8))],
                ins=[cc_in.ap().opt()], outs=[cc_out.ap().opt()]),
                waits=[tcc1], inc="cc", asynchronous=True)
            t_gbc = S.dma("sync", lambda e: e.dma_start(
                out=scal[:, 2:3], in_=cc_out.ap()[0:1, 0:1].partition_broadcast(128),
                allow_slow_non_contiguous=True), waits=[tcol])
            S.op("vector", lambda e: e.reciprocal(out=scal[:, 5:6], in_=scal[:, 2:3]),
                 waits=[t_gbc], drain=True)
            ts1 = S.op("vector", lambda e: e.tensor_scalar(
                out=scal[:, 3:4], in0=scal[:, 5:6], scalar1=128.0, scalar2=None,
                op0=ALU.mult), inc="dv", drain=True)
            ts2 = S.op("vector", lambda e: e.scalar_tensor_tensor(
                out=scal[:, 4:5], in0=scal[:, 2:3], scalar=1.0 / 128.0,
                in1=scal[:, 1:2], op0=ALU.mult, op1=ALU.mult), inc="dv", drain=True)

            xq_free = None
            xqfm_rd = None
            yout_free = [None, None]
            psC_free = {0: None, 1: None}
            for t in range(NT):
                pb = t % 2
                tx = S.dma("sync", lambda e, t=t, pb=pb: e.dma_start(
                    out=x1_in[:, pb * D:(pb + 1) * D],
                    in_=X1s[t * 128:(t + 1) * 128, :]), waits=[x1_free[pb]])
                tq1 = S.op("vector", lambda e, t=t: e.tensor_scalar(
                    out=xq_bf[:], in0=xn_res[:, t * D:(t + 1) * D],
                    scalar1=scal[:, 3:4], scalar2=CLIP, op0=ALU.mult, op1=ALU.min),
                    waits=[ts1, xn_toks[t], xq_free])
                tq2 = S.op("vector", lambda e: e.tensor_scalar(
                    out=xq_bf[:], in0=xq_bf[:], scalar1=-CLIP, scalar2=None,
                    op0=ALU.max), inc="dv", drain=True)
                evl = None
                for c in range(DC):
                    ev = S.dma("scalar", lambda e, c=c: e.dma_start(
                        out=xq_fm[:, c * 128:(c + 1) * 128],
                        in_=xq_bf[:, c * 128:(c + 1) * 128],
                        transpose=True),
                        waits=[tq2, xqfm_rd if c == 0 else None], sem="tr")
                    evl = ev
                xq_free = evl
                pg = t % 2
                psu = (ps_a, ps_b)[pg]
                mml = None
                for ic in range(DC):
                    for h in range(2):
                        def mmf(e, psu=psu, ic=ic, h=h, st=(ic == 0), sp=(ic == DC - 1)):
                            return e.matmul(
                                out=psu[:, h * 512:(h + 1) * 512],
                                lhsT=xq_fm[:, ic * 128:(ic + 1) * 128],
                                rhs=w_tern[:, ic * D + h * 512: ic * D + h * 512 + 512],
                                start=st, stop=sp)
                        w = [evl, psC_free[pg]] if ic == 0 and h == 0 else []
                        mml = S.op("tensor", mmf, waits=w,
                                   inc="pe" if (ic == DC - 1 and h == 1) else None,
                                   drain=(ic == DC - 1 and h == 1))
                xqfm_rd = mml
                tep = S.op("vector", lambda e, psu=psu, pb=pb: e.scalar_tensor_tensor(
                    out=y_out[:, pb * D:(pb + 1) * D], in0=psu[:],
                    scalar=scal[:, 4:5], in1=x1_in[:, pb * D:(pb + 1) * D],
                    op0=ALU.mult, op1=ALU.add),
                    waits=[mml, tx, ts2, yout_free[pb]], inc="dv", drain=True)
                psC_free[pg] = tep
                x1_free[pb] = tep
                yout_free[pb] = S.dma("gpsimd", lambda e, t=t, pb=pb: e.dma_start(
                    out=out_h[t * 128:(t + 1) * 128, :],
                    in_=y_out[:, pb * D:(pb + 1) * D]), waits=[tep], sem="st")

            S.op("gpsimd", lambda e: e.nop(), waits=S.queue_drained("gpsimd"))

            blockC = PC.enter_context(nc.Block("phaseC"))
            S.emit(blockC)

    return nc


_CONSTS = None


def _consts():
    global _CONSTS
    if _CONSTS is None:
        mt = np.triu(np.ones((128, 128), dtype=np.float32), 0)
        _CONSTS = (mt.astype(ml_dtypes.bfloat16),
                   np.ones((128, 8), dtype=ml_dtypes.bfloat16))
    return _CONSTS


def kernel(L0_attn_in_features, L0_attn_w_q, L0_attn_w_k, L0_attn_w_v,
           L0_attn_w_o, L0_ffn_weight, L0_ffn_ln_gamma, L0_ffn_ln_beta):
    x = np.ascontiguousarray(np.asarray(L0_attn_in_features, dtype=np.float32))
    wqT = np.ascontiguousarray(np.asarray(L0_attn_w_q, dtype=np.float32).T)
    wkT = np.ascontiguousarray(np.asarray(L0_attn_w_k, dtype=np.float32).T)
    wvT = np.ascontiguousarray(np.asarray(L0_attn_w_v, dtype=np.float32).T)
    woT = np.ascontiguousarray(np.asarray(L0_attn_w_o, dtype=np.float32).T)
    wf = np.ascontiguousarray(np.asarray(L0_ffn_weight, dtype=np.float32))
    gam = np.ascontiguousarray(np.asarray(L0_ffn_ln_gamma, dtype=np.float32).reshape(1, D))
    bet = np.ascontiguousarray(np.asarray(L0_ffn_ln_beta, dtype=np.float32).reshape(1, D))
    maskT, ones8 = _consts()

    nc = build_program()
    in_maps = []
    for c in range(8):
        in_maps.append({
            "x": np.ascontiguousarray(x[c]),
            "wqT": wqT, "wkT": wkT, "wvT": wvT, "woT": woT, "wf": wf,
            "gam": gam, "bet": bet, "maskT": maskT, "ones8": ones8,
        })
    res = run_bass_kernel_spmd(nc, in_maps, core_ids=list(range(8)))
    return np.stack([res.results[c]["out"] for c in range(8)], axis=0)


# version 12
# speedup vs baseline: 1.4732x; 1.0398x over previous
import numpy as np
import ml_dtypes
from contextlib import ExitStack

import concourse.bass as bass
import concourse.mybir as mybir
from concourse.bass_utils import run_bass_kernel_spmd

F32 = mybir.dt.float32
BF16 = mybir.dt.bfloat16
AF = mybir.ActivationFunctionType
ALU = mybir.AluOpType
AX = mybir.AxisListType.X

T = 8192
D = 1024
NT = T // 128
DC = D // 128
NSA = 8
NSB = 32
NEG = -1e30
CLIP = 128.0 - 1e-5

ENGINES = ("sync", "tensor", "vector", "scalar", "gpsimd")

QPOOL = {"sync": 6, "gpsimd": 5, "scalar": 4}
QSEMS = {e: [f"q_{e}{i}" for i in range(n)] for e, n in QPOOL.items()}


class Sched:
    def __init__(self, sems, counts, waited, qrot):
        self.sems = sems
        self.cnt = counts
        self.waited = waited
        self.qrot = qrot
        self.ops = {e: [] for e in ENGINES}

    def op(self, eng, fn, waits=(), inc=None, inc_by=1, asynchronous=False,
           drain=False):
        self.ops[eng].append((fn, [w for w in waits if w is not None],
                              (inc, inc_by) if inc else None, drain))
        if inc:
            self.cnt[inc] += inc_by
            return (eng, inc, self.cnt[inc], asynchronous)
        return None

    def dma(self, eng, fn, waits=(), sem=None):
        pool = QSEMS[eng]
        s = pool[self.qrot[eng] % len(pool)]
        self.qrot[eng] += 1
        ws = list(waits)
        if self.cnt[s]:
            ws.append((eng, s, self.cnt[s], True))
        return self.op(eng, fn, ws, inc=s, inc_by=16, asynchronous=True)

    def queue_drained(self, eng):
        return [(eng, s, self.cnt[s], True) for s in QSEMS[eng] if self.cnt[s]]

    def emit(self, block):
        for name in ENGINES:
            ops = self.ops[name]
            if not ops:
                continue
            waited = self.waited.setdefault(name, {})
            sems = self.sems

            def body(eng, ops=ops, waited=waited, name=name):
                for fn, waits, inc, drain in ops:
                    for (peng, sname, val, is_async) in waits:
                        if peng == name and not is_async and name != "tensor":
                            continue
                        if waited.get(sname, -1) >= val:
                            continue
                        eng.wait_ge(sems[sname], val)
                        waited[sname] = val
                    inst = fn(eng)
                    if drain:
                        inst = eng.drain()
                    if inc is not None:
                        inst.then_inc(sems[inc[0]], inc[1])

            getattr(block, name)(body)


def build_program():
    nc = bass.Bass("TRN2", target_bir_lowering=False)

    x_h = nc.dram_tensor("x", [T, D], F32, kind="ExternalInput").ap()
    wqT_h = nc.dram_tensor("wqT", [D, D], F32, kind="ExternalInput").ap()
    wkT_h = nc.dram_tensor("wkT", [D, D], F32, kind="ExternalInput").ap()
    wvT_h = nc.dram_tensor("wvT", [D, D], F32, kind="ExternalInput").ap()
    woT_h = nc.dram_tensor("woT", [D, D], F32, kind="ExternalInput").ap()
    wf_h = nc.dram_tensor("wf", [D, D], F32, kind="ExternalInput").ap()
    gam_h = nc.dram_tensor("gam", [1, D], F32, kind="ExternalInput").ap()
    bet_h = nc.dram_tensor("bet", [1, D], F32, kind="ExternalInput").ap()
    maskT_h = nc.dram_tensor("maskT", [128, 128], BF16, kind="ExternalInput").ap()
    ones8_h = nc.dram_tensor("ones8", [128, 8], BF16, kind="ExternalInput").ap()
    out_h = nc.dram_tensor("out", [T, D], F32, kind="ExternalOutput").ap()

    Qs = nc.dram_tensor("Qs", [D, T], BF16, kind="Internal").ap()
    Ks = nc.dram_tensor("Ks", [D, T], BF16, kind="Internal").ap()
    Vs = nc.dram_tensor("Vs", [T, D], BF16, kind="Internal").ap()
    X1s = nc.dram_tensor("X1s", [T, D], F32, kind="Internal").ap()
    Xns = nc.dram_tensor("Xns", [T, D], BF16, kind="Internal").ap()
    hbounce = nc.dram_tensor("hbounce", [1, 128], F32, kind="Internal").ap()
    smd = nc.dram_tensor("smd", [1, 8], F32, kind="Internal").ap()
    cc_in = nc.dram_tensor("cc_in", [1, 512], F32, kind="Internal")
    cc_out = nc.dram_tensor("cc_out", [1, 512], F32, kind="Internal")

    with ExitStack() as P:
        def sb(name, shape, dtype, stack):
            return stack.enter_context(nc.sbuf_tensor(name, shape, dtype))

        maskT = sb("maskT_s", [128, 128], BF16, P)
        ones8 = sb("ones8_s", [128, 8], BF16, P)
        gam_bc = sb("gam_bc", [128, D], F32, P)
        bet_bc = sb("bet_bc", [128, D], F32, P)
        w_tern = sb("w_tern", [128, DC * D], BF16, P)
        scal = sb("scal", [128, 8], F32, P)
        sm = sb("sm", [1, 8], F32, P)
        cc_sb = sb("cc_sb", [1, 512], F32, P)
        redrow = sb("redrow", [1, 128], F32, P)
        epsc = sb("epsc", [128, 2], F32, P)
        absm = sb("absm", [128, 72], F32, P)

        ps_a = P.enter_context(nc.psum_tensor("ps_a", [128, 1024], F32))
        ps_b = P.enter_context(nc.psum_tensor("ps_b", [128, 1024], F32))
        ps_s0 = P.enter_context(nc.psum_tensor("ps_s0", [128, 256], F32))
        ps_s1 = P.enter_context(nc.psum_tensor("ps_s1", [128, 256], F32))
        ps_s = (ps_s0, ps_s1)
        ps_d0 = P.enter_context(nc.psum_tensor("ps_d0", [128, 8], F32))
        ps_d1 = P.enter_context(nc.psum_tensor("ps_d1", [128, 8], F32))
        ps_dd = (ps_d0, ps_d1)

        semnames = (["pe", "dv", "ac", "cc", "bar"]
                    + [s for pool in QSEMS.values() for s in pool])
        sems = {n: P.enter_context(nc.semaphore(n)) for n in semnames}
        counts = {n: 0 for n in semnames}
        waited = {}
        qrot = {e: 0 for e in QPOOL}

        with ExitStack() as PA:
            wqT_b = sb("wqT_b", [128, DC * D], BF16, PA)
            wkT_b = sb("wkT_b", [128, DC * D], BF16, PA)
            wvT_b = sb("wvT_b", [128, DC * D], BF16, PA)
            woT_b = sb("woT_b", [128, DC * D], BF16, PA)
            V_fm = sb("V_fm", [128, DC * D], BF16, PA)
            stg = sb("stg", [128, 2 * D], F32, PA)
            x_in = sb("x_in", [128, 2 * D], F32, PA)
            sq_scr = sb("sq_scr", [128, 2 * D], F32, PA)
            stats = sb("stats", [128, 32], F32, PA)
            xn_tm = sb("xn_tm", [128, 2 * D], BF16, PA)
            xn_fm = sb("xn_fm", [128, 2 * DC * D], BF16, PA)
            pstg = sb("pstg", [128, 2 * D], BF16, PA)
            abss = sb("abss", [128, 9], F32, PA)
            zs = sb("zs", [128, D], F32, PA)
            z2 = sb("z2", [128, D], F32, PA)

            S = Sched(sems, counts, waited, qrot)

            t_mask = S.dma("sync", lambda e: e.dma_start(out=maskT[:], in_=maskT_h))
            t_ones = S.dma("sync", lambda e: e.dma_start(out=ones8[:], in_=ones8_h))
            S.dma("sync", lambda e: e.dma_start(
                out=gam_bc[:], in_=gam_h.partition_broadcast(128)))
            S.dma("sync", lambda e: e.dma_start(
                out=bet_bc[:], in_=bet_h.partition_broadcast(128)))
            S.op("vector", lambda e: e.memset(epsc[:, 0:1], 1e-5))
            t_eps = S.op("vector", lambda e: e.memset(epsc[:, 1:2], 1e-8),
                         inc="dv", drain=True)

            stg_free = [None, None]
            for wi, (src, dst, scale) in enumerate([
                    (wqT_h, wqT_b, 1.0), (wkT_h, wkT_b, 1.0 / 32.0),
                    (wvT_h, wvT_b, 1.0), (woT_h, woT_b, 1.0)]):
                for c in range(DC):
                    slot = (wi * DC + c) % 2
                    tl = S.dma("sync", lambda e, src=src, c=c, slot=slot: e.dma_start(
                        out=stg[:, slot * D:(slot + 1) * D],
                        in_=src[c * 128:(c + 1) * 128, :]),
                        waits=[stg_free[slot]])
                    stg_free[slot] = S.op(
                        "vector", lambda e, dst=dst, c=c, slot=slot, scale=scale:
                        e.tensor_scalar(out=dst[:, c * D:(c + 1) * D],
                                        in0=stg[:, slot * D:(slot + 1) * D],
                                        scalar1=scale, scalar2=None, op0=ALU.mult),
                        waits=[tl], inc="dv", drain=True)
            t_wcast = stg_free[1]

            for c in range(DC):
                slot = c % 2
                tl = S.dma("sync", lambda e, c=c, slot=slot: e.dma_start(
                    out=stg[:, slot * D:(slot + 1) * D],
                    in_=wf_h[c * 128:(c + 1) * 128, :]), waits=[stg_free[slot]])
                stg_free[slot] = S.op("vector", lambda e, c=c, slot=slot: e.tensor_reduce(
                    out=abss[:, c:c + 1], in_=stg[:, slot * D:(slot + 1) * D],
                    axis=AX, op=ALU.add, apply_absolute_value=True),
                    waits=[tl], inc="dv", drain=True)
            t_rs = S.op("vector", lambda e: e.tensor_reduce(
                out=abss[:, 8:9], in_=abss[:, 0:8], axis=AX, op=ALU.add),
                waits=[stg_free[0], stg_free[1]], inc="dv", drain=True)
            t_hb = S.dma("gpsimd", lambda e: e.dma_start(
                out=hbounce, in_=abss[:, 8:9], allow_slow_non_contiguous=True),
                waits=[t_rs], sem="st")
            t_rr = S.dma("sync", lambda e: e.dma_start(out=redrow[:], in_=hbounce),
                         waits=[t_hb])
            S.op("vector", lambda e: e.tensor_reduce(
                out=sm[0:1, 4:5], in_=redrow[:], axis=AX, op=ALU.add),
                waits=[t_rr], drain=True)
            S.op("vector", lambda e: e.tensor_scalar(
                out=sm[0:1, 1:2], in0=sm[0:1, 4:5], scalar1=1.0 / (D * D),
                scalar2=None, op0=ALU.mult), drain=True)
            t_thr = S.op("vector", lambda e: e.tensor_scalar(
                out=sm[0:1, 0:1], in0=sm[0:1, 1:2], scalar1=1e-5, scalar2=0.5,
                op0=ALU.add, op1=ALU.mult), inc="dv", drain=True)
            t_smd = S.dma("gpsimd", lambda e: e.dma_start(out=smd[0:1, 0:2],
                                                          in_=sm[0:1, 0:2]),
                          waits=[t_thr], sem="st")
            t_thrb = S.dma("sync", lambda e: e.dma_start(
                out=scal[:, 0:2], in_=smd[0:1, 0:2].partition_broadcast(128),
                allow_slow_non_contiguous=True), waits=[t_smd])
            t_nthr = S.op("vector", lambda e: e.tensor_scalar(
                out=scal[:, 6:7], in0=scal[:, 0:1], scalar1=-1.0, scalar2=None,
                op0=ALU.mult), waits=[t_thrb], inc="dv", drain=True)
            for c in range(DC):
                slot = c % 2
                tl = S.dma("sync", lambda e, c=c, slot=slot: e.dma_start(
                    out=stg[:, slot * D:(slot + 1) * D],
                    in_=wf_h[c * 128:(c + 1) * 128, :]), waits=[stg_free[slot]])
                S.op("vector", lambda e, slot=slot: e.tensor_scalar(
                    out=zs[:], in0=stg[:, slot * D:(slot + 1) * D],
                    scalar1=scal[:, 0:1], scalar2=None, op0=ALU.is_ge),
                    waits=[tl, t_nthr])
                S.op("vector", lambda e, slot=slot: e.tensor_scalar(
                    out=z2[:], in0=stg[:, slot * D:(slot + 1) * D],
                    scalar1=scal[:, 6:7], scalar2=None, op0=ALU.is_le),
                    drain=True)
                stg_free[slot] = S.op("vector", lambda e, c=c: e.tensor_tensor(
                    out=w_tern[:, c * D:(c + 1) * D], in0=zs[:], in1=z2[:],
                    op=ALU.subtract), inc="dv", drain=True)

            x_free = [None, None]
            xntm_free = [None, None]
            sq_free = [None, None]
            psAB_free = {0: None, 1: None}
            pstg_free = [None, None]
            xnfm_rd = [None, None]
            vfm_rd = None
            pr_ct = 0
            for s in range(NSA):
                sf = (s % 2) * DC * D
                xnfm_toks = []
                for tl_ in range(8):
                    gt = s * 8 + tl_
                    pb = gt % 2
                    base = pb * 16
                    tx = S.dma("sync", lambda e, gt=gt, pb=pb: e.dma_start(
                        out=x_in[:, pb * D:(pb + 1) * D],
                        in_=x_h[gt * 128:(gt + 1) * 128, :]), waits=[x_free[pb]])
                    S.op("vector", lambda e, pb=pb, base=base: e.tensor_reduce(
                        out=stats[:, base:base + 1], in_=x_in[:, pb * D:(pb + 1) * D],
                        axis=AX, op=ALU.add), waits=[tx], drain=True)
                    tsq = S.op("scalar", lambda e, pb=pb, base=base: e.activation(
                        out=sq_scr[:, pb * D:(pb + 1) * D],
                        in_=x_in[:, pb * D:(pb + 1) * D], func=AF.Square),
                        waits=[tx, sq_free[pb]], inc="ac", drain=True)
                    S.op("vector", lambda e, base=base: e.tensor_scalar(
                        out=stats[:, base + 2:base + 3], in0=stats[:, base:base + 1],
                        scalar1=1.0 / D, scalar2=None, op0=ALU.mult), drain=True)
                    S.op("vector", lambda e, base=base: e.tensor_tensor(
                        out=stats[:, base + 3:base + 4], in0=stats[:, base + 2:base + 3],
                        in1=stats[:, base + 2:base + 3], op=ALU.mult), drain=True)
                    sq_free[pb] = S.op("vector", lambda e, pb=pb, base=base: e.tensor_reduce(
                        out=stats[:, base + 1:base + 2],
                        in_=sq_scr[:, pb * D:(pb + 1) * D],
                        axis=AX, op=ALU.add), waits=[tsq], drain=True)
                    tvar = S.op("vector", lambda e, base=base: e.scalar_tensor_tensor(
                        out=stats[:, base + 4:base + 5], in0=stats[:, base + 1:base + 2],
                        scalar=1.0 / D, in1=stats[:, base + 3:base + 4],
                        op0=ALU.mult, op1=ALU.subtract), inc="dv", drain=True)
                    S.op("scalar", lambda e, base=base: e.activation(
                        out=stats[:, base + 5:base + 6], in_=stats[:, base + 4:base + 5],
                        func=AF.Ln, bias=epsc[:, 0:1], scale=1.0),
                        waits=[tvar, t_eps], drain=True)
                    trstd = S.op("scalar", lambda e, base=base: e.activation(
                        out=stats[:, base + 6:base + 7], in_=stats[:, base + 5:base + 6],
                        func=AF.Exp, bias=0.0, scale=-0.5), inc="ac", drain=True)
                    tneg = S.op("vector", lambda e, base=base: e.scalar_tensor_tensor(
                        out=stats[:, base + 7:base + 8], in0=stats[:, base + 2:base + 3],
                        scalar=-1.0, in1=stats[:, base + 6:base + 7],
                        op0=ALU.mult, op1=ALU.mult), waits=[trstd], inc="dv", drain=True)
                    txn = S.op("scalar", lambda e, pb=pb, base=base: e.activation(
                        out=xn_tm[:, pb * D:(pb + 1) * D],
                        in_=x_in[:, pb * D:(pb + 1) * D], func=AF.Identity,
                        bias=stats[:, base + 7:base + 8],
                        scale=stats[:, base + 6:base + 7]),
                        waits=[tneg, xntm_free[pb]], inc="ac", drain=True)
                    x_free[pb] = txn
                    ev_last = None
                    for c in range(DC):
                        ev = S.dma("sync", lambda e, pb=pb, c=c, tl_=tl_, sf=sf:
                                   e.dma_start(
                            out=xn_fm[:, sf + c * D + tl_ * 128:
                                      sf + c * D + (tl_ + 1) * 128],
                            in_=xn_tm[:, pb * D + c * 128: pb * D + (c + 1) * 128],
                            transpose=True),
                            waits=[txn, xnfm_rd[s % 2]], sem="tr")
                        ev_last = ev
                    xntm_free[pb] = ev_last
                    xnfm_toks.append(ev_last)
                tok0 = s * 1024
                vfm_toks = []
                for oc in range(DC):
                    pg = pr_ct % 2
                    pr_ct += 1
                    psu = (ps_a, ps_b)[pg]
                    mm_last = None
                    for ic in range(DC):
                        for h in range(2):
                            def mmf(e, psu=psu, ic=ic, oc=oc, h=h, sf=sf,
                                    st=(ic == 0), sp=(ic == DC - 1)):
                                return e.matmul(
                                    out=psu[:, h * 512:(h + 1) * 512],
                                    lhsT=wvT_b[:, ic * D + oc * 128:
                                               ic * D + oc * 128 + 128],
                                    rhs=xn_fm[:, sf + ic * D + h * 512:
                                              sf + ic * D + h * 512 + 512],
                                    start=st, stop=sp)
                            w = ([psAB_free[pg], t_wcast] + xnfm_toks
                                 if (ic == 0 and h == 0) else [])
                            mm_last = S.op("tensor", mmf, waits=w,
                                           inc="pe" if (ic == DC - 1 and h == 1)
                                           else None,
                                           drain=(ic == DC - 1 and h == 1))
                    xnfm_rd[s % 2] = mm_last
                    tev = S.op("vector", lambda e, psu=psu, oc=oc: e.tensor_copy(
                        out=V_fm[:, oc * D:(oc + 1) * D], in_=psu[:]),
                        waits=[mm_last, vfm_rd], inc="dv", drain=True)
                    psAB_free[pg] = tev
                    vfm_toks.append(tev)
                for tt in range(8):
                    pg = pr_ct % 2
                    pr_ct += 1
                    psu = (ps_a, ps_b)[pg]
                    mm_last = None
                    for oc in range(DC):
                        for h in range(2):
                            def mmf(e, psu=psu, oc=oc, tt=tt, h=h,
                                    st=(oc == 0), sp=(oc == DC - 1)):
                                return e.matmul(
                                    out=psu[:, h * 512:(h + 1) * 512],
                                    lhsT=V_fm[:, oc * D + tt * 128:
                                              oc * D + tt * 128 + 128],
                                    rhs=woT_b[:, oc * D + h * 512:
                                              oc * D + h * 512 + 512],
                                    start=st, stop=sp)
                            w = ([psAB_free[pg]] + vfm_toks
                                 if (oc == 0 and h == 0) else [])
                            mm_last = S.op("tensor", mmf, waits=w,
                                           inc="pe" if (oc == DC - 1 and h == 1)
                                           else None,
                                           drain=(oc == DC - 1 and h == 1))
                    vfm_rd = mm_last
                    tev = S.op("vector", lambda e, psu=psu, pg=pg: e.tensor_copy(
                        out=pstg[:, pg * D:(pg + 1) * D], in_=psu[:]),
                        waits=[mm_last, pstg_free[pg]], inc="dv", drain=True)
                    psAB_free[pg] = tev
                    tst = S.dma("gpsimd",
                                lambda e, tt=tt, tok0=tok0, pg=pg: e.dma_start(
                                    out=Vs[tok0 + tt * 128:
                                           tok0 + (tt + 1) * 128, :],
                                    in_=pstg[:, pg * D:(pg + 1) * D]),
                                waits=[tev])
                    pstg_free[pg] = tst
                for (wb, dstT) in ((wkT_b, Ks), (wqT_b, Qs)):
                    for oc in range(DC):
                        pg = pr_ct % 2
                        pr_ct += 1
                        psu = (ps_a, ps_b)[pg]
                        mm_last = None
                        for ic in range(DC):
                            for h in range(2):
                                def mmf(e, wb=wb, psu=psu, ic=ic, oc=oc, h=h,
                                        sf=sf, st=(ic == 0), sp=(ic == DC - 1)):
                                    return e.matmul(
                                        out=psu[:, h * 512:(h + 1) * 512],
                                        lhsT=wb[:, ic * D + oc * 128:
                                                ic * D + oc * 128 + 128],
                                        rhs=xn_fm[:, sf + ic * D + h * 512:
                                                  sf + ic * D + h * 512 + 512],
                                        start=st, stop=sp)
                                w = ([psAB_free[pg], t_wcast] + xnfm_toks
                                     if (ic == 0 and h == 0) else [])
                                mm_last = S.op("tensor", mmf, waits=w,
                                               inc="pe" if (ic == DC - 1 and h == 1)
                                               else None,
                                               drain=(ic == DC - 1 and h == 1))
                        xnfm_rd[s % 2] = mm_last
                        tev = S.op("vector", lambda e, psu=psu, pg=pg: e.tensor_copy(
                            out=pstg[:, pg * D:(pg + 1) * D], in_=psu[:]),
                            waits=[mm_last, pstg_free[pg]], inc="dv", drain=True)
                        psAB_free[pg] = tev
                        tst = S.dma("gpsimd",
                                    lambda e, dstT=dstT, oc=oc, tok0=tok0, pg=pg:
                                    e.dma_start(
                                        out=dstT[oc * 128:(oc + 1) * 128,
                                                 tok0:tok0 + 1024],
                                        in_=pstg[:, pg * D:(pg + 1) * D]),
                                    waits=[tev])
                        pstg_free[pg] = tst

            S.op("tensor", lambda e: e.nop(), inc="bar", drain=True)
            S.op("vector", lambda e: e.nop(), inc="bar", drain=True)
            S.op("scalar", lambda e: e.nop(), inc="bar", drain=True)
            S.op("gpsimd", lambda e: e.nop(),
                 waits=S.queue_drained("gpsimd"), inc="bar")
            bar_a = counts["bar"]
            psAB_last = dict(psAB_free)

            blockA = PA.enter_context(nc.Block("phaseA"))
            S.emit(blockA)

        with ExitStack() as PB:
            qbuf = sb("qbuf", [128, 2 * DC * 256], BF16, PB)
            kst = sb("kst", [128, 2 * DC * 512], BF16, PB)
            vst = sb("vst", [128, 2 * 4 * D], BF16, PB)
            pt = sb("pt", [128, 4 * 256], BF16, PB)
            xres = sb("xres", [128, 2 * D], F32, PB)
            x1st = sb("x1st", [128, 2 * D], F32, PB)
            den = sb("den", [128, 32], F32, PB)
            bst = sb("bst", [128, 32], F32, PB)
            sqb = sb("sqb", [128, 2 * D], F32, PB)
            xhb = sb("xhb", [128, 2 * D], F32, PB)
            xnb = sb("xnb", [128, 2 * D], BF16, PB)

            S = Sched(sems, counts, waited, qrot)
            barw = [("sync", "bar", bar_a, True)]

            qb_free = [None, None]
            kst_free = [None, None]
            vst_free = [None, None]
            pt_free = [None] * 4
            psS_free = [None, None]
            psC_free = {0: psAB_last[0], 1: psAB_last[1]}
            psD_free = [None, None]
            xres_free = [None, None]
            x1_free = [[], []]
            bst_free = [None] * 4
            sqb_free = [None, None]
            xhb_free = [None, None]
            xnb_free = [None, None]
            sc = 0
            ptn = 0
            for jq in range(NSB):
                qs_ = jq % 2
                tq = S.dma("sync", lambda e, jq=jq, qs_=qs_: e.dma_start(
                    out=qbuf[:, qs_ * DC * 256:(qs_ + 1) * DC * 256]
                        .rearrange("p (c n) -> p c n", n=256),
                    in_=Qs.rearrange("(c p) m -> p c m", p=128)[
                        :, :, jq * 256:(jq + 1) * 256]),
                    waits=barw + [qb_free[qs_]])
                txr = [None, None]
                for qt in (0, 1):
                    g = 2 * jq + qt
                    txr[qt] = S.dma("sync", lambda e, g=g, qt=qt: e.dma_start(
                        out=xres[:, qt * D:(qt + 1) * D],
                        in_=x_h[g * 128:(g + 1) * 128, :]),
                        waits=barw + [xres_free[qt]])
                nkb = 2 * jq + 2
                nch = (nkb + 3) // 4
                pv_last = {0: None, 1: None}
                dn_last = {0: None, 1: None}
                pend = None
                last_sc = None

                def issue_pv(info):
                    rd = None
                    for (qt, lo) in info["cols"]:
                        kg = info["kg"]
                        psu = (ps_a, ps_b)[qt]
                        first = (kg == 0)
                        last = (kg == 2 * jq + qt)
                        mm2 = None
                        for h in range(2):
                            def mmf(e, psu=psu, lo=lo, ksv=info["ks"],
                                    kb=info["kb"], h=h, st=first, sp=last):
                                return e.matmul(
                                    out=psu[:, h * 512:(h + 1) * 512],
                                    lhsT=pt[:, lo:lo + 128],
                                    rhs=vst[:, ksv * 4096 + kb * 1024 + h * 512:
                                            ksv * 4096 + kb * 1024 + h * 512 + 512],
                                    start=st, stop=sp)
                            w_ = (([info["ttok"], info["tv"]]
                                   + ([psC_free[qt]] if first else []))
                                  if h == 0 else [])
                            mm2 = S.op("tensor", mmf, waits=w_,
                                       inc="pe" if h == 1 else None,
                                       drain=(h == 1 and last))
                        pv_last[qt] = mm2
                        vst_free[info["ks"]] = mm2

                        def mmd(e, lo=lo, qt=qt, st=first, sp=last):
                            return e.matmul(out=ps_dd[qt][:, 0:8],
                                            lhsT=pt[:, lo:lo + 128],
                                            rhs=ones8[:], start=st, stop=sp)
                        tdm = S.op("tensor", mmd,
                                   waits=([t_ones, psD_free[qt]]
                                          if first else []),
                                   inc="pe" if last else None, drain=last)
                        dn_last[qt] = tdm
                        rd = tdm
                    pt_free[info["p"]] = rd

                for ch in range(nch):
                    ks_ = ch % 2
                    tk = S.dma("sync", lambda e, ch=ch, ks_=ks_: e.dma_start(
                        out=kst[:, ks_ * DC * 512:(ks_ + 1) * DC * 512]
                            .rearrange("p (c n) -> p c n", n=512),
                        in_=Ks.rearrange("(c p) m -> p c m", p=128)[
                            :, :, ch * 512:(ch + 1) * 512]),
                        waits=barw + [kst_free[ks_]])
                    tv = S.dma("gpsimd", lambda e, ch=ch, ks_=ks_: e.dma_start(
                        out=vst[:, ks_ * 4096:(ks_ + 1) * 4096]
                            .rearrange("p (t n) -> p t n", n=1024),
                        in_=Vs.rearrange("(t p) n -> p t n", p=128)[
                            :, ch * 4:(ch + 1) * 4, :]),
                        waits=barw + [vst_free[ks_]])
                    for kb in range(4):
                        kg = ch * 4 + kb
                        if kg >= nkb:
                            break
                        narrow = (kg == nkb - 1)
                        c0, w = (128, 128) if narrow else (0, 256)
                        sb_ = sc % 2
                        sc += 1
                        mml = None
                        for dc in range(DC):
                            def mmf(e, dc=dc, sb_=sb_, ks_=ks_, kb=kb, qs_=qs_,
                                    c0=c0, w=w, st=(dc == 0), sp=(dc == DC - 1)):
                                return e.matmul(
                                    out=ps_s[sb_][:, 0:w],
                                    lhsT=kst[:, ks_ * DC * 512 + dc * 512 + kb * 128:
                                             ks_ * DC * 512 + dc * 512 + kb * 128 + 128],
                                    rhs=qbuf[:, qs_ * DC * 256 + dc * 256 + c0:
                                             qs_ * DC * 256 + dc * 256 + c0 + w],
                                    start=st, stop=sp)
                            w_ = [tk, tq, psS_free[sb_]] if dc == 0 else []
                            mml = S.op("tensor", mmf, waits=w_,
                                       inc="pe" if dc == DC - 1 else None,
                                       drain=(dc == DC - 1))
                        last_sc = mml
                        kst_free[ks_] = mml
                        p_ = ptn % 4
                        ptn += 1
                        texp = S.op("scalar", lambda e, p_=p_, sb_=sb_, w=w:
                                    e.activation(
                                        out=pt[:, p_ * 256:p_ * 256 + w],
                                        in_=ps_s[sb_][:, 0:w],
                                        func=AF.Exp, bias=0.0, scale=1.0),
                                    waits=[mml, pt_free[p_]], inc="ac", drain=True)
                        psS_free[sb_] = texp
                        ttok = texp
                        if kg >= nkb - 2:
                            mc0 = p_ * 256
                            ttok = S.op("vector", lambda e, mc0=mc0:
                                        e.tensor_tensor(
                                            out=pt[:, mc0:mc0 + 128],
                                            in0=pt[:, mc0:mc0 + 128],
                                            in1=maskT[:], op=ALU.mult),
                                        waits=[texp, t_mask], inc="dv", drain=True)
                        if narrow:
                            cols = [(1, p_ * 256)]
                        else:
                            cols = [(0, p_ * 256), (1, p_ * 256 + 128)]
                        cur = {"kg": kg, "kb": kb, "ks": ks_, "p": p_,
                               "ttok": ttok, "tv": tv, "cols": cols}
                        if pend is not None:
                            issue_pv(pend)
                        pend = cur
                if pend is not None:
                    issue_pv(pend)
                    pend = None
                qb_free[qs_] = last_sc

                S.op("vector", lambda e: e.tensor_copy(
                    out=den[:, 0:8], in_=ps_d0[:, 0:8]),
                    waits=[dn_last[0], dn_last[1]], drain=True)
                tde = S.op("vector", lambda e: e.tensor_copy(
                    out=den[:, 8:16], in_=ps_d1[:, 0:8]), inc="dv", drain=True)
                trc = S.op("vector", lambda e: e.reciprocal(
                    out=den[:, 16:32], in_=den[:, 0:16]), inc="dv", drain=True)
                psD_free[0] = tde
                psD_free[1] = tde
                for qt in (0, 1):
                    g = 2 * jq + qt
                    tep = S.op("vector", lambda e, qt=qt: e.scalar_tensor_tensor(
                        out=x1st[:, qt * D:(qt + 1) * D],
                        in0=(ps_a, ps_b)[qt][:],
                        scalar=den[:, 16 + qt * 8:17 + qt * 8],
                        in1=xres[:, qt * D:(qt + 1) * D],
                        op0=ALU.mult, op1=ALU.add),
                        waits=[pv_last[qt], trc, txr[qt]] + x1_free[qt],
                        inc="dv", drain=True)
                    psC_free[qt] = tep
                    xres_free[qt] = tep
                    tsx = S.dma("gpsimd", lambda e, g=g, qt=qt: e.dma_start(
                        out=X1s[g * 128:(g + 1) * 128, :],
                        in_=x1st[:, qt * D:(qt + 1) * D]), waits=[tep])
                    sg = g % 2
                    base = (g % 4) * 8
                    S.op("vector", lambda e, qt=qt, base=base: e.tensor_reduce(
                        out=bst[:, base:base + 1], in_=x1st[:, qt * D:(qt + 1) * D],
                        axis=AX, op=ALU.add), waits=[bst_free[g % 4]], drain=True)
                    tsq = S.op("scalar", lambda e, qt=qt, sg=sg: e.activation(
                        out=sqb[:, sg * D:(sg + 1) * D],
                        in_=x1st[:, qt * D:(qt + 1) * D], func=AF.Square),
                        waits=[tep, sqb_free[sg]], inc="ac", drain=True)
                    S.op("vector", lambda e, base=base: e.tensor_scalar(
                        out=bst[:, base + 2:base + 3], in0=bst[:, base:base + 1],
                        scalar1=1.0 / D, scalar2=None, op0=ALU.mult), drain=True)
                    S.op("vector", lambda e, base=base: e.tensor_tensor(
                        out=bst[:, base + 3:base + 4], in0=bst[:, base + 2:base + 3],
                        in1=bst[:, base + 2:base + 3], op=ALU.mult), drain=True)
                    S.op("vector", lambda e, sg=sg, base=base: e.tensor_reduce(
                        out=bst[:, base + 1:base + 2], in_=sqb[:, sg * D:(sg + 1) * D],
                        axis=AX, op=ALU.add), waits=[tsq], drain=True)
                    tvar = S.op("vector", lambda e, base=base: e.scalar_tensor_tensor(
                        out=bst[:, base + 4:base + 5], in0=bst[:, base + 1:base + 2],
                        scalar=1.0 / D, in1=bst[:, base + 3:base + 4],
                        op0=ALU.mult, op1=ALU.subtract), inc="dv", drain=True)
                    S.op("scalar", lambda e, base=base: e.activation(
                        out=bst[:, base + 5:base + 6], in_=bst[:, base + 4:base + 5],
                        func=AF.Ln, bias=epsc[:, 1:2], scale=1.001),
                        waits=[tvar], drain=True)
                    tse = S.op("scalar", lambda e, base=base: e.activation(
                        out=bst[:, base + 6:base + 7], in_=bst[:, base + 5:base + 6],
                        func=AF.Exp, bias=0.0, scale=-0.5), inc="ac", drain=True)
                    tng = S.op("vector", lambda e, base=base: e.scalar_tensor_tensor(
                        out=bst[:, base + 7:base + 8], in0=bst[:, base + 2:base + 3],
                        scalar=-1.0, in1=bst[:, base + 6:base + 7],
                        op0=ALU.mult, op1=ALU.mult), waits=[tse], inc="dv", drain=True)
                    txh = S.op("scalar", lambda e, qt=qt, sg=sg, base=base:
                               e.activation(
                        out=xhb[:, sg * D:(sg + 1) * D],
                        in_=x1st[:, qt * D:(qt + 1) * D], func=AF.Identity,
                        bias=bst[:, base + 7:base + 8],
                        scale=bst[:, base + 6:base + 7]),
                        waits=[tng, xhb_free[sg]], inc="ac", drain=True)
                    tg1 = S.op("vector", lambda e, sg=sg: e.tensor_tensor(
                        out=sqb[:, sg * D:(sg + 1) * D],
                        in0=xhb[:, sg * D:(sg + 1) * D], in1=gam_bc[:],
                        op=ALU.mult), waits=[txh], drain=True)
                    txn_ = S.op("vector", lambda e, sg=sg: e.tensor_tensor(
                        out=xnb[:, sg * D:(sg + 1) * D],
                        in0=sqb[:, sg * D:(sg + 1) * D], in1=bet_bc[:],
                        op=ALU.add), waits=[xnb_free[sg]], inc="dv", drain=True)
                    S.op("vector", lambda e, sg=sg, g=g: e.tensor_reduce(
                        out=absm[:, g:g + 1], in_=xnb[:, sg * D:(sg + 1) * D],
                        axis=AX, op=ALU.max, apply_absolute_value=True), drain=True)
                    tsn = S.dma("gpsimd", lambda e, g=g, sg=sg: e.dma_start(
                        out=Xns[g * 128:(g + 1) * 128, :],
                        in_=xnb[:, sg * D:(sg + 1) * D]), waits=[txn_])
                    bst_free[g % 4] = txh
                    sqb_free[sg] = txn_
                    xhb_free[sg] = tg1
                    xnb_free[sg] = tsn
                    x1_free[qt] = [tsx, txh]

            S.op("tensor", lambda e: e.nop(), inc="bar", drain=True)
            S.op("vector", lambda e: e.nop(), inc="bar", drain=True)
            S.op("scalar", lambda e: e.nop(), inc="bar", drain=True)
            S.op("gpsimd", lambda e: e.nop(),
                 waits=S.queue_drained("gpsimd"), inc="bar")
            bar_b = counts["bar"]

            blockB = PB.enter_context(nc.Block("phaseB"))
            S.emit(blockB)

        with ExitStack() as PC:
            x1_in = sb("x1_in", [128, 2 * D], F32, PC)
            xnld = sb("xnld", [128, 2 * D], BF16, PC)
            xq_bf = sb("xq_bf", [128, D], BF16, PC)
            xq_fm = sb("xq_fm", [128, D], BF16, PC)
            y_out = sb("y_out", [128, 2 * D], F32, PC)

            S = Sched(sems, counts, waited, qrot)
            barw = [("sync", "bar", bar_b, True)]

            S.op("vector", lambda e: e.memset(cc_sb[:], 0.0), waits=barw,
                 inc="dv", drain=True)

            x1_free = [None, None]

            tam = S.op("vector", lambda e: e.tensor_reduce(
                out=absm[:, 64:65], in_=absm[:, 0:64], axis=AX, op=ALU.max),
                inc="dv", drain=True)
            t_hb2 = S.dma("gpsimd", lambda e: e.dma_start(
                out=hbounce, in_=absm[:, 64:65], allow_slow_non_contiguous=True),
                waits=[tam], sem="st")
            t_rr2 = S.dma("sync", lambda e: e.dma_start(out=redrow[:], in_=hbounce),
                          waits=[t_hb2])
            tred = S.op("vector", lambda e: e.tensor_reduce(
                out=cc_sb[0:1, 0:1], in_=redrow[:], axis=AX, op=ALU.max),
                waits=[t_rr2], inc="dv", drain=True)
            tcc1 = S.dma("gpsimd", lambda e: e.dma_start(out=cc_in[:, :], in_=cc_sb[:]),
                         waits=[tred], sem="st")
            tcol = S.op("gpsimd", lambda e: e.collective_compute(
                "AllReduce", ALU.max, replica_groups=[list(range(8))],
                ins=[cc_in.ap().opt()], outs=[cc_out.ap().opt()]),
                waits=[tcc1], inc="cc", asynchronous=True)
            t_gbc = S.dma("sync", lambda e: e.dma_start(
                out=scal[:, 2:3], in_=cc_out.ap()[0:1, 0:1].partition_broadcast(128),
                allow_slow_non_contiguous=True), waits=[tcol])
            S.op("vector", lambda e: e.reciprocal(out=scal[:, 5:6], in_=scal[:, 2:3]),
                 waits=[t_gbc], drain=True)
            ts1 = S.op("vector", lambda e: e.tensor_scalar(
                out=scal[:, 3:4], in0=scal[:, 5:6], scalar1=128.0, scalar2=None,
                op0=ALU.mult), inc="dv", drain=True)
            ts2 = S.op("vector", lambda e: e.scalar_tensor_tensor(
                out=scal[:, 4:5], in0=scal[:, 2:3], scalar=1.0 / 128.0,
                in1=scal[:, 1:2], op0=ALU.mult, op1=ALU.mult), inc="dv", drain=True)

            xq_free = None
            xqfm_rd = None
            yout_free = [None, None]
            psC_free = {0: None, 1: None}
            xnld_free = [None, None]
            for t in range(NT):
                pb = t % 2
                tx = S.dma("sync", lambda e, t=t, pb=pb: e.dma_start(
                    out=x1_in[:, pb * D:(pb + 1) * D],
                    in_=X1s[t * 128:(t + 1) * 128, :]),
                    waits=barw + [x1_free[pb]])
                txnl = S.dma("sync", lambda e, t=t, pb=pb: e.dma_start(
                    out=xnld[:, pb * D:(pb + 1) * D],
                    in_=Xns[t * 128:(t + 1) * 128, :]),
                    waits=barw + [xnld_free[pb]])
                tq1 = S.op("vector", lambda e, pb=pb: e.tensor_scalar(
                    out=xq_bf[:], in0=xnld[:, pb * D:(pb + 1) * D],
                    scalar1=scal[:, 3:4], scalar2=CLIP, op0=ALU.mult, op1=ALU.min),
                    waits=[ts1, txnl, xq_free])
                tq2 = S.op("vector", lambda e: e.tensor_scalar(
                    out=xq_bf[:], in0=xq_bf[:], scalar1=-CLIP, scalar2=None,
                    op0=ALU.max), inc="dv", drain=True)
                evl = None
                for c in range(DC):
                    ev = S.dma("scalar", lambda e, c=c: e.dma_start(
                        out=xq_fm[:, c * 128:(c + 1) * 128],
                        in_=xq_bf[:, c * 128:(c + 1) * 128],
                        transpose=True),
                        waits=[tq2, xqfm_rd if c == 0 else None], sem="tr")
                    evl = ev
                xq_free = evl
                pg = t % 2
                psu = (ps_a, ps_b)[pg]
                mml = None
                for ic in range(DC):
                    for h in range(2):
                        def mmf(e, psu=psu, ic=ic, h=h, st=(ic == 0), sp=(ic == DC - 1)):
                            return e.matmul(
                                out=psu[:, h * 512:(h + 1) * 512],
                                lhsT=xq_fm[:, ic * 128:(ic + 1) * 128],
                                rhs=w_tern[:, ic * D + h * 512: ic * D + h * 512 + 512],
                                start=st, stop=sp)
                        w = [evl, psC_free[pg]] if ic == 0 and h == 0 else []
                        mml = S.op("tensor", mmf, waits=w,
                                   inc="pe" if (ic == DC - 1 and h == 1) else None,
                                   drain=(ic == DC - 1 and h == 1))
                xqfm_rd = mml
                xnld_free[pb] = tq2
                tep = S.op("vector", lambda e, psu=psu, pb=pb: e.scalar_tensor_tensor(
                    out=y_out[:, pb * D:(pb + 1) * D], in0=psu[:],
                    scalar=scal[:, 4:5], in1=x1_in[:, pb * D:(pb + 1) * D],
                    op0=ALU.mult, op1=ALU.add),
                    waits=[mml, tx, ts2, yout_free[pb]], inc="dv", drain=True)
                psC_free[pg] = tep
                x1_free[pb] = tep
                yout_free[pb] = S.dma("gpsimd", lambda e, t=t, pb=pb: e.dma_start(
                    out=out_h[t * 128:(t + 1) * 128, :],
                    in_=y_out[:, pb * D:(pb + 1) * D]), waits=[tep], sem="st")

            S.op("gpsimd", lambda e: e.nop(), waits=S.queue_drained("gpsimd"))

            blockC = PC.enter_context(nc.Block("phaseC"))
            S.emit(blockC)

    return nc


_CONSTS = None


def _consts():
    global _CONSTS
    if _CONSTS is None:
        mt = np.triu(np.ones((128, 128), dtype=np.float32), 0)
        _CONSTS = (mt.astype(ml_dtypes.bfloat16),
                   np.ones((128, 8), dtype=ml_dtypes.bfloat16))
    return _CONSTS


def kernel(L0_attn_in_features, L0_attn_w_q, L0_attn_w_k, L0_attn_w_v,
           L0_attn_w_o, L0_ffn_weight, L0_ffn_ln_gamma, L0_ffn_ln_beta):
    x = np.ascontiguousarray(np.asarray(L0_attn_in_features, dtype=np.float32))
    wqT = np.ascontiguousarray(np.asarray(L0_attn_w_q, dtype=np.float32).T)
    wkT = np.ascontiguousarray(np.asarray(L0_attn_w_k, dtype=np.float32).T)
    wvT = np.ascontiguousarray(np.asarray(L0_attn_w_v, dtype=np.float32).T)
    woT = np.ascontiguousarray(np.asarray(L0_attn_w_o, dtype=np.float32).T)
    wf = np.ascontiguousarray(np.asarray(L0_ffn_weight, dtype=np.float32))
    gam = np.ascontiguousarray(np.asarray(L0_ffn_ln_gamma, dtype=np.float32).reshape(1, D))
    bet = np.ascontiguousarray(np.asarray(L0_ffn_ln_beta, dtype=np.float32).reshape(1, D))
    maskT, ones8 = _consts()

    nc = build_program()
    in_maps = []
    for c in range(8):
        in_maps.append({
            "x": np.ascontiguousarray(x[c]),
            "wqT": wqT, "wkT": wkT, "wvT": wvT, "woT": woT, "wf": wf,
            "gam": gam, "bet": bet, "maskT": maskT, "ones8": ones8,
        })
    res = run_bass_kernel_spmd(nc, in_maps, core_ids=list(range(8)))
    return np.stack([res.results[c]["out"] for c in range(8)], axis=0)


# version 13
# speedup vs baseline: 1.5200x; 1.0318x over previous
import numpy as np
import ml_dtypes
from contextlib import ExitStack

import concourse.bass as bass
import concourse.mybir as mybir
from concourse.bass_utils import run_bass_kernel_spmd

F32 = mybir.dt.float32
BF16 = mybir.dt.bfloat16
AF = mybir.ActivationFunctionType
ALU = mybir.AluOpType
AX = mybir.AxisListType.X

T = 8192
D = 1024
NT = T // 128
DC = D // 128
NSA = 8
NSB = 32
NEG = -1e30
CLIP = 128.0 - 1e-5

ENGINES = ("sync", "tensor", "vector", "scalar", "gpsimd")

QPOOL = {"sync": 6, "gpsimd": 5, "scalar": 4}
QSEMS = {e: [f"q_{e}{i}" for i in range(n)] for e, n in QPOOL.items()}


class Sched:
    def __init__(self, sems, counts, waited, qrot):
        self.sems = sems
        self.cnt = counts
        self.waited = waited
        self.qrot = qrot
        self.ops = {e: [] for e in ENGINES}

    def op(self, eng, fn, waits=(), inc=None, inc_by=1, asynchronous=False,
           drain=False):
        self.ops[eng].append((fn, [w for w in waits if w is not None],
                              (inc, inc_by) if inc else None, drain))
        if inc:
            self.cnt[inc] += inc_by
            return (eng, inc, self.cnt[inc], asynchronous)
        return None

    def dma(self, eng, fn, waits=(), sem=None):
        pool = QSEMS[eng]
        s = pool[self.qrot[eng] % len(pool)]
        self.qrot[eng] += 1
        ws = list(waits)
        if self.cnt[s]:
            ws.append((eng, s, self.cnt[s], True))
        return self.op(eng, fn, ws, inc=s, inc_by=16, asynchronous=True)

    def queue_drained(self, eng):
        return [(eng, s, self.cnt[s], True) for s in QSEMS[eng] if self.cnt[s]]

    def emit(self, block):
        for name in ENGINES:
            ops = self.ops[name]
            if not ops:
                continue
            waited = self.waited.setdefault(name, {})
            sems = self.sems

            def body(eng, ops=ops, waited=waited, name=name):
                for fn, waits, inc, drain in ops:
                    for (peng, sname, val, is_async) in waits:
                        if peng == name and not is_async and name != "tensor":
                            continue
                        if waited.get(sname, -1) >= val:
                            continue
                        eng.wait_ge(sems[sname], val)
                        waited[sname] = val
                    inst = fn(eng)
                    if drain:
                        inst = eng.drain()
                    if inc is not None:
                        inst.then_inc(sems[inc[0]], inc[1])

            getattr(block, name)(body)


def build_program():
    nc = bass.Bass("TRN2", target_bir_lowering=False)

    x_h = nc.dram_tensor("x", [T, D], F32, kind="ExternalInput").ap()
    wqT_h = nc.dram_tensor("wqT", [D, D], F32, kind="ExternalInput").ap()
    wkT_h = nc.dram_tensor("wkT", [D, D], F32, kind="ExternalInput").ap()
    wvT_h = nc.dram_tensor("wvT", [D, D], F32, kind="ExternalInput").ap()
    woT_h = nc.dram_tensor("woT", [D, D], F32, kind="ExternalInput").ap()
    wf_h = nc.dram_tensor("wf", [D, D], F32, kind="ExternalInput").ap()
    gam_h = nc.dram_tensor("gam", [1, D], F32, kind="ExternalInput").ap()
    bet_h = nc.dram_tensor("bet", [1, D], F32, kind="ExternalInput").ap()
    maskT_h = nc.dram_tensor("maskT", [128, 128], BF16, kind="ExternalInput").ap()
    ones8_h = nc.dram_tensor("ones8", [128, 8], BF16, kind="ExternalInput").ap()
    out_h = nc.dram_tensor("out", [T, D], F32, kind="ExternalOutput").ap()

    Qs = nc.dram_tensor("Qs", [D, T], BF16, kind="Internal").ap()
    Ks = nc.dram_tensor("Ks", [D, T], BF16, kind="Internal").ap()
    Vs = nc.dram_tensor("Vs", [T, D], BF16, kind="Internal").ap()
    X1s = nc.dram_tensor("X1s", [T, D], F32, kind="Internal").ap()
    Xns = nc.dram_tensor("Xns", [D, T], BF16, kind="Internal").ap()
    hbounce = nc.dram_tensor("hbounce", [1, 128], F32, kind="Internal").ap()
    smd = nc.dram_tensor("smd", [1, 8], F32, kind="Internal").ap()
    cc_in = nc.dram_tensor("cc_in", [1, 512], F32, kind="Internal")
    cc_out = nc.dram_tensor("cc_out", [1, 512], F32, kind="Internal")

    with ExitStack() as P:
        def sb(name, shape, dtype, stack):
            return stack.enter_context(nc.sbuf_tensor(name, shape, dtype))

        maskT = sb("maskT_s", [128, 128], BF16, P)
        ones8 = sb("ones8_s", [128, 8], BF16, P)
        gam_bc = sb("gam_bc", [128, D], F32, P)
        bet_bc = sb("bet_bc", [128, D], F32, P)
        w_tern = sb("w_tern", [128, DC * D], BF16, P)
        scal = sb("scal", [128, 8], F32, P)
        sm = sb("sm", [1, 8], F32, P)
        cc_sb = sb("cc_sb", [1, 512], F32, P)
        redrow = sb("redrow", [1, 128], F32, P)
        epsc = sb("epsc", [128, 2], F32, P)
        absm = sb("absm", [128, 72], F32, P)

        ps_a = P.enter_context(nc.psum_tensor("ps_a", [128, 1024], F32))
        ps_b = P.enter_context(nc.psum_tensor("ps_b", [128, 1024], F32))
        ps_s0 = P.enter_context(nc.psum_tensor("ps_s0", [128, 256], F32))
        ps_s1 = P.enter_context(nc.psum_tensor("ps_s1", [128, 256], F32))
        ps_s = (ps_s0, ps_s1)
        ps_d0 = P.enter_context(nc.psum_tensor("ps_d0", [128, 8], F32))
        ps_d1 = P.enter_context(nc.psum_tensor("ps_d1", [128, 8], F32))
        ps_dd = (ps_d0, ps_d1)

        semnames = (["pe", "dv", "ac", "cc", "bar"]
                    + [s for pool in QSEMS.values() for s in pool])
        sems = {n: P.enter_context(nc.semaphore(n)) for n in semnames}
        counts = {n: 0 for n in semnames}
        waited = {}
        qrot = {e: 0 for e in QPOOL}

        with ExitStack() as PA:
            wqT_b = sb("wqT_b", [128, DC * D], BF16, PA)
            wkT_b = sb("wkT_b", [128, DC * D], BF16, PA)
            wvT_b = sb("wvT_b", [128, DC * D], BF16, PA)
            woT_b = sb("woT_b", [128, DC * D], BF16, PA)
            V_fm = sb("V_fm", [128, DC * D], BF16, PA)
            stg = sb("stg", [128, 2 * D], F32, PA)
            x_in = sb("x_in", [128, 2 * D], F32, PA)
            sq_scr = sb("sq_scr", [128, 2 * D], F32, PA)
            stats = sb("stats", [128, 32], F32, PA)
            xn_tm = sb("xn_tm", [128, 2 * D], BF16, PA)
            xn_fm = sb("xn_fm", [128, 2 * DC * D], BF16, PA)
            pstg = sb("pstg", [128, 2 * D], BF16, PA)
            abss = sb("abss", [128, 9], F32, PA)
            zs = sb("zs", [128, D], F32, PA)
            z2 = sb("z2", [128, D], F32, PA)

            S = Sched(sems, counts, waited, qrot)

            t_mask = S.dma("sync", lambda e: e.dma_start(out=maskT[:], in_=maskT_h))
            t_ones = S.dma("sync", lambda e: e.dma_start(out=ones8[:], in_=ones8_h))
            S.dma("sync", lambda e: e.dma_start(
                out=gam_bc[:], in_=gam_h.partition_broadcast(128)))
            S.dma("sync", lambda e: e.dma_start(
                out=bet_bc[:], in_=bet_h.partition_broadcast(128)))
            S.op("vector", lambda e: e.memset(epsc[:, 0:1], 1e-5))
            t_eps = S.op("vector", lambda e: e.memset(epsc[:, 1:2], 1e-8),
                         inc="dv", drain=True)

            stg_free = [None, None]
            for wi, (src, dst, scale) in enumerate([
                    (wqT_h, wqT_b, 1.0), (wkT_h, wkT_b, 1.0 / 32.0),
                    (wvT_h, wvT_b, 1.0), (woT_h, woT_b, 1.0)]):
                for c in range(DC):
                    slot = (wi * DC + c) % 2
                    tl = S.dma("sync", lambda e, src=src, c=c, slot=slot: e.dma_start(
                        out=stg[:, slot * D:(slot + 1) * D],
                        in_=src[c * 128:(c + 1) * 128, :]),
                        waits=[stg_free[slot]])
                    stg_free[slot] = S.op(
                        "vector", lambda e, dst=dst, c=c, slot=slot, scale=scale:
                        e.tensor_scalar(out=dst[:, c * D:(c + 1) * D],
                                        in0=stg[:, slot * D:(slot + 1) * D],
                                        scalar1=scale, scalar2=None, op0=ALU.mult),
                        waits=[tl], inc="dv", drain=True)
            t_wcast = stg_free[1]

            for c in range(DC):
                slot = c % 2
                tl = S.dma("sync", lambda e, c=c, slot=slot: e.dma_start(
                    out=stg[:, slot * D:(slot + 1) * D],
                    in_=wf_h[c * 128:(c + 1) * 128, :]), waits=[stg_free[slot]])
                stg_free[slot] = S.op("vector", lambda e, c=c, slot=slot: e.tensor_reduce(
                    out=abss[:, c:c + 1], in_=stg[:, slot * D:(slot + 1) * D],
                    axis=AX, op=ALU.add, apply_absolute_value=True),
                    waits=[tl], inc="dv", drain=True)
            t_rs = S.op("vector", lambda e: e.tensor_reduce(
                out=abss[:, 8:9], in_=abss[:, 0:8], axis=AX, op=ALU.add),
                waits=[stg_free[0], stg_free[1]], inc="dv", drain=True)
            t_hb = S.dma("gpsimd", lambda e: e.dma_start(
                out=hbounce, in_=abss[:, 8:9], allow_slow_non_contiguous=True),
                waits=[t_rs], sem="st")
            t_rr = S.dma("sync", lambda e: e.dma_start(out=redrow[:], in_=hbounce),
                         waits=[t_hb])
            S.op("vector", lambda e: e.tensor_reduce(
                out=sm[0:1, 4:5], in_=redrow[:], axis=AX, op=ALU.add),
                waits=[t_rr], drain=True)
            S.op("vector", lambda e: e.tensor_scalar(
                out=sm[0:1, 1:2], in0=sm[0:1, 4:5], scalar1=1.0 / (D * D),
                scalar2=None, op0=ALU.mult), drain=True)
            t_thr = S.op("vector", lambda e: e.tensor_scalar(
                out=sm[0:1, 0:1], in0=sm[0:1, 1:2], scalar1=1e-5, scalar2=0.5,
                op0=ALU.add, op1=ALU.mult), inc="dv", drain=True)
            t_smd = S.dma("gpsimd", lambda e: e.dma_start(out=smd[0:1, 0:2],
                                                          in_=sm[0:1, 0:2]),
                          waits=[t_thr], sem="st")
            t_thrb = S.dma("sync", lambda e: e.dma_start(
                out=scal[:, 0:2], in_=smd[0:1, 0:2].partition_broadcast(128),
                allow_slow_non_contiguous=True), waits=[t_smd])
            t_nthr = S.op("vector", lambda e: e.tensor_scalar(
                out=scal[:, 6:7], in0=scal[:, 0:1], scalar1=-1.0, scalar2=None,
                op0=ALU.mult), waits=[t_thrb], inc="dv", drain=True)
            for c in range(DC):
                slot = c % 2
                tl = S.dma("sync", lambda e, c=c, slot=slot: e.dma_start(
                    out=stg[:, slot * D:(slot + 1) * D],
                    in_=wf_h[c * 128:(c + 1) * 128, :]), waits=[stg_free[slot]])
                S.op("vector", lambda e, slot=slot: e.tensor_scalar(
                    out=zs[:], in0=stg[:, slot * D:(slot + 1) * D],
                    scalar1=scal[:, 0:1], scalar2=None, op0=ALU.is_ge),
                    waits=[tl, t_nthr])
                S.op("vector", lambda e, slot=slot: e.tensor_scalar(
                    out=z2[:], in0=stg[:, slot * D:(slot + 1) * D],
                    scalar1=scal[:, 6:7], scalar2=None, op0=ALU.is_le),
                    drain=True)
                stg_free[slot] = S.op("vector", lambda e, c=c: e.tensor_tensor(
                    out=w_tern[:, c * D:(c + 1) * D], in0=zs[:], in1=z2[:],
                    op=ALU.subtract), inc="dv", drain=True)

            x_free = [None, None]
            xntm_free = [None, None]
            sq_free = [None, None]
            psAB_free = {0: None, 1: None}
            pstg_free = [None, None]
            xnfm_rd = [None, None]
            vfm_rd = None
            pr_ct = 0
            for s in range(NSA):
                sf = (s % 2) * DC * D
                xnfm_toks = []
                for tl_ in range(8):
                    gt = s * 8 + tl_
                    pb = gt % 2
                    base = pb * 16
                    tx = S.dma("sync", lambda e, gt=gt, pb=pb: e.dma_start(
                        out=x_in[:, pb * D:(pb + 1) * D],
                        in_=x_h[gt * 128:(gt + 1) * 128, :]), waits=[x_free[pb]])
                    S.op("vector", lambda e, pb=pb, base=base: e.tensor_reduce(
                        out=stats[:, base:base + 1], in_=x_in[:, pb * D:(pb + 1) * D],
                        axis=AX, op=ALU.add), waits=[tx], drain=True)
                    tsq = S.op("scalar", lambda e, pb=pb, base=base: e.activation(
                        out=sq_scr[:, pb * D:(pb + 1) * D],
                        in_=x_in[:, pb * D:(pb + 1) * D], func=AF.Square),
                        waits=[tx, sq_free[pb]], inc="ac", drain=True)
                    S.op("vector", lambda e, base=base: e.tensor_scalar(
                        out=stats[:, base + 2:base + 3], in0=stats[:, base:base + 1],
                        scalar1=1.0 / D, scalar2=None, op0=ALU.mult), drain=True)
                    S.op("vector", lambda e, base=base: e.tensor_tensor(
                        out=stats[:, base + 3:base + 4], in0=stats[:, base + 2:base + 3],
                        in1=stats[:, base + 2:base + 3], op=ALU.mult), drain=True)
                    sq_free[pb] = S.op("vector", lambda e, pb=pb, base=base: e.tensor_reduce(
                        out=stats[:, base + 1:base + 2],
                        in_=sq_scr[:, pb * D:(pb + 1) * D],
                        axis=AX, op=ALU.add), waits=[tsq], drain=True)
                    tvar = S.op("vector", lambda e, base=base: e.scalar_tensor_tensor(
                        out=stats[:, base + 4:base + 5], in0=stats[:, base + 1:base + 2],
                        scalar=1.0 / D, in1=stats[:, base + 3:base + 4],
                        op0=ALU.mult, op1=ALU.subtract), inc="dv", drain=True)
                    S.op("scalar", lambda e, base=base: e.activation(
                        out=stats[:, base + 5:base + 6], in_=stats[:, base + 4:base + 5],
                        func=AF.Ln, bias=epsc[:, 0:1], scale=1.0),
                        waits=[tvar, t_eps], drain=True)
                    trstd = S.op("scalar", lambda e, base=base: e.activation(
                        out=stats[:, base + 6:base + 7], in_=stats[:, base + 5:base + 6],
                        func=AF.Exp, bias=0.0, scale=-0.5), inc="ac", drain=True)
                    tneg = S.op("vector", lambda e, base=base: e.scalar_tensor_tensor(
                        out=stats[:, base + 7:base + 8], in0=stats[:, base + 2:base + 3],
                        scalar=-1.0, in1=stats[:, base + 6:base + 7],
                        op0=ALU.mult, op1=ALU.mult), waits=[trstd], inc="dv", drain=True)
                    txn = S.op("scalar", lambda e, pb=pb, base=base: e.activation(
                        out=xn_tm[:, pb * D:(pb + 1) * D],
                        in_=x_in[:, pb * D:(pb + 1) * D], func=AF.Identity,
                        bias=stats[:, base + 7:base + 8],
                        scale=stats[:, base + 6:base + 7]),
                        waits=[tneg, xntm_free[pb]], inc="ac", drain=True)
                    x_free[pb] = txn
                    ev_last = None
                    for c in range(DC):
                        ev = S.dma("sync", lambda e, pb=pb, c=c, tl_=tl_, sf=sf:
                                   e.dma_start(
                            out=xn_fm[:, sf + c * D + tl_ * 128:
                                      sf + c * D + (tl_ + 1) * 128],
                            in_=xn_tm[:, pb * D + c * 128: pb * D + (c + 1) * 128],
                            transpose=True),
                            waits=[txn, xnfm_rd[s % 2]], sem="tr")
                        ev_last = ev
                    xntm_free[pb] = ev_last
                    xnfm_toks.append(ev_last)
                tok0 = s * 1024
                vfm_toks = []
                for oc in range(DC):
                    pg = pr_ct % 2
                    pr_ct += 1
                    psu = (ps_a, ps_b)[pg]
                    mm_last = None
                    for ic in range(DC):
                        for h in range(2):
                            def mmf(e, psu=psu, ic=ic, oc=oc, h=h, sf=sf,
                                    st=(ic == 0), sp=(ic == DC - 1)):
                                return e.matmul(
                                    out=psu[:, h * 512:(h + 1) * 512],
                                    lhsT=wvT_b[:, ic * D + oc * 128:
                                               ic * D + oc * 128 + 128],
                                    rhs=xn_fm[:, sf + ic * D + h * 512:
                                              sf + ic * D + h * 512 + 512],
                                    start=st, stop=sp)
                            w = ([psAB_free[pg], t_wcast] + xnfm_toks
                                 if (ic == 0 and h == 0) else [])
                            mm_last = S.op("tensor", mmf, waits=w,
                                           inc="pe" if (ic == DC - 1 and h == 1)
                                           else None,
                                           drain=(ic == DC - 1 and h == 1))
                    xnfm_rd[s % 2] = mm_last
                    tev = S.op("vector", lambda e, psu=psu, oc=oc: e.tensor_copy(
                        out=V_fm[:, oc * D:(oc + 1) * D], in_=psu[:]),
                        waits=[mm_last, vfm_rd], inc="dv", drain=True)
                    psAB_free[pg] = tev
                    vfm_toks.append(tev)
                for tt in range(8):
                    pg = pr_ct % 2
                    pr_ct += 1
                    psu = (ps_a, ps_b)[pg]
                    mm_last = None
                    for oc in range(DC):
                        for h in range(2):
                            def mmf(e, psu=psu, oc=oc, tt=tt, h=h,
                                    st=(oc == 0), sp=(oc == DC - 1)):
                                return e.matmul(
                                    out=psu[:, h * 512:(h + 1) * 512],
                                    lhsT=V_fm[:, oc * D + tt * 128:
                                              oc * D + tt * 128 + 128],
                                    rhs=woT_b[:, oc * D + h * 512:
                                              oc * D + h * 512 + 512],
                                    start=st, stop=sp)
                            w = ([psAB_free[pg]] + vfm_toks
                                 if (oc == 0 and h == 0) else [])
                            mm_last = S.op("tensor", mmf, waits=w,
                                           inc="pe" if (oc == DC - 1 and h == 1)
                                           else None,
                                           drain=(oc == DC - 1 and h == 1))
                    vfm_rd = mm_last
                    tev = S.op("vector", lambda e, psu=psu, pg=pg: e.tensor_copy(
                        out=pstg[:, pg * D:(pg + 1) * D], in_=psu[:]),
                        waits=[mm_last, pstg_free[pg]], inc="dv", drain=True)
                    psAB_free[pg] = tev
                    tst = S.dma("gpsimd",
                                lambda e, tt=tt, tok0=tok0, pg=pg: e.dma_start(
                                    out=Vs[tok0 + tt * 128:
                                           tok0 + (tt + 1) * 128, :],
                                    in_=pstg[:, pg * D:(pg + 1) * D]),
                                waits=[tev])
                    pstg_free[pg] = tst
                for (wb, dstT) in ((wkT_b, Ks), (wqT_b, Qs)):
                    for oc in range(DC):
                        pg = pr_ct % 2
                        pr_ct += 1
                        psu = (ps_a, ps_b)[pg]
                        mm_last = None
                        for ic in range(DC):
                            for h in range(2):
                                def mmf(e, wb=wb, psu=psu, ic=ic, oc=oc, h=h,
                                        sf=sf, st=(ic == 0), sp=(ic == DC - 1)):
                                    return e.matmul(
                                        out=psu[:, h * 512:(h + 1) * 512],
                                        lhsT=wb[:, ic * D + oc * 128:
                                                ic * D + oc * 128 + 128],
                                        rhs=xn_fm[:, sf + ic * D + h * 512:
                                                  sf + ic * D + h * 512 + 512],
                                        start=st, stop=sp)
                                w = ([psAB_free[pg], t_wcast] + xnfm_toks
                                     if (ic == 0 and h == 0) else [])
                                mm_last = S.op("tensor", mmf, waits=w,
                                               inc="pe" if (ic == DC - 1 and h == 1)
                                               else None,
                                               drain=(ic == DC - 1 and h == 1))
                        xnfm_rd[s % 2] = mm_last
                        tev = S.op("vector", lambda e, psu=psu, pg=pg: e.tensor_copy(
                            out=pstg[:, pg * D:(pg + 1) * D], in_=psu[:]),
                            waits=[mm_last, pstg_free[pg]], inc="dv", drain=True)
                        psAB_free[pg] = tev
                        tst = S.dma("gpsimd",
                                    lambda e, dstT=dstT, oc=oc, tok0=tok0, pg=pg:
                                    e.dma_start(
                                        out=dstT[oc * 128:(oc + 1) * 128,
                                                 tok0:tok0 + 1024],
                                        in_=pstg[:, pg * D:(pg + 1) * D]),
                                    waits=[tev])
                        pstg_free[pg] = tst

            S.op("tensor", lambda e: e.nop(), inc="bar", drain=True)
            S.op("vector", lambda e: e.nop(), inc="bar", drain=True)
            S.op("scalar", lambda e: e.nop(), inc="bar", drain=True)
            S.op("gpsimd", lambda e: e.nop(),
                 waits=S.queue_drained("gpsimd"), inc="bar")
            bar_a = counts["bar"]
            psAB_last = dict(psAB_free)

            blockA = PA.enter_context(nc.Block("phaseA"))
            S.emit(blockA)

        with ExitStack() as PB:
            qbuf = sb("qbuf", [128, 2 * DC * 256], BF16, PB)
            kst = sb("kst", [128, 2 * DC * 512], BF16, PB)
            vst = sb("vst", [128, 2 * 4 * D], BF16, PB)
            pt = sb("pt", [128, 4 * 256], BF16, PB)
            xres = sb("xres", [128, 2 * D], F32, PB)
            x1st = sb("x1st", [128, 2 * D], F32, PB)
            den = sb("den", [128, 32], F32, PB)
            bst = sb("bst", [128, 32], F32, PB)
            sqb = sb("sqb", [128, 2 * D], F32, PB)
            xhb = sb("xhb", [128, 2 * D], F32, PB)
            xnb = sb("xnb", [128, 2 * D], BF16, PB)
            xnf = sb("xnf", [128, 2 * D], BF16, PB)

            S = Sched(sems, counts, waited, qrot)
            barw = [("sync", "bar", bar_a, True)]

            qb_free = [None, None]
            kst_free = [None, None]
            vst_free = [None, None]
            pt_free = [None] * 4
            psS_free = [None, None]
            psC_free = {0: psAB_last[0], 1: psAB_last[1]}
            psD_free = [None, None]
            xres_free = [None, None]
            x1_free = [[], []]
            bst_free = [None] * 4
            sqb_free = [None, None]
            xhb_free = [None, None]
            xnb_free = [None, None]
            xnf_free = [None, None]
            sc = 0
            ptn = 0
            for jq in range(NSB):
                qs_ = jq % 2
                tq = S.dma("sync", lambda e, jq=jq, qs_=qs_: e.dma_start(
                    out=qbuf[:, qs_ * DC * 256:(qs_ + 1) * DC * 256]
                        .rearrange("p (c n) -> p c n", n=256),
                    in_=Qs.rearrange("(c p) m -> p c m", p=128)[
                        :, :, jq * 256:(jq + 1) * 256]),
                    waits=barw + [qb_free[qs_]])
                txr = [None, None]
                for qt in (0, 1):
                    g = 2 * jq + qt
                    txr[qt] = S.dma("sync", lambda e, g=g, qt=qt: e.dma_start(
                        out=xres[:, qt * D:(qt + 1) * D],
                        in_=x_h[g * 128:(g + 1) * 128, :]),
                        waits=barw + [xres_free[qt]])
                nkb = 2 * jq + 2
                nch = (nkb + 3) // 4
                pv_last = {0: None, 1: None}
                dn_last = {0: None, 1: None}
                pend = None
                last_sc = None

                def issue_pv(info):
                    rd = None
                    for (qt, lo) in info["cols"]:
                        kg = info["kg"]
                        psu = (ps_a, ps_b)[qt]
                        first = (kg == 0)
                        last = (kg == 2 * jq + qt)
                        mm2 = None
                        for h in range(2):
                            def mmf(e, psu=psu, lo=lo, ksv=info["ks"],
                                    kb=info["kb"], h=h, st=first, sp=last):
                                return e.matmul(
                                    out=psu[:, h * 512:(h + 1) * 512],
                                    lhsT=pt[:, lo:lo + 128],
                                    rhs=vst[:, ksv * 4096 + kb * 1024 + h * 512:
                                            ksv * 4096 + kb * 1024 + h * 512 + 512],
                                    start=st, stop=sp)
                            w_ = (([info["ttok"], info["tv"]]
                                   + ([psC_free[qt]] if first else []))
                                  if h == 0 else [])
                            mm2 = S.op("tensor", mmf, waits=w_,
                                       inc="pe" if h == 1 else None,
                                       drain=(h == 1 and last))
                        pv_last[qt] = mm2
                        vst_free[info["ks"]] = mm2

                        def mmd(e, lo=lo, qt=qt, st=first, sp=last):
                            return e.matmul(out=ps_dd[qt][:, 0:8],
                                            lhsT=pt[:, lo:lo + 128],
                                            rhs=ones8[:], start=st, stop=sp)
                        tdm = S.op("tensor", mmd,
                                   waits=([t_ones, psD_free[qt]]
                                          if first else []),
                                   inc="pe" if last else None, drain=last)
                        dn_last[qt] = tdm
                        rd = tdm
                    pt_free[info["p"]] = rd

                for ch in range(nch):
                    ks_ = ch % 2
                    tk = S.dma("sync", lambda e, ch=ch, ks_=ks_: e.dma_start(
                        out=kst[:, ks_ * DC * 512:(ks_ + 1) * DC * 512]
                            .rearrange("p (c n) -> p c n", n=512),
                        in_=Ks.rearrange("(c p) m -> p c m", p=128)[
                            :, :, ch * 512:(ch + 1) * 512]),
                        waits=barw + [kst_free[ks_]])
                    tv = S.dma("gpsimd", lambda e, ch=ch, ks_=ks_: e.dma_start(
                        out=vst[:, ks_ * 4096:(ks_ + 1) * 4096]
                            .rearrange("p (t n) -> p t n", n=1024),
                        in_=Vs.rearrange("(t p) n -> p t n", p=128)[
                            :, ch * 4:(ch + 1) * 4, :]),
                        waits=barw + [vst_free[ks_]])
                    for kb in range(4):
                        kg = ch * 4 + kb
                        if kg >= nkb:
                            break
                        narrow = (kg == nkb - 1)
                        c0, w = (128, 128) if narrow else (0, 256)
                        sb_ = sc % 2
                        sc += 1
                        mml = None
                        for dc in range(DC):
                            def mmf(e, dc=dc, sb_=sb_, ks_=ks_, kb=kb, qs_=qs_,
                                    c0=c0, w=w, st=(dc == 0), sp=(dc == DC - 1)):
                                return e.matmul(
                                    out=ps_s[sb_][:, 0:w],
                                    lhsT=kst[:, ks_ * DC * 512 + dc * 512 + kb * 128:
                                             ks_ * DC * 512 + dc * 512 + kb * 128 + 128],
                                    rhs=qbuf[:, qs_ * DC * 256 + dc * 256 + c0:
                                             qs_ * DC * 256 + dc * 256 + c0 + w],
                                    start=st, stop=sp)
                            w_ = [tk, tq, psS_free[sb_]] if dc == 0 else []
                            mml = S.op("tensor", mmf, waits=w_,
                                       inc="pe" if dc == DC - 1 else None,
                                       drain=(dc == DC - 1))
                        last_sc = mml
                        kst_free[ks_] = mml
                        p_ = ptn % 4
                        ptn += 1
                        texp = S.op("scalar", lambda e, p_=p_, sb_=sb_, w=w:
                                    e.activation(
                                        out=pt[:, p_ * 256:p_ * 256 + w],
                                        in_=ps_s[sb_][:, 0:w],
                                        func=AF.Exp, bias=0.0, scale=1.0),
                                    waits=[mml, pt_free[p_]], inc="ac", drain=True)
                        psS_free[sb_] = texp
                        ttok = texp
                        if kg >= nkb - 2:
                            mc0 = p_ * 256
                            ttok = S.op("vector", lambda e, mc0=mc0:
                                        e.tensor_tensor(
                                            out=pt[:, mc0:mc0 + 128],
                                            in0=pt[:, mc0:mc0 + 128],
                                            in1=maskT[:], op=ALU.mult),
                                        waits=[texp, t_mask], inc="dv", drain=True)
                        if narrow:
                            cols = [(1, p_ * 256)]
                        else:
                            cols = [(0, p_ * 256), (1, p_ * 256 + 128)]
                        cur = {"kg": kg, "kb": kb, "ks": ks_, "p": p_,
                               "ttok": ttok, "tv": tv, "cols": cols}
                        if pend is not None:
                            issue_pv(pend)
                        pend = cur
                if pend is not None:
                    issue_pv(pend)
                    pend = None
                qb_free[qs_] = last_sc

                S.op("vector", lambda e: e.tensor_copy(
                    out=den[:, 0:8], in_=ps_d0[:, 0:8]),
                    waits=[dn_last[0], dn_last[1]], drain=True)
                tde = S.op("vector", lambda e: e.tensor_copy(
                    out=den[:, 8:16], in_=ps_d1[:, 0:8]), inc="dv", drain=True)
                trc = S.op("vector", lambda e: e.reciprocal(
                    out=den[:, 16:32], in_=den[:, 0:16]), inc="dv", drain=True)
                psD_free[0] = tde
                psD_free[1] = tde
                for qt in (0, 1):
                    g = 2 * jq + qt
                    tep = S.op("vector", lambda e, qt=qt: e.scalar_tensor_tensor(
                        out=x1st[:, qt * D:(qt + 1) * D],
                        in0=(ps_a, ps_b)[qt][:],
                        scalar=den[:, 16 + qt * 8:17 + qt * 8],
                        in1=xres[:, qt * D:(qt + 1) * D],
                        op0=ALU.mult, op1=ALU.add),
                        waits=[pv_last[qt], trc, txr[qt]] + x1_free[qt],
                        inc="dv", drain=True)
                    psC_free[qt] = tep
                    xres_free[qt] = tep
                    tsx = S.dma("gpsimd", lambda e, g=g, qt=qt: e.dma_start(
                        out=X1s[g * 128:(g + 1) * 128, :],
                        in_=x1st[:, qt * D:(qt + 1) * D]), waits=[tep])
                    sg = g % 2
                    base = (g % 4) * 8
                    S.op("vector", lambda e, qt=qt, base=base: e.tensor_reduce(
                        out=bst[:, base:base + 1], in_=x1st[:, qt * D:(qt + 1) * D],
                        axis=AX, op=ALU.add), waits=[bst_free[g % 4]], drain=True)
                    tsq = S.op("scalar", lambda e, qt=qt, sg=sg: e.activation(
                        out=sqb[:, sg * D:(sg + 1) * D],
                        in_=x1st[:, qt * D:(qt + 1) * D], func=AF.Square),
                        waits=[tep, sqb_free[sg]], inc="ac", drain=True)
                    S.op("vector", lambda e, base=base: e.tensor_scalar(
                        out=bst[:, base + 2:base + 3], in0=bst[:, base:base + 1],
                        scalar1=1.0 / D, scalar2=None, op0=ALU.mult), drain=True)
                    S.op("vector", lambda e, base=base: e.tensor_tensor(
                        out=bst[:, base + 3:base + 4], in0=bst[:, base + 2:base + 3],
                        in1=bst[:, base + 2:base + 3], op=ALU.mult), drain=True)
                    S.op("vector", lambda e, sg=sg, base=base: e.tensor_reduce(
                        out=bst[:, base + 1:base + 2], in_=sqb[:, sg * D:(sg + 1) * D],
                        axis=AX, op=ALU.add), waits=[tsq], drain=True)
                    tvar = S.op("vector", lambda e, base=base: e.scalar_tensor_tensor(
                        out=bst[:, base + 4:base + 5], in0=bst[:, base + 1:base + 2],
                        scalar=1.0 / D, in1=bst[:, base + 3:base + 4],
                        op0=ALU.mult, op1=ALU.subtract), inc="dv", drain=True)
                    S.op("scalar", lambda e, base=base: e.activation(
                        out=bst[:, base + 5:base + 6], in_=bst[:, base + 4:base + 5],
                        func=AF.Ln, bias=epsc[:, 1:2], scale=1.001),
                        waits=[tvar], drain=True)
                    tse = S.op("scalar", lambda e, base=base: e.activation(
                        out=bst[:, base + 6:base + 7], in_=bst[:, base + 5:base + 6],
                        func=AF.Exp, bias=0.0, scale=-0.5), inc="ac", drain=True)
                    tng = S.op("vector", lambda e, base=base: e.scalar_tensor_tensor(
                        out=bst[:, base + 7:base + 8], in0=bst[:, base + 2:base + 3],
                        scalar=-1.0, in1=bst[:, base + 6:base + 7],
                        op0=ALU.mult, op1=ALU.mult), waits=[tse], inc="dv", drain=True)
                    txh = S.op("scalar", lambda e, qt=qt, sg=sg, base=base:
                               e.activation(
                        out=xhb[:, sg * D:(sg + 1) * D],
                        in_=x1st[:, qt * D:(qt + 1) * D], func=AF.Identity,
                        bias=bst[:, base + 7:base + 8],
                        scale=bst[:, base + 6:base + 7]),
                        waits=[tng, xhb_free[sg]], inc="ac", drain=True)
                    tg1 = S.op("vector", lambda e, sg=sg: e.tensor_tensor(
                        out=sqb[:, sg * D:(sg + 1) * D],
                        in0=xhb[:, sg * D:(sg + 1) * D], in1=gam_bc[:],
                        op=ALU.mult), waits=[txh], drain=True)
                    txn_ = S.op("vector", lambda e, sg=sg: e.tensor_tensor(
                        out=xnb[:, sg * D:(sg + 1) * D],
                        in0=sqb[:, sg * D:(sg + 1) * D], in1=bet_bc[:],
                        op=ALU.add), waits=[xnb_free[sg]], inc="dv", drain=True)
                    S.op("vector", lambda e, sg=sg, g=g: e.tensor_reduce(
                        out=absm[:, g:g + 1], in_=xnb[:, sg * D:(sg + 1) * D],
                        axis=AX, op=ALU.max, apply_absolute_value=True), drain=True)
                    trl = None
                    for c in range(DC):
                        trl = S.dma("scalar", lambda e, sg=sg, c=c: e.dma_start(
                            out=xnf[:, sg * D + c * 128: sg * D + (c + 1) * 128],
                            in_=xnb[:, sg * D + c * 128: sg * D + (c + 1) * 128],
                            transpose=True),
                            waits=[txn_, xnf_free[sg]] if c == 0 else [])
                    tsn = S.dma("gpsimd", lambda e, g=g, sg=sg: e.dma_start(
                        out=Xns.rearrange("(c p) t -> p c t", p=128)[
                            :, :, g * 128:(g + 1) * 128],
                        in_=xnf[:, sg * D:(sg + 1) * D]
                            .rearrange("p (c t) -> p c t", t=128)),
                        waits=[trl])
                    bst_free[g % 4] = txh
                    sqb_free[sg] = txn_
                    xhb_free[sg] = tg1
                    xnb_free[sg] = trl
                    xnf_free[sg] = tsn
                    x1_free[qt] = [tsx, txh]

            S.op("tensor", lambda e: e.nop(), inc="bar", drain=True)
            S.op("vector", lambda e: e.nop(), inc="bar", drain=True)
            S.op("scalar", lambda e: e.nop(), inc="bar", drain=True)
            S.op("gpsimd", lambda e: e.nop(),
                 waits=S.queue_drained("gpsimd"), inc="bar")
            bar_b = counts["bar"]

            blockB = PB.enter_context(nc.Block("phaseB"))
            S.emit(blockB)

        with ExitStack() as PC:
            x1_in = sb("x1_in", [128, 2 * D], F32, PC)
            xnld = sb("xnld", [128, 2 * D], BF16, PC)
            xq_bf = sb("xq_bf", [128, 2 * D], BF16, PC)
            y_out = sb("y_out", [128, 2 * D], F32, PC)

            S = Sched(sems, counts, waited, qrot)
            barw = [("sync", "bar", bar_b, True)]

            S.op("vector", lambda e: e.memset(cc_sb[:], 0.0), waits=barw,
                 inc="dv", drain=True)

            x1_free = [None, None]

            tam = S.op("vector", lambda e: e.tensor_reduce(
                out=absm[:, 64:65], in_=absm[:, 0:64], axis=AX, op=ALU.max),
                inc="dv", drain=True)
            t_hb2 = S.dma("gpsimd", lambda e: e.dma_start(
                out=hbounce, in_=absm[:, 64:65], allow_slow_non_contiguous=True),
                waits=[tam], sem="st")
            t_rr2 = S.dma("sync", lambda e: e.dma_start(out=redrow[:], in_=hbounce),
                          waits=[t_hb2])
            tred = S.op("vector", lambda e: e.tensor_reduce(
                out=cc_sb[0:1, 0:1], in_=redrow[:], axis=AX, op=ALU.max),
                waits=[t_rr2], inc="dv", drain=True)
            tcc1 = S.dma("gpsimd", lambda e: e.dma_start(out=cc_in[:, :], in_=cc_sb[:]),
                         waits=[tred], sem="st")
            tcol = S.op("gpsimd", lambda e: e.collective_compute(
                "AllReduce", ALU.max, replica_groups=[list(range(8))],
                ins=[cc_in.ap().opt()], outs=[cc_out.ap().opt()]),
                waits=[tcc1], inc="cc", asynchronous=True)
            t_gbc = S.dma("sync", lambda e: e.dma_start(
                out=scal[:, 2:3], in_=cc_out.ap()[0:1, 0:1].partition_broadcast(128),
                allow_slow_non_contiguous=True), waits=[tcol])
            S.op("vector", lambda e: e.reciprocal(out=scal[:, 5:6], in_=scal[:, 2:3]),
                 waits=[t_gbc], drain=True)
            ts1 = S.op("vector", lambda e: e.tensor_scalar(
                out=scal[:, 3:4], in0=scal[:, 5:6], scalar1=128.0, scalar2=None,
                op0=ALU.mult), inc="dv", drain=True)
            ts2 = S.op("vector", lambda e: e.scalar_tensor_tensor(
                out=scal[:, 4:5], in0=scal[:, 2:3], scalar=1.0 / 128.0,
                in1=scal[:, 1:2], op0=ALU.mult, op1=ALU.mult), inc="dv", drain=True)

            xq_free = [None, None]
            yout_free = [None, None]
            psC_free = {0: None, 1: None}
            xnld_free = [None, None]
            for t in range(NT):
                pb = t % 2
                tx = S.dma("sync", lambda e, t=t, pb=pb: e.dma_start(
                    out=x1_in[:, pb * D:(pb + 1) * D],
                    in_=X1s[t * 128:(t + 1) * 128, :]),
                    waits=barw + [x1_free[pb]])
                txnl = S.dma("sync", lambda e, t=t, pb=pb: e.dma_start(
                    out=xnld[:, pb * D:(pb + 1) * D]
                        .rearrange("p (c m) -> p c m", m=128),
                    in_=Xns.rearrange("(c p) m -> p c m", p=128)[
                        :, :, t * 128:(t + 1) * 128]),
                    waits=barw + [xnld_free[pb]])
                tq1 = S.op("vector", lambda e, pb=pb: e.tensor_scalar(
                    out=xq_bf[:, pb * D:(pb + 1) * D],
                    in0=xnld[:, pb * D:(pb + 1) * D],
                    scalar1=scal[:, 3:4], scalar2=CLIP, op0=ALU.mult, op1=ALU.min),
                    waits=[ts1, txnl, xq_free[pb]])
                tq2 = S.op("vector", lambda e, pb=pb: e.tensor_scalar(
                    out=xq_bf[:, pb * D:(pb + 1) * D],
                    in0=xq_bf[:, pb * D:(pb + 1) * D],
                    scalar1=-CLIP, scalar2=None,
                    op0=ALU.max), inc="dv", drain=True)
                pg = t % 2
                psu = (ps_a, ps_b)[pg]
                mml = None
                for ic in range(DC):
                    for h in range(2):
                        def mmf(e, psu=psu, pb=pb, ic=ic, h=h,
                                st=(ic == 0), sp=(ic == DC - 1)):
                            return e.matmul(
                                out=psu[:, h * 512:(h + 1) * 512],
                                lhsT=xq_bf[:, pb * D + ic * 128:
                                           pb * D + (ic + 1) * 128],
                                rhs=w_tern[:, ic * D + h * 512: ic * D + h * 512 + 512],
                                start=st, stop=sp)
                        w = [tq2, psC_free[pg]] if ic == 0 and h == 0 else []
                        mml = S.op("tensor", mmf, waits=w,
                                   inc="pe" if (ic == DC - 1 and h == 1) else None,
                                   drain=(ic == DC - 1 and h == 1))
                xq_free[pb] = mml
                xnld_free[pb] = tq2
                tep = S.op("vector", lambda e, psu=psu, pb=pb: e.scalar_tensor_tensor(
                    out=y_out[:, pb * D:(pb + 1) * D], in0=psu[:],
                    scalar=scal[:, 4:5], in1=x1_in[:, pb * D:(pb + 1) * D],
                    op0=ALU.mult, op1=ALU.add),
                    waits=[mml, tx, ts2, yout_free[pb]], inc="dv", drain=True)
                psC_free[pg] = tep
                x1_free[pb] = tep
                yout_free[pb] = S.dma("gpsimd", lambda e, t=t, pb=pb: e.dma_start(
                    out=out_h[t * 128:(t + 1) * 128, :],
                    in_=y_out[:, pb * D:(pb + 1) * D]), waits=[tep], sem="st")

            S.op("gpsimd", lambda e: e.nop(), waits=S.queue_drained("gpsimd"))

            blockC = PC.enter_context(nc.Block("phaseC"))
            S.emit(blockC)

    return nc


_CONSTS = None


def _consts():
    global _CONSTS
    if _CONSTS is None:
        mt = np.triu(np.ones((128, 128), dtype=np.float32), 0)
        _CONSTS = (mt.astype(ml_dtypes.bfloat16),
                   np.ones((128, 8), dtype=ml_dtypes.bfloat16))
    return _CONSTS


def kernel(L0_attn_in_features, L0_attn_w_q, L0_attn_w_k, L0_attn_w_v,
           L0_attn_w_o, L0_ffn_weight, L0_ffn_ln_gamma, L0_ffn_ln_beta):
    x = np.ascontiguousarray(np.asarray(L0_attn_in_features, dtype=np.float32))
    wqT = np.ascontiguousarray(np.asarray(L0_attn_w_q, dtype=np.float32).T)
    wkT = np.ascontiguousarray(np.asarray(L0_attn_w_k, dtype=np.float32).T)
    wvT = np.ascontiguousarray(np.asarray(L0_attn_w_v, dtype=np.float32).T)
    woT = np.ascontiguousarray(np.asarray(L0_attn_w_o, dtype=np.float32).T)
    wf = np.ascontiguousarray(np.asarray(L0_ffn_weight, dtype=np.float32))
    gam = np.ascontiguousarray(np.asarray(L0_ffn_ln_gamma, dtype=np.float32).reshape(1, D))
    bet = np.ascontiguousarray(np.asarray(L0_ffn_ln_beta, dtype=np.float32).reshape(1, D))
    maskT, ones8 = _consts()

    nc = build_program()
    in_maps = []
    for c in range(8):
        in_maps.append({
            "x": np.ascontiguousarray(x[c]),
            "wqT": wqT, "wkT": wkT, "wvT": wvT, "woT": woT, "wf": wf,
            "gam": gam, "bet": bet, "maskT": maskT, "ones8": ones8,
        })
    res = run_bass_kernel_spmd(nc, in_maps, core_ids=list(range(8)))
    return np.stack([res.results[c]["out"] for c in range(8)], axis=0)


# version 15
# speedup vs baseline: 1.7027x; 1.1202x over previous
import numpy as np
import ml_dtypes
from contextlib import ExitStack

import concourse.bass as bass
import concourse.mybir as mybir
from concourse.bass_utils import run_bass_kernel_spmd

F32 = mybir.dt.float32
BF16 = mybir.dt.bfloat16
AF = mybir.ActivationFunctionType
ALU = mybir.AluOpType
AX = mybir.AxisListType.X

T = 8192
D = 1024
NT = T // 128
DC = D // 128
NSA = 8
NSB = 32
NEG = -1e30
CLIP = 128.0 - 1e-5

ENGINES = ("sync", "tensor", "vector", "scalar", "gpsimd")

QPOOL = {"sync": 6, "gpsimd": 5, "scalar": 4}
QSEMS = {e: [f"q_{e}{i}" for i in range(n)] for e, n in QPOOL.items()}


class Sched:
    def __init__(self, sems, counts, waited, qrot):
        self.sems = sems
        self.cnt = counts
        self.waited = waited
        self.qrot = qrot
        self.ops = {e: [] for e in ENGINES}

    def op(self, eng, fn, waits=(), inc=None, inc_by=1, asynchronous=False,
           drain=False):
        self.ops[eng].append((fn, [w for w in waits if w is not None],
                              (inc, inc_by) if inc else None, drain))
        if inc:
            self.cnt[inc] += inc_by
            return (eng, inc, self.cnt[inc], asynchronous)
        return None

    def dma(self, eng, fn, waits=(), sem=None):
        pool = QSEMS[eng]
        s = pool[self.qrot[eng] % len(pool)]
        self.qrot[eng] += 1
        ws = list(waits)
        if self.cnt[s]:
            ws.append((eng, s, self.cnt[s], True))
        return self.op(eng, fn, ws, inc=s, inc_by=16, asynchronous=True)

    def queue_drained(self, eng):
        return [(eng, s, self.cnt[s], True) for s in QSEMS[eng] if self.cnt[s]]

    def emit(self, block):
        for name in ENGINES:
            ops = self.ops[name]
            if not ops:
                continue
            waited = self.waited.setdefault(name, {})
            sems = self.sems

            def body(eng, ops=ops, waited=waited, name=name):
                for fn, waits, inc, drain in ops:
                    for (peng, sname, val, is_async) in waits:
                        if peng == name and not is_async and name != "tensor":
                            continue
                        if waited.get(sname, -1) >= val:
                            continue
                        eng.wait_ge(sems[sname], val)
                        waited[sname] = val
                    inst = fn(eng)
                    if drain:
                        inst = eng.drain()
                    if inc is not None:
                        inst.then_inc(sems[inc[0]], inc[1])

            getattr(block, name)(body)


def build_program():
    nc = bass.Bass("TRN2", target_bir_lowering=False)

    x_h = nc.dram_tensor("x", [T, D], F32, kind="ExternalInput").ap()
    wqT_h = nc.dram_tensor("wqT", [D, D], F32, kind="ExternalInput").ap()
    wkT_h = nc.dram_tensor("wkT", [D, D], F32, kind="ExternalInput").ap()
    wvT_h = nc.dram_tensor("wvT", [D, D], F32, kind="ExternalInput").ap()
    woT_h = nc.dram_tensor("woT", [D, D], F32, kind="ExternalInput").ap()
    wf_h = nc.dram_tensor("wf", [D, D], F32, kind="ExternalInput").ap()
    gam_h = nc.dram_tensor("gam", [1, D], F32, kind="ExternalInput").ap()
    bet_h = nc.dram_tensor("bet", [1, D], F32, kind="ExternalInput").ap()
    maskT_h = nc.dram_tensor("maskT", [128, 128], BF16, kind="ExternalInput").ap()
    ones8_h = nc.dram_tensor("ones8", [128, 8], BF16, kind="ExternalInput").ap()
    out_h = nc.dram_tensor("out", [T, D], F32, kind="ExternalOutput").ap()

    Qs = nc.dram_tensor("Qs", [D, T], BF16, kind="Internal").ap()
    Ks = nc.dram_tensor("Ks", [D, T], BF16, kind="Internal").ap()
    Vs = nc.dram_tensor("Vs", [T, D], BF16, kind="Internal").ap()
    X1s = nc.dram_tensor("X1s", [T, D], F32, kind="Internal").ap()
    Xns = nc.dram_tensor("Xns", [D, T], BF16, kind="Internal").ap()
    hbounce = nc.dram_tensor("hbounce", [1, 128], F32, kind="Internal").ap()
    smd = nc.dram_tensor("smd", [1, 8], F32, kind="Internal").ap()
    cc_in = nc.dram_tensor("cc_in", [1, 512], F32, kind="Internal")
    cc_out = nc.dram_tensor("cc_out", [1, 512], F32, kind="Internal")

    with ExitStack() as P:
        def sb(name, shape, dtype, stack):
            return stack.enter_context(nc.sbuf_tensor(name, shape, dtype))

        maskT = sb("maskT_s", [128, 128], BF16, P)
        ones8 = sb("ones8_s", [128, 8], BF16, P)
        gam_bc = sb("gam_bc", [128, D], F32, P)
        bet_bc = sb("bet_bc", [128, D], F32, P)
        w_tern = sb("w_tern", [128, DC * D], BF16, P)
        scal = sb("scal", [128, 8], F32, P)
        sm = sb("sm", [1, 8], F32, P)
        cc_sb = sb("cc_sb", [1, 512], F32, P)
        redrow = sb("redrow", [1, 128], F32, P)
        epsc = sb("epsc", [128, 2], F32, P)
        absm = sb("absm", [128, 72], F32, P)

        ps_a = P.enter_context(nc.psum_tensor("ps_a", [128, 1024], F32))
        ps_b = P.enter_context(nc.psum_tensor("ps_b", [128, 1024], F32))
        ps_s0 = P.enter_context(nc.psum_tensor("ps_s0", [128, 256], F32))
        ps_s1 = P.enter_context(nc.psum_tensor("ps_s1", [128, 256], F32))
        ps_s = (ps_s0, ps_s1)
        ps_d0 = P.enter_context(nc.psum_tensor("ps_d0", [128, 8], F32))
        ps_d1 = P.enter_context(nc.psum_tensor("ps_d1", [128, 8], F32))
        ps_dd = (ps_d0, ps_d1)

        semnames = (["pe", "dv", "ac", "po", "cc", "bar"]
                    + [s for pool in QSEMS.values() for s in pool])
        sems = {n: P.enter_context(nc.semaphore(n)) for n in semnames}
        counts = {n: 0 for n in semnames}
        waited = {}
        qrot = {e: 0 for e in QPOOL}

        with ExitStack() as PA:
            wqT_b = sb("wqT_b", [128, DC * D], BF16, PA)
            wkT_b = sb("wkT_b", [128, DC * D], BF16, PA)
            wvT_b = sb("wvT_b", [128, DC * D], BF16, PA)
            woT_b = sb("woT_b", [128, DC * D], BF16, PA)
            V_fm = sb("V_fm", [128, DC * D], BF16, PA)
            stg = sb("stg", [128, 2 * D], F32, PA)
            x_in = sb("x_in", [128, 2 * D], F32, PA)
            sq_scr = sb("sq_scr", [128, 2 * D], F32, PA)
            stats = sb("stats", [128, 32], F32, PA)
            xn_tm = sb("xn_tm", [128, 2 * D], BF16, PA)
            xn_fm = sb("xn_fm", [128, 2 * DC * D], BF16, PA)
            pstg = sb("pstg", [128, 4 * D], BF16, PA)
            abss = sb("abss", [128, 9], F32, PA)
            zs = sb("zs", [128, D], F32, PA)
            z2 = sb("z2", [128, D], F32, PA)

            S = Sched(sems, counts, waited, qrot)

            t_mask = S.dma("sync", lambda e: e.dma_start(out=maskT[:], in_=maskT_h))
            t_ones = S.dma("sync", lambda e: e.dma_start(out=ones8[:], in_=ones8_h))
            S.dma("sync", lambda e: e.dma_start(
                out=gam_bc[:], in_=gam_h.partition_broadcast(128)))
            S.dma("sync", lambda e: e.dma_start(
                out=bet_bc[:], in_=bet_h.partition_broadcast(128)))
            S.op("vector", lambda e: e.memset(epsc[:, 0:1], 1e-5))
            t_eps = S.op("vector", lambda e: e.memset(epsc[:, 1:2], 1e-8),
                         inc="dv", drain=True)

            stg_free = [None, None]
            for wi, (src, dst, scale) in enumerate([
                    (wqT_h, wqT_b, 1.0), (wkT_h, wkT_b, 1.0 / 32.0),
                    (wvT_h, wvT_b, 1.0), (woT_h, woT_b, 1.0)]):
                for c in range(DC):
                    slot = (wi * DC + c) % 2
                    tl = S.dma("sync", lambda e, src=src, c=c, slot=slot: e.dma_start(
                        out=stg[:, slot * D:(slot + 1) * D],
                        in_=src[c * 128:(c + 1) * 128, :]),
                        waits=[stg_free[slot]])
                    stg_free[slot] = S.op(
                        "vector", lambda e, dst=dst, c=c, slot=slot, scale=scale:
                        e.tensor_scalar(out=dst[:, c * D:(c + 1) * D],
                                        in0=stg[:, slot * D:(slot + 1) * D],
                                        scalar1=scale, scalar2=None, op0=ALU.mult),
                        waits=[tl], inc="dv", drain=True)
            t_wcast = stg_free[1]

            for c in range(DC):
                slot = c % 2
                tl = S.dma("sync", lambda e, c=c, slot=slot: e.dma_start(
                    out=stg[:, slot * D:(slot + 1) * D],
                    in_=wf_h[c * 128:(c + 1) * 128, :]), waits=[stg_free[slot]])
                stg_free[slot] = S.op("vector", lambda e, c=c, slot=slot: e.tensor_reduce(
                    out=abss[:, c:c + 1], in_=stg[:, slot * D:(slot + 1) * D],
                    axis=AX, op=ALU.add, apply_absolute_value=True),
                    waits=[tl], inc="dv", drain=True)
            t_rs = S.op("vector", lambda e: e.tensor_reduce(
                out=abss[:, 8:9], in_=abss[:, 0:8], axis=AX, op=ALU.add),
                waits=[stg_free[0], stg_free[1]], inc="dv", drain=True)
            t_hb = S.dma("gpsimd", lambda e: e.dma_start(
                out=hbounce, in_=abss[:, 8:9], allow_slow_non_contiguous=True),
                waits=[t_rs], sem="st")
            t_rr = S.dma("sync", lambda e: e.dma_start(out=redrow[:], in_=hbounce),
                         waits=[t_hb])
            S.op("vector", lambda e: e.tensor_reduce(
                out=sm[0:1, 4:5], in_=redrow[:], axis=AX, op=ALU.add),
                waits=[t_rr], drain=True)
            S.op("vector", lambda e: e.tensor_scalar(
                out=sm[0:1, 1:2], in0=sm[0:1, 4:5], scalar1=1.0 / (D * D),
                scalar2=None, op0=ALU.mult), drain=True)
            t_thr = S.op("vector", lambda e: e.tensor_scalar(
                out=sm[0:1, 0:1], in0=sm[0:1, 1:2], scalar1=1e-5, scalar2=0.5,
                op0=ALU.add, op1=ALU.mult), inc="dv", drain=True)
            t_smd = S.dma("gpsimd", lambda e: e.dma_start(out=smd[0:1, 0:2],
                                                          in_=sm[0:1, 0:2]),
                          waits=[t_thr], sem="st")
            t_thrb = S.dma("sync", lambda e: e.dma_start(
                out=scal[:, 0:2], in_=smd[0:1, 0:2].partition_broadcast(128),
                allow_slow_non_contiguous=True), waits=[t_smd])
            t_nthr = S.op("vector", lambda e: e.tensor_scalar(
                out=scal[:, 6:7], in0=scal[:, 0:1], scalar1=-1.0, scalar2=None,
                op0=ALU.mult), waits=[t_thrb], inc="dv", drain=True)
            for c in range(DC):
                slot = c % 2
                tl = S.dma("sync", lambda e, c=c, slot=slot: e.dma_start(
                    out=stg[:, slot * D:(slot + 1) * D],
                    in_=wf_h[c * 128:(c + 1) * 128, :]), waits=[stg_free[slot]])
                S.op("vector", lambda e, slot=slot: e.tensor_scalar(
                    out=zs[:], in0=stg[:, slot * D:(slot + 1) * D],
                    scalar1=scal[:, 0:1], scalar2=None, op0=ALU.is_ge),
                    waits=[tl, t_nthr])
                S.op("vector", lambda e, slot=slot: e.tensor_scalar(
                    out=z2[:], in0=stg[:, slot * D:(slot + 1) * D],
                    scalar1=scal[:, 6:7], scalar2=None, op0=ALU.is_le),
                    drain=True)
                stg_free[slot] = S.op("vector", lambda e, c=c: e.tensor_tensor(
                    out=w_tern[:, c * D:(c + 1) * D], in0=zs[:], in1=z2[:],
                    op=ALU.subtract), inc="dv", drain=True)

            x_free = [None, None]
            xntm_free = [None, None]
            sq_free = [None, None]
            psAB_free = {0: None, 1: None}
            pstg_free = [None] * 4
            xnfm_rd = [None, None]
            vfm_rd = None
            pr_ct = 0
            xnfm_toks_by_s = {}

            def emit_ln(s, tl_):
                sf = (s % 2) * DC * D
                gt = s * 8 + tl_
                pb = gt % 2
                base = pb * 16
                tx = S.dma("sync", lambda e, gt=gt, pb=pb: e.dma_start(
                    out=x_in[:, pb * D:(pb + 1) * D],
                    in_=x_h[gt * 128:(gt + 1) * 128, :]), waits=[x_free[pb]])
                tsm = S.op("vector", lambda e, pb=pb, base=base: e.tensor_reduce(
                    out=stats[:, base:base + 1], in_=x_in[:, pb * D:(pb + 1) * D],
                    axis=AX, op=ALU.add), waits=[tx], inc="dv", drain=True)
                tsq = S.op("scalar", lambda e, pb=pb, base=base: e.activation(
                    out=sq_scr[:, pb * D:(pb + 1) * D],
                    in_=x_in[:, pb * D:(pb + 1) * D], func=AF.Square),
                    waits=[tx, sq_free[pb]], inc="ac", drain=True)
                S.op("vector", lambda e, base=base: e.tensor_scalar(
                    out=stats[:, base + 2:base + 3], in0=stats[:, base:base + 1],
                    scalar1=1.0 / D, scalar2=None, op0=ALU.mult),
                    waits=[tsm], drain=True)
                S.op("vector", lambda e, base=base: e.tensor_tensor(
                    out=stats[:, base + 3:base + 4], in0=stats[:, base + 2:base + 3],
                    in1=stats[:, base + 2:base + 3], op=ALU.mult), drain=True)
                tss = S.op("vector", lambda e, pb=pb, base=base: e.tensor_reduce(
                    out=stats[:, base + 1:base + 2],
                    in_=sq_scr[:, pb * D:(pb + 1) * D],
                    axis=AX, op=ALU.add), waits=[tsq], inc="dv", drain=True)
                sq_free[pb] = tss
                tvar = S.op("vector", lambda e, base=base: e.scalar_tensor_tensor(
                    out=stats[:, base + 4:base + 5], in0=stats[:, base + 1:base + 2],
                    scalar=1.0 / D, in1=stats[:, base + 3:base + 4],
                    op0=ALU.mult, op1=ALU.subtract), waits=[tss],
                    inc="dv", drain=True)
                S.op("scalar", lambda e, base=base: e.activation(
                    out=stats[:, base + 5:base + 6], in_=stats[:, base + 4:base + 5],
                    func=AF.Ln, bias=epsc[:, 0:1], scale=1.0),
                    waits=[tvar, t_eps], drain=True)
                trstd = S.op("scalar", lambda e, base=base: e.activation(
                    out=stats[:, base + 6:base + 7], in_=stats[:, base + 5:base + 6],
                    func=AF.Exp, bias=0.0, scale=-0.5), inc="ac", drain=True)
                tneg = S.op("vector", lambda e, base=base: e.scalar_tensor_tensor(
                    out=stats[:, base + 7:base + 8], in0=stats[:, base + 2:base + 3],
                    scalar=-1.0, in1=stats[:, base + 6:base + 7],
                    op0=ALU.mult, op1=ALU.mult), waits=[trstd], inc="dv", drain=True)
                txn = S.op("scalar", lambda e, pb=pb, base=base: e.activation(
                    out=xn_tm[:, pb * D:(pb + 1) * D],
                    in_=x_in[:, pb * D:(pb + 1) * D], func=AF.Identity,
                    bias=stats[:, base + 7:base + 8],
                    scale=stats[:, base + 6:base + 7]),
                    waits=[tneg, xntm_free[pb]], inc="ac", drain=True)
                x_free[pb] = txn
                ev_last = None
                for c in range(DC):
                    ev = S.dma("sync", lambda e, pb=pb, c=c, tl_=tl_, sf=sf:
                               e.dma_start(
                        out=xn_fm[:, sf + c * D + tl_ * 128:
                                  sf + c * D + (tl_ + 1) * 128],
                        in_=xn_tm[:, pb * D + c * 128: pb * D + (c + 1) * 128],
                        transpose=True),
                        waits=[txn, xnfm_rd[s % 2]], sem="tr")
                    ev_last = ev
                xntm_free[pb] = ev_last
                xnfm_toks_by_s[s].append(ev_last)

            xnfm_toks_by_s[0] = []
            for tl_ in range(8):
                emit_ln(0, tl_)

            for s in range(NSA):
                sf = (s % 2) * DC * D
                xnfm_toks = xnfm_toks_by_s[s]
                if s + 1 < NSA:
                    xnfm_toks_by_s[s + 1] = []
                ln_state = {"next": 0, "ctr": 0}

                def hook(s=s, ln_state=ln_state):
                    ln_state["ctr"] += 1
                    if (s + 1 < NSA and ln_state["next"] < 8
                            and ln_state["ctr"] % 4 == 0):
                        emit_ln(s + 1, ln_state["next"])
                        ln_state["next"] += 1

                tok0 = s * 1024
                vfm_toks = []
                for oc in range(DC):
                    hook()
                    pg = pr_ct % 2
                    pr_ct += 1
                    psu = (ps_a, ps_b)[pg]
                    mm_last = None
                    for ic in range(DC):
                        for h in range(2):
                            def mmf(e, psu=psu, ic=ic, oc=oc, h=h, sf=sf,
                                    st=(ic == 0), sp=(ic == DC - 1)):
                                return e.matmul(
                                    out=psu[:, h * 512:(h + 1) * 512],
                                    lhsT=wvT_b[:, ic * D + oc * 128:
                                               ic * D + oc * 128 + 128],
                                    rhs=xn_fm[:, sf + ic * D + h * 512:
                                              sf + ic * D + h * 512 + 512],
                                    start=st, stop=sp)
                            w = ([psAB_free[pg], t_wcast] + xnfm_toks
                                 if (ic == 0 and h == 0) else [])
                            mm_last = S.op("tensor", mmf, waits=w,
                                           inc="pe" if (ic == DC - 1 and h == 1)
                                           else None,
                                           drain=(ic == DC - 1 and h == 1))
                    xnfm_rd[s % 2] = mm_last
                    tev = S.op("vector", lambda e, psu=psu, oc=oc: e.tensor_copy(
                        out=V_fm[:, oc * D:(oc + 1) * D], in_=psu[:]),
                        waits=[mm_last, vfm_rd], inc="dv", drain=True)
                    psAB_free[pg] = tev
                    vfm_toks.append(tev)
                for tt in range(8):
                    hook()
                    pg = pr_ct % 2
                    pst_i = pr_ct % 4
                    pr_ct += 1
                    psu = (ps_a, ps_b)[pg]
                    mm_last = None
                    for oc in range(DC):
                        for h in range(2):
                            def mmf(e, psu=psu, oc=oc, tt=tt, h=h,
                                    st=(oc == 0), sp=(oc == DC - 1)):
                                return e.matmul(
                                    out=psu[:, h * 512:(h + 1) * 512],
                                    lhsT=V_fm[:, oc * D + tt * 128:
                                              oc * D + tt * 128 + 128],
                                    rhs=woT_b[:, oc * D + h * 512:
                                              oc * D + h * 512 + 512],
                                    start=st, stop=sp)
                            w = ([psAB_free[pg]] + vfm_toks
                                 if (oc == 0 and h == 0) else [])
                            mm_last = S.op("tensor", mmf, waits=w,
                                           inc="pe" if (oc == DC - 1 and h == 1)
                                           else None,
                                           drain=(oc == DC - 1 and h == 1))
                    vfm_rd = mm_last
                    tev = S.op("vector", lambda e, psu=psu, pst_i=pst_i:
                               e.tensor_copy(
                        out=pstg[:, pst_i * D:(pst_i + 1) * D], in_=psu[:]),
                        waits=[mm_last, pstg_free[pst_i]], inc="dv", drain=True)
                    psAB_free[pg] = tev
                    tst = S.dma("gpsimd",
                                lambda e, tt=tt, tok0=tok0, pst_i=pst_i:
                                e.dma_start(
                                    out=Vs[tok0 + tt * 128:
                                           tok0 + (tt + 1) * 128, :],
                                    in_=pstg[:, pst_i * D:(pst_i + 1) * D]),
                                waits=[tev])
                    pstg_free[pst_i] = tst
                for (wb, dstT) in ((wkT_b, Ks), (wqT_b, Qs)):
                    for oc in range(DC):
                        hook()
                        pg = pr_ct % 2
                        pst_i = pr_ct % 4
                        pr_ct += 1
                        psu = (ps_a, ps_b)[pg]
                        mm_last = None
                        for ic in range(DC):
                            for h in range(2):
                                def mmf(e, wb=wb, psu=psu, ic=ic, oc=oc, h=h,
                                        sf=sf, st=(ic == 0), sp=(ic == DC - 1)):
                                    return e.matmul(
                                        out=psu[:, h * 512:(h + 1) * 512],
                                        lhsT=wb[:, ic * D + oc * 128:
                                                ic * D + oc * 128 + 128],
                                        rhs=xn_fm[:, sf + ic * D + h * 512:
                                                  sf + ic * D + h * 512 + 512],
                                        start=st, stop=sp)
                                w = ([psAB_free[pg], t_wcast] + xnfm_toks
                                     if (ic == 0 and h == 0) else [])
                                mm_last = S.op("tensor", mmf, waits=w,
                                               inc="pe" if (ic == DC - 1 and h == 1)
                                               else None,
                                               drain=(ic == DC - 1 and h == 1))
                        xnfm_rd[s % 2] = mm_last
                        tev = S.op("vector", lambda e, psu=psu, pst_i=pst_i:
                                   e.tensor_copy(
                            out=pstg[:, pst_i * D:(pst_i + 1) * D], in_=psu[:]),
                            waits=[mm_last, pstg_free[pst_i]], inc="dv", drain=True)
                        psAB_free[pg] = tev
                        tst = S.dma("gpsimd",
                                    lambda e, dstT=dstT, oc=oc, tok0=tok0,
                                    pst_i=pst_i: e.dma_start(
                                        out=dstT[oc * 128:(oc + 1) * 128,
                                                 tok0:tok0 + 1024],
                                        in_=pstg[:, pst_i * D:(pst_i + 1) * D]),
                                    waits=[tev])
                        pstg_free[pst_i] = tst

            S.op("tensor", lambda e: e.nop(), inc="bar", drain=True)
            S.op("vector", lambda e: e.nop(), inc="bar", drain=True)
            S.op("scalar", lambda e: e.nop(), inc="bar", drain=True)
            S.op("gpsimd", lambda e: e.nop(),
                 waits=S.queue_drained("gpsimd"), inc="bar")
            bar_a = counts["bar"]
            psAB_last = dict(psAB_free)

            blockA = PA.enter_context(nc.Block("phaseA"))
            S.emit(blockA)

        with ExitStack() as PB:
            qbuf = sb("qbuf", [128, 2 * DC * 256], BF16, PB)
            kst = sb("kst", [128, 2 * DC * 512], BF16, PB)
            vst = sb("vst", [128, 2 * 4 * D], BF16, PB)
            pt = sb("pt", [128, 4 * 256], BF16, PB)
            xres = sb("xres", [128, 2 * D], F32, PB)
            x1st = sb("x1st", [128, 2 * D], F32, PB)
            den = sb("den", [128, 32], F32, PB)
            bst = sb("bst", [128, 32], F32, PB)
            sqb = sb("sqb", [128, 2 * D], F32, PB)
            xhb = sb("xhb", [128, 2 * D], F32, PB)
            xnb = sb("xnb", [128, 2 * D], BF16, PB)
            xnf = sb("xnf", [128, 2 * D], BF16, PB)

            S = Sched(sems, counts, waited, qrot)
            barw = [("sync", "bar", bar_a, True)]

            qb_free = [None, None]
            kst_free = [None, None]
            vst_free = [None, None]
            pt_free = [None] * 4
            psS_free = [None, None]
            psC_free = {0: psAB_last[0], 1: psAB_last[1]}
            psD_free = [None, None]
            xres_free = [None, None]
            x1_free = [[], []]
            bst_free = [None] * 4
            sqb_free = [None, None]
            xhb_free = [None, None]
            xnb_free = [None, None]
            xnf_free = [None, None]
            sc = 0
            ptn = 0
            for jq in range(NSB):
                qs_ = jq % 2
                tq = S.dma("sync", lambda e, jq=jq, qs_=qs_: e.dma_start(
                    out=qbuf[:, qs_ * DC * 256:(qs_ + 1) * DC * 256]
                        .rearrange("p (c n) -> p c n", n=256),
                    in_=Qs.rearrange("(c p) m -> p c m", p=128)[
                        :, :, jq * 256:(jq + 1) * 256]),
                    waits=barw + [qb_free[qs_]])
                txr = [None, None]
                for qt in (0, 1):
                    g = 2 * jq + qt
                    txr[qt] = S.dma("sync", lambda e, g=g, qt=qt: e.dma_start(
                        out=xres[:, qt * D:(qt + 1) * D],
                        in_=x_h[g * 128:(g + 1) * 128, :]),
                        waits=barw + [xres_free[qt]])
                nkb = 2 * jq + 2
                nch = (nkb + 3) // 4
                pv_last = {0: None, 1: None}
                dn_last = {0: None, 1: None}
                pend = None
                last_sc = None

                def issue_pv(info):
                    rd = None
                    for (qt, lo) in info["cols"]:
                        kg = info["kg"]
                        psu = (ps_a, ps_b)[qt]
                        first = (kg == 0)
                        last = (kg == 2 * jq + qt)
                        mm2 = None
                        for h in range(2):
                            def mmf(e, psu=psu, lo=lo, ksv=info["ks"],
                                    kb=info["kb"], h=h, st=first, sp=last):
                                return e.matmul(
                                    out=psu[:, h * 512:(h + 1) * 512],
                                    lhsT=pt[:, lo:lo + 128],
                                    rhs=vst[:, ksv * 4096 + kb * 1024 + h * 512:
                                            ksv * 4096 + kb * 1024 + h * 512 + 512],
                                    start=st, stop=sp)
                            w_ = (([info["ttok"], info["tv"]]
                                   + ([psC_free[qt]] if first else []))
                                  if h == 0 else [])
                            mm2 = S.op("tensor", mmf, waits=w_,
                                       inc="pe" if h == 1 else None,
                                       drain=(h == 1 and last))
                        pv_last[qt] = mm2
                        vst_free[info["ks"]] = mm2

                        def mmd(e, lo=lo, qt=qt, st=first, sp=last):
                            return e.matmul(out=ps_dd[qt][:, 0:8],
                                            lhsT=pt[:, lo:lo + 128],
                                            rhs=ones8[:], start=st, stop=sp)
                        tdm = S.op("tensor", mmd,
                                   waits=([t_ones, psD_free[qt]]
                                          if first else []),
                                   inc="pe" if last else None, drain=last)
                        dn_last[qt] = tdm
                        rd = tdm
                    pt_free[info["p"]] = rd

                for ch in range(nch):
                    ks_ = ch % 2
                    tk = S.dma("sync", lambda e, ch=ch, ks_=ks_: e.dma_start(
                        out=kst[:, ks_ * DC * 512:(ks_ + 1) * DC * 512]
                            .rearrange("p (c n) -> p c n", n=512),
                        in_=Ks.rearrange("(c p) m -> p c m", p=128)[
                            :, :, ch * 512:(ch + 1) * 512]),
                        waits=barw + [kst_free[ks_]])
                    tv = S.dma("gpsimd", lambda e, ch=ch, ks_=ks_: e.dma_start(
                        out=vst[:, ks_ * 4096:(ks_ + 1) * 4096]
                            .rearrange("p (t n) -> p t n", n=1024),
                        in_=Vs.rearrange("(t p) n -> p t n", p=128)[
                            :, ch * 4:(ch + 1) * 4, :]),
                        waits=barw + [vst_free[ks_]])
                    for kb in range(4):
                        kg = ch * 4 + kb
                        if kg >= nkb:
                            break
                        narrow = (kg == nkb - 1)
                        c0, w = (128, 128) if narrow else (0, 256)
                        sb_ = sc % 2
                        sc += 1
                        mml = None
                        for dc in range(DC):
                            def mmf(e, dc=dc, sb_=sb_, ks_=ks_, kb=kb, qs_=qs_,
                                    c0=c0, w=w, st=(dc == 0), sp=(dc == DC - 1)):
                                return e.matmul(
                                    out=ps_s[sb_][:, 0:w],
                                    lhsT=kst[:, ks_ * DC * 512 + dc * 512 + kb * 128:
                                             ks_ * DC * 512 + dc * 512 + kb * 128 + 128],
                                    rhs=qbuf[:, qs_ * DC * 256 + dc * 256 + c0:
                                             qs_ * DC * 256 + dc * 256 + c0 + w],
                                    start=st, stop=sp)
                            w_ = [tk, tq, psS_free[sb_]] if dc == 0 else []
                            mml = S.op("tensor", mmf, waits=w_,
                                       inc="pe" if dc == DC - 1 else None,
                                       drain=(dc == DC - 1))
                        last_sc = mml
                        kst_free[ks_] = mml
                        p_ = ptn % 4
                        ptn += 1
                        texp = S.op("scalar", lambda e, p_=p_, sb_=sb_, w=w:
                                    e.activation(
                                        out=pt[:, p_ * 256:p_ * 256 + w],
                                        in_=ps_s[sb_][:, 0:w],
                                        func=AF.Exp, bias=0.0, scale=1.0),
                                    waits=[mml, pt_free[p_]], inc="ac", drain=True)
                        psS_free[sb_] = texp
                        ttok = texp
                        if kg >= nkb - 2:
                            mc0 = p_ * 256
                            ttok = S.op("vector", lambda e, mc0=mc0:
                                        e.tensor_tensor(
                                            out=pt[:, mc0:mc0 + 128],
                                            in0=pt[:, mc0:mc0 + 128],
                                            in1=maskT[:], op=ALU.mult),
                                        waits=[texp, t_mask], inc="dv", drain=True)
                        if narrow:
                            cols = [(1, p_ * 256)]
                        else:
                            cols = [(0, p_ * 256), (1, p_ * 256 + 128)]
                        cur = {"kg": kg, "kb": kb, "ks": ks_, "p": p_,
                               "ttok": ttok, "tv": tv, "cols": cols}
                        if pend is not None:
                            issue_pv(pend)
                        pend = cur
                if pend is not None:
                    issue_pv(pend)
                    pend = None
                qb_free[qs_] = last_sc

                S.op("vector", lambda e: e.tensor_copy(
                    out=den[:, 0:8], in_=ps_d0[:, 0:8]),
                    waits=[dn_last[0], dn_last[1]], drain=True)
                tde = S.op("vector", lambda e: e.tensor_copy(
                    out=den[:, 8:16], in_=ps_d1[:, 0:8]), inc="dv", drain=True)
                trc = S.op("vector", lambda e: e.reciprocal(
                    out=den[:, 16:32], in_=den[:, 0:16]), inc="dv", drain=True)
                psD_free[0] = tde
                psD_free[1] = tde
                for qt in (0, 1):
                    g = 2 * jq + qt
                    tep = S.op("vector", lambda e, qt=qt: e.scalar_tensor_tensor(
                        out=x1st[:, qt * D:(qt + 1) * D],
                        in0=(ps_a, ps_b)[qt][:],
                        scalar=den[:, 16 + qt * 8:17 + qt * 8],
                        in1=xres[:, qt * D:(qt + 1) * D],
                        op0=ALU.mult, op1=ALU.add),
                        waits=[pv_last[qt], trc, txr[qt]] + x1_free[qt],
                        inc="dv", drain=True)
                    psC_free[qt] = tep
                    xres_free[qt] = tep
                    tsx = S.dma("gpsimd", lambda e, g=g, qt=qt: e.dma_start(
                        out=X1s[g * 128:(g + 1) * 128, :],
                        in_=x1st[:, qt * D:(qt + 1) * D]), waits=[tep])
                    sg = g % 2
                    base = (g % 4) * 8
                    S.op("vector", lambda e, qt=qt, base=base: e.tensor_reduce(
                        out=bst[:, base:base + 1], in_=x1st[:, qt * D:(qt + 1) * D],
                        axis=AX, op=ALU.add), waits=[bst_free[g % 4]], drain=True)
                    tsq = S.op("scalar", lambda e, qt=qt, sg=sg: e.activation(
                        out=sqb[:, sg * D:(sg + 1) * D],
                        in_=x1st[:, qt * D:(qt + 1) * D], func=AF.Square),
                        waits=[tep, sqb_free[sg]], inc="ac", drain=True)
                    S.op("vector", lambda e, base=base: e.tensor_scalar(
                        out=bst[:, base + 2:base + 3], in0=bst[:, base:base + 1],
                        scalar1=1.0 / D, scalar2=None, op0=ALU.mult), drain=True)
                    S.op("vector", lambda e, base=base: e.tensor_tensor(
                        out=bst[:, base + 3:base + 4], in0=bst[:, base + 2:base + 3],
                        in1=bst[:, base + 2:base + 3], op=ALU.mult), drain=True)
                    S.op("vector", lambda e, sg=sg, base=base: e.tensor_reduce(
                        out=bst[:, base + 1:base + 2], in_=sqb[:, sg * D:(sg + 1) * D],
                        axis=AX, op=ALU.add), waits=[tsq], drain=True)
                    tvar = S.op("vector", lambda e, base=base: e.scalar_tensor_tensor(
                        out=bst[:, base + 4:base + 5], in0=bst[:, base + 1:base + 2],
                        scalar=1.0 / D, in1=bst[:, base + 3:base + 4],
                        op0=ALU.mult, op1=ALU.subtract), inc="dv", drain=True)
                    S.op("scalar", lambda e, base=base: e.activation(
                        out=bst[:, base + 5:base + 6], in_=bst[:, base + 4:base + 5],
                        func=AF.Ln, bias=epsc[:, 1:2], scale=1.001),
                        waits=[tvar], drain=True)
                    tse = S.op("scalar", lambda e, base=base: e.activation(
                        out=bst[:, base + 6:base + 7], in_=bst[:, base + 5:base + 6],
                        func=AF.Exp, bias=0.0, scale=-0.5), inc="ac", drain=True)
                    tng = S.op("vector", lambda e, base=base: e.scalar_tensor_tensor(
                        out=bst[:, base + 7:base + 8], in0=bst[:, base + 2:base + 3],
                        scalar=-1.0, in1=bst[:, base + 6:base + 7],
                        op0=ALU.mult, op1=ALU.mult), waits=[tse], inc="dv", drain=True)
                    txh = S.op("scalar", lambda e, qt=qt, sg=sg, base=base:
                               e.activation(
                        out=xhb[:, sg * D:(sg + 1) * D],
                        in_=x1st[:, qt * D:(qt + 1) * D], func=AF.Identity,
                        bias=bst[:, base + 7:base + 8],
                        scale=bst[:, base + 6:base + 7]),
                        waits=[tng, xhb_free[sg]], inc="ac", drain=True)
                    tg1 = S.op("vector", lambda e, sg=sg: e.tensor_tensor(
                        out=sqb[:, sg * D:(sg + 1) * D],
                        in0=xhb[:, sg * D:(sg + 1) * D], in1=gam_bc[:],
                        op=ALU.mult), waits=[txh], drain=True)
                    txn_ = S.op("vector", lambda e, sg=sg: e.tensor_tensor(
                        out=xnb[:, sg * D:(sg + 1) * D],
                        in0=sqb[:, sg * D:(sg + 1) * D], in1=bet_bc[:],
                        op=ALU.add), waits=[xnb_free[sg]], inc="dv", drain=True)
                    S.op("vector", lambda e, sg=sg, g=g: e.tensor_reduce(
                        out=absm[:, g:g + 1], in_=xnb[:, sg * D:(sg + 1) * D],
                        axis=AX, op=ALU.max, apply_absolute_value=True), drain=True)
                    trl = None
                    for c in range(DC):
                        trl = S.dma("scalar", lambda e, sg=sg, c=c: e.dma_start(
                            out=xnf[:, sg * D + c * 128: sg * D + (c + 1) * 128],
                            in_=xnb[:, sg * D + c * 128: sg * D + (c + 1) * 128],
                            transpose=True),
                            waits=[txn_, xnf_free[sg]] if c == 0 else [])
                    tsn = S.dma("gpsimd", lambda e, g=g, sg=sg: e.dma_start(
                        out=Xns.rearrange("(c p) t -> p c t", p=128)[
                            :, :, g * 128:(g + 1) * 128],
                        in_=xnf[:, sg * D:(sg + 1) * D]
                            .rearrange("p (c t) -> p c t", t=128)),
                        waits=[trl])
                    bst_free[g % 4] = txh
                    sqb_free[sg] = txn_
                    xhb_free[sg] = tg1
                    xnb_free[sg] = trl
                    xnf_free[sg] = tsn
                    x1_free[qt] = [tsx, txh]

            S.op("tensor", lambda e: e.nop(), inc="bar", drain=True)
            S.op("vector", lambda e: e.nop(), inc="bar", drain=True)
            S.op("scalar", lambda e: e.nop(), inc="bar", drain=True)
            S.op("gpsimd", lambda e: e.nop(),
                 waits=S.queue_drained("gpsimd"), inc="bar")
            bar_b = counts["bar"]

            blockB = PB.enter_context(nc.Block("phaseB"))
            S.emit(blockB)

        with ExitStack() as PC:
            x1_in = sb("x1_in", [128, 2 * D], F32, PC)
            xnld = sb("xnld", [128, 2 * D], BF16, PC)
            xq_bf = sb("xq_bf", [128, 2 * D], BF16, PC)
            y_out = sb("y_out", [128, 2 * D], F32, PC)

            S = Sched(sems, counts, waited, qrot)
            barw = [("sync", "bar", bar_b, True)]

            S.op("vector", lambda e: e.memset(cc_sb[:], 0.0), waits=barw,
                 inc="dv", drain=True)

            x1_free = [None, None]

            tam = S.op("vector", lambda e: e.tensor_reduce(
                out=absm[:, 64:65], in_=absm[:, 0:64], axis=AX, op=ALU.max),
                inc="dv", drain=True)
            t_hb2 = S.dma("gpsimd", lambda e: e.dma_start(
                out=hbounce, in_=absm[:, 64:65], allow_slow_non_contiguous=True),
                waits=[tam], sem="st")
            t_rr2 = S.dma("sync", lambda e: e.dma_start(out=redrow[:], in_=hbounce),
                          waits=[t_hb2])
            tred = S.op("vector", lambda e: e.tensor_reduce(
                out=cc_sb[0:1, 0:1], in_=redrow[:], axis=AX, op=ALU.max),
                waits=[t_rr2], inc="dv", drain=True)
            tcc1 = S.dma("gpsimd", lambda e: e.dma_start(out=cc_in[:, :], in_=cc_sb[:]),
                         waits=[tred], sem="st")
            tcol = S.op("gpsimd", lambda e: e.collective_compute(
                "AllReduce", ALU.max, replica_groups=[list(range(8))],
                ins=[cc_in.ap().opt()], outs=[cc_out.ap().opt()]),
                waits=[tcc1], inc="cc", asynchronous=True)
            t_gbc = S.dma("sync", lambda e: e.dma_start(
                out=scal[:, 2:3], in_=cc_out.ap()[0:1, 0:1].partition_broadcast(128),
                allow_slow_non_contiguous=True), waits=[tcol])
            S.op("vector", lambda e: e.reciprocal(out=scal[:, 5:6], in_=scal[:, 2:3]),
                 waits=[t_gbc], drain=True)
            ts1 = S.op("vector", lambda e: e.tensor_scalar(
                out=scal[:, 3:4], in0=scal[:, 5:6], scalar1=128.0, scalar2=None,
                op0=ALU.mult), inc="dv", drain=True)
            ts2 = S.op("vector", lambda e: e.scalar_tensor_tensor(
                out=scal[:, 4:5], in0=scal[:, 2:3], scalar=1.0 / 128.0,
                in1=scal[:, 1:2], op0=ALU.mult, op1=ALU.mult), inc="dv", drain=True)

            xq_free = [None, None]
            yout_free = [None, None]
            psC_free = {0: None, 1: None}
            xnld_free = [None, None]
            for t in range(NT):
                pb = t % 2
                tx = S.dma("sync", lambda e, t=t, pb=pb: e.dma_start(
                    out=x1_in[:, pb * D:(pb + 1) * D],
                    in_=X1s[t * 128:(t + 1) * 128, :]),
                    waits=barw + [x1_free[pb]])
                txnl = S.dma("sync", lambda e, t=t, pb=pb: e.dma_start(
                    out=xnld[:, pb * D:(pb + 1) * D]
                        .rearrange("p (c m) -> p c m", m=128),
                    in_=Xns.rearrange("(c p) m -> p c m", p=128)[
                        :, :, t * 128:(t + 1) * 128]),
                    waits=barw + [xnld_free[pb]])
                tq1 = S.op("vector", lambda e, pb=pb: e.tensor_scalar(
                    out=xq_bf[:, pb * D:(pb + 1) * D],
                    in0=xnld[:, pb * D:(pb + 1) * D],
                    scalar1=scal[:, 3:4], scalar2=CLIP, op0=ALU.mult, op1=ALU.min),
                    waits=[ts1, txnl, xq_free[pb]])
                tq2 = S.op("vector", lambda e, pb=pb: e.tensor_scalar(
                    out=xq_bf[:, pb * D:(pb + 1) * D],
                    in0=xq_bf[:, pb * D:(pb + 1) * D],
                    scalar1=-CLIP, scalar2=None,
                    op0=ALU.max), inc="dv", drain=True)
                pg = t % 2
                psu = (ps_a, ps_b)[pg]
                mml = None
                for ic in range(DC):
                    for h in range(2):
                        def mmf(e, psu=psu, pb=pb, ic=ic, h=h,
                                st=(ic == 0), sp=(ic == DC - 1)):
                            return e.matmul(
                                out=psu[:, h * 512:(h + 1) * 512],
                                lhsT=xq_bf[:, pb * D + ic * 128:
                                           pb * D + (ic + 1) * 128],
                                rhs=w_tern[:, ic * D + h * 512: ic * D + h * 512 + 512],
                                start=st, stop=sp)
                        w = [tq2, psC_free[pg]] if ic == 0 and h == 0 else []
                        mml = S.op("tensor", mmf, waits=w,
                                   inc="pe" if (ic == DC - 1 and h == 1) else None,
                                   drain=(ic == DC - 1 and h == 1))
                xq_free[pb] = mml
                xnld_free[pb] = tq2
                tep = S.op("vector", lambda e, psu=psu, pb=pb: e.scalar_tensor_tensor(
                    out=y_out[:, pb * D:(pb + 1) * D], in0=psu[:],
                    scalar=scal[:, 4:5], in1=x1_in[:, pb * D:(pb + 1) * D],
                    op0=ALU.mult, op1=ALU.add),
                    waits=[mml, tx, ts2, yout_free[pb]], inc="dv", drain=True)
                psC_free[pg] = tep
                x1_free[pb] = tep
                yout_free[pb] = S.dma("gpsimd", lambda e, t=t, pb=pb: e.dma_start(
                    out=out_h[t * 128:(t + 1) * 128, :],
                    in_=y_out[:, pb * D:(pb + 1) * D]), waits=[tep], sem="st")

            S.op("gpsimd", lambda e: e.nop(), waits=S.queue_drained("gpsimd"))

            blockC = PC.enter_context(nc.Block("phaseC"))
            S.emit(blockC)

    return nc


_CONSTS = None


def _consts():
    global _CONSTS
    if _CONSTS is None:
        mt = np.triu(np.ones((128, 128), dtype=np.float32), 0)
        _CONSTS = (mt.astype(ml_dtypes.bfloat16),
                   np.ones((128, 8), dtype=ml_dtypes.bfloat16))
    return _CONSTS


def kernel(L0_attn_in_features, L0_attn_w_q, L0_attn_w_k, L0_attn_w_v,
           L0_attn_w_o, L0_ffn_weight, L0_ffn_ln_gamma, L0_ffn_ln_beta):
    x = np.ascontiguousarray(np.asarray(L0_attn_in_features, dtype=np.float32))
    wqT = np.ascontiguousarray(np.asarray(L0_attn_w_q, dtype=np.float32).T)
    wkT = np.ascontiguousarray(np.asarray(L0_attn_w_k, dtype=np.float32).T)
    wvT = np.ascontiguousarray(np.asarray(L0_attn_w_v, dtype=np.float32).T)
    woT = np.ascontiguousarray(np.asarray(L0_attn_w_o, dtype=np.float32).T)
    wf = np.ascontiguousarray(np.asarray(L0_ffn_weight, dtype=np.float32))
    gam = np.ascontiguousarray(np.asarray(L0_ffn_ln_gamma, dtype=np.float32).reshape(1, D))
    bet = np.ascontiguousarray(np.asarray(L0_ffn_ln_beta, dtype=np.float32).reshape(1, D))
    maskT, ones8 = _consts()

    nc = build_program()
    in_maps = []
    for c in range(8):
        in_maps.append({
            "x": np.ascontiguousarray(x[c]),
            "wqT": wqT, "wkT": wkT, "wvT": wvT, "woT": woT, "wf": wf,
            "gam": gam, "bet": bet, "maskT": maskT, "ones8": ones8,
        })
    res = run_bass_kernel_spmd(nc, in_maps, core_ids=list(range(8)))
    return np.stack([res.results[c]["out"] for c in range(8)], axis=0)


# version 17
# speedup vs baseline: 1.7553x; 1.0309x over previous
import numpy as np
import ml_dtypes
from contextlib import ExitStack

import concourse.bass as bass
import concourse.mybir as mybir
from concourse.bass_utils import run_bass_kernel_spmd

F32 = mybir.dt.float32
BF16 = mybir.dt.bfloat16
AF = mybir.ActivationFunctionType
ALU = mybir.AluOpType
AX = mybir.AxisListType.X

T = 8192
D = 1024
NT = T // 128
DC = D // 128
NSA = 8
NSB = 32
NEG = -1e30
CLIP = 128.0 - 1e-5

ENGINES = ("sync", "tensor", "vector", "scalar", "gpsimd")

QPOOL = {"sync": 6, "gpsimd": 5, "scalar": 4}
QSEMS = {e: [f"q_{e}{i}" for i in range(n)] for e, n in QPOOL.items()}


class Sched:
    def __init__(self, sems, counts, waited, qrot):
        self.sems = sems
        self.cnt = counts
        self.waited = waited
        self.qrot = qrot
        self.ops = {e: [] for e in ENGINES}

    def op(self, eng, fn, waits=(), inc=None, inc_by=1, asynchronous=False,
           drain=False):
        self.ops[eng].append((fn, [w for w in waits if w is not None],
                              (inc, inc_by) if inc else None, drain))
        if inc:
            self.cnt[inc] += inc_by
            return (eng, inc, self.cnt[inc], asynchronous)
        return None

    def dma(self, eng, fn, waits=(), sem=None):
        pool = QSEMS[eng]
        s = pool[self.qrot[eng] % len(pool)]
        self.qrot[eng] += 1
        ws = list(waits)
        if self.cnt[s]:
            ws.append((eng, s, self.cnt[s], True))
        return self.op(eng, fn, ws, inc=s, inc_by=16, asynchronous=True)

    def queue_drained(self, eng):
        return [(eng, s, self.cnt[s], True) for s in QSEMS[eng] if self.cnt[s]]

    def emit(self, block):
        for name in ENGINES:
            ops = self.ops[name]
            if not ops:
                continue
            waited = self.waited.setdefault(name, {})
            sems = self.sems

            def body(eng, ops=ops, waited=waited, name=name):
                for fn, waits, inc, drain in ops:
                    for (peng, sname, val, is_async) in waits:
                        if peng == name and not is_async and name != "tensor":
                            continue
                        if waited.get(sname, -1) >= val:
                            continue
                        eng.wait_ge(sems[sname], val)
                        waited[sname] = val
                    inst = fn(eng)
                    if drain:
                        inst = eng.drain()
                    if inc is not None:
                        inst.then_inc(sems[inc[0]], inc[1])

            getattr(block, name)(body)


def build_program():
    nc = bass.Bass("TRN2", target_bir_lowering=False)

    x_h = nc.dram_tensor("x", [T, D], F32, kind="ExternalInput").ap()
    wqT_h = nc.dram_tensor("wqT", [D, D], F32, kind="ExternalInput").ap()
    wkT_h = nc.dram_tensor("wkT", [D, D], F32, kind="ExternalInput").ap()
    wvT_h = nc.dram_tensor("wvT", [D, D], F32, kind="ExternalInput").ap()
    woT_h = nc.dram_tensor("woT", [D, D], F32, kind="ExternalInput").ap()
    wf_h = nc.dram_tensor("wf", [D, D], F32, kind="ExternalInput").ap()
    gam_h = nc.dram_tensor("gam", [1, D], F32, kind="ExternalInput").ap()
    bet_h = nc.dram_tensor("bet", [1, D], F32, kind="ExternalInput").ap()
    maskT_h = nc.dram_tensor("maskT", [128, 128], BF16, kind="ExternalInput").ap()
    ones8_h = nc.dram_tensor("ones8", [128, 8], BF16, kind="ExternalInput").ap()
    out_h = nc.dram_tensor("out", [T, D], F32, kind="ExternalOutput").ap()

    Qs = nc.dram_tensor("Qs", [D, T], BF16, kind="Internal").ap()
    Ks = nc.dram_tensor("Ks", [D, T], BF16, kind="Internal").ap()
    Vs = nc.dram_tensor("Vs", [T, D], BF16, kind="Internal").ap()
    X1s = nc.dram_tensor("X1s", [T, D], F32, kind="Internal").ap()
    Xns = nc.dram_tensor("Xns", [D, T], BF16, kind="Internal").ap()
    hbounce = nc.dram_tensor("hbounce", [1, 128], F32, kind="Internal").ap()
    smd = nc.dram_tensor("smd", [1, 8], F32, kind="Internal").ap()
    cc_in = nc.dram_tensor("cc_in", [1, 512], F32, kind="Internal")
    cc_out = nc.dram_tensor("cc_out", [1, 512], F32, kind="Internal")

    with ExitStack() as P:
        def sb(name, shape, dtype, stack):
            return stack.enter_context(nc.sbuf_tensor(name, shape, dtype))

        maskT = sb("maskT_s", [128, 128], BF16, P)
        ones8 = sb("ones8_s", [128, 8], BF16, P)
        gam_bc = sb("gam_bc", [128, D], F32, P)
        bet_bc = sb("bet_bc", [128, D], F32, P)
        w_tern = sb("w_tern", [128, DC * D], BF16, P)
        scal = sb("scal", [128, 8], F32, P)
        sm = sb("sm", [1, 8], F32, P)
        cc_sb = sb("cc_sb", [1, 512], F32, P)
        redrow = sb("redrow", [1, 128], F32, P)
        epsc = sb("epsc", [128, 2], F32, P)
        absm = sb("absm", [128, 72], F32, P)

        ps_a = P.enter_context(nc.psum_tensor("ps_a", [128, 1024], F32))
        ps_b = P.enter_context(nc.psum_tensor("ps_b", [128, 1024], F32))
        ps_s0 = P.enter_context(nc.psum_tensor("ps_s0", [128, 256], F32))
        ps_s1 = P.enter_context(nc.psum_tensor("ps_s1", [128, 256], F32))
        ps_s = (ps_s0, ps_s1)
        ps_d0 = P.enter_context(nc.psum_tensor("ps_d0", [128, 8], F32))
        ps_d1 = P.enter_context(nc.psum_tensor("ps_d1", [128, 8], F32))
        ps_dd = (ps_d0, ps_d1)

        semnames = (["pe", "dv", "ac", "po", "cc", "bar"]
                    + [s for pool in QSEMS.values() for s in pool])
        sems = {n: P.enter_context(nc.semaphore(n)) for n in semnames}
        counts = {n: 0 for n in semnames}
        waited = {}
        qrot = {e: 0 for e in QPOOL}

        with ExitStack() as PA:
            wqT_b = sb("wqT_b", [128, DC * D], BF16, PA)
            wkT_b = sb("wkT_b", [128, DC * D], BF16, PA)
            wvT_b = sb("wvT_b", [128, DC * D], BF16, PA)
            woT_b = sb("woT_b", [128, DC * D], BF16, PA)
            V_fm = sb("V_fm", [128, DC * D], BF16, PA)
            stg = sb("stg", [128, 2 * D], F32, PA)
            x_in = sb("x_in", [128, 2 * D], F32, PA)
            sq_scr = sb("sq_scr", [128, 2 * D], F32, PA)
            stats = sb("stats", [128, 32], F32, PA)
            xn_tm = sb("xn_tm", [128, 2 * D], BF16, PA)
            xn_fm = sb("xn_fm", [128, 2 * DC * D], BF16, PA)
            pstg = sb("pstg", [128, 4 * D], BF16, PA)
            abss = sb("abss", [128, 9], F32, PA)
            zs = sb("zs", [128, D], F32, PA)
            z2 = sb("z2", [128, D], F32, PA)

            S = Sched(sems, counts, waited, qrot)

            t_mask = S.dma("sync", lambda e: e.dma_start(out=maskT[:], in_=maskT_h))
            t_ones = S.dma("sync", lambda e: e.dma_start(out=ones8[:], in_=ones8_h))
            S.dma("sync", lambda e: e.dma_start(
                out=gam_bc[:], in_=gam_h.partition_broadcast(128)))
            S.dma("sync", lambda e: e.dma_start(
                out=bet_bc[:], in_=bet_h.partition_broadcast(128)))
            S.op("vector", lambda e: e.memset(epsc[:, 0:1], 1e-5))
            t_eps = S.op("vector", lambda e: e.memset(epsc[:, 1:2], 1e-8),
                         inc="dv", drain=True)

            stg_free = [None, None]
            for wi, (src, dst, scale) in enumerate([
                    (wqT_h, wqT_b, 1.0), (wkT_h, wkT_b, 1.0 / 32.0),
                    (wvT_h, wvT_b, 1.0), (woT_h, woT_b, 1.0)]):
                for c in range(DC):
                    slot = (wi * DC + c) % 2
                    tl = S.dma("sync", lambda e, src=src, c=c, slot=slot: e.dma_start(
                        out=stg[:, slot * D:(slot + 1) * D],
                        in_=src[c * 128:(c + 1) * 128, :]),
                        waits=[stg_free[slot]])
                    stg_free[slot] = S.op(
                        "vector", lambda e, dst=dst, c=c, slot=slot, scale=scale:
                        e.tensor_scalar(out=dst[:, c * D:(c + 1) * D],
                                        in0=stg[:, slot * D:(slot + 1) * D],
                                        scalar1=scale, scalar2=None, op0=ALU.mult),
                        waits=[tl], inc="dv", drain=True)
            t_wcast = stg_free[1]

            for c in range(DC):
                slot = c % 2
                tl = S.dma("sync", lambda e, c=c, slot=slot: e.dma_start(
                    out=stg[:, slot * D:(slot + 1) * D],
                    in_=wf_h[c * 128:(c + 1) * 128, :]), waits=[stg_free[slot]])
                stg_free[slot] = S.op("vector", lambda e, c=c, slot=slot: e.tensor_reduce(
                    out=abss[:, c:c + 1], in_=stg[:, slot * D:(slot + 1) * D],
                    axis=AX, op=ALU.add, apply_absolute_value=True),
                    waits=[tl], inc="dv", drain=True)
            t_rs = S.op("vector", lambda e: e.tensor_reduce(
                out=abss[:, 8:9], in_=abss[:, 0:8], axis=AX, op=ALU.add),
                waits=[stg_free[0], stg_free[1]], inc="dv", drain=True)
            t_hb = S.dma("gpsimd", lambda e: e.dma_start(
                out=hbounce, in_=abss[:, 8:9], allow_slow_non_contiguous=True),
                waits=[t_rs], sem="st")
            t_rr = S.dma("sync", lambda e: e.dma_start(out=redrow[:], in_=hbounce),
                         waits=[t_hb])
            S.op("vector", lambda e: e.tensor_reduce(
                out=sm[0:1, 4:5], in_=redrow[:], axis=AX, op=ALU.add),
                waits=[t_rr], drain=True)
            S.op("vector", lambda e: e.tensor_scalar(
                out=sm[0:1, 1:2], in0=sm[0:1, 4:5], scalar1=1.0 / (D * D),
                scalar2=None, op0=ALU.mult), drain=True)
            t_thr = S.op("vector", lambda e: e.tensor_scalar(
                out=sm[0:1, 0:1], in0=sm[0:1, 1:2], scalar1=1e-5, scalar2=0.5,
                op0=ALU.add, op1=ALU.mult), inc="dv", drain=True)
            t_smd = S.dma("gpsimd", lambda e: e.dma_start(out=smd[0:1, 0:2],
                                                          in_=sm[0:1, 0:2]),
                          waits=[t_thr], sem="st")
            t_thrb = S.dma("sync", lambda e: e.dma_start(
                out=scal[:, 0:2], in_=smd[0:1, 0:2].partition_broadcast(128),
                allow_slow_non_contiguous=True), waits=[t_smd])
            t_nthr = S.op("vector", lambda e: e.tensor_scalar(
                out=scal[:, 6:7], in0=scal[:, 0:1], scalar1=-1.0, scalar2=None,
                op0=ALU.mult), waits=[t_thrb], inc="dv", drain=True)
            for c in range(DC):
                slot = c % 2
                tl = S.dma("sync", lambda e, c=c, slot=slot: e.dma_start(
                    out=stg[:, slot * D:(slot + 1) * D],
                    in_=wf_h[c * 128:(c + 1) * 128, :]), waits=[stg_free[slot]])
                S.op("vector", lambda e, slot=slot: e.tensor_scalar(
                    out=zs[:], in0=stg[:, slot * D:(slot + 1) * D],
                    scalar1=scal[:, 0:1], scalar2=None, op0=ALU.is_ge),
                    waits=[tl, t_nthr])
                S.op("vector", lambda e, slot=slot: e.tensor_scalar(
                    out=z2[:], in0=stg[:, slot * D:(slot + 1) * D],
                    scalar1=scal[:, 6:7], scalar2=None, op0=ALU.is_le),
                    drain=True)
                stg_free[slot] = S.op("vector", lambda e, c=c: e.tensor_tensor(
                    out=w_tern[:, c * D:(c + 1) * D], in0=zs[:], in1=z2[:],
                    op=ALU.subtract), inc="dv", drain=True)

            x_free = [None, None]
            xntm_free = [None, None]
            sq_free = [None, None]
            psAB_free = {0: None, 1: None}
            pstg_free = [None] * 4
            xnfm_rd = [None, None]
            vfm_rd = None
            pr_ct = 0
            xnfm_toks_by_s = {}

            def emit_ln(s, tl_):
                sf = (s % 2) * DC * D
                gt = s * 8 + tl_
                pb = gt % 2
                base = pb * 16
                tx = S.dma("sync", lambda e, gt=gt, pb=pb: e.dma_start(
                    out=x_in[:, pb * D:(pb + 1) * D],
                    in_=x_h[gt * 128:(gt + 1) * 128, :]), waits=[x_free[pb]])
                tsm = S.op("vector", lambda e, pb=pb, base=base: e.tensor_reduce(
                    out=stats[:, base:base + 1], in_=x_in[:, pb * D:(pb + 1) * D],
                    axis=AX, op=ALU.add), waits=[tx], inc="dv", drain=True)
                tsq = S.op("scalar", lambda e, pb=pb, base=base: e.activation(
                    out=sq_scr[:, pb * D:(pb + 1) * D],
                    in_=x_in[:, pb * D:(pb + 1) * D], func=AF.Square),
                    waits=[tx, sq_free[pb]], inc="ac", drain=True)
                S.op("vector", lambda e, base=base: e.tensor_scalar(
                    out=stats[:, base + 2:base + 3], in0=stats[:, base:base + 1],
                    scalar1=1.0 / D, scalar2=None, op0=ALU.mult),
                    waits=[tsm], drain=True)
                S.op("vector", lambda e, base=base: e.tensor_tensor(
                    out=stats[:, base + 3:base + 4], in0=stats[:, base + 2:base + 3],
                    in1=stats[:, base + 2:base + 3], op=ALU.mult), drain=True)
                tss = S.op("vector", lambda e, pb=pb, base=base: e.tensor_reduce(
                    out=stats[:, base + 1:base + 2],
                    in_=sq_scr[:, pb * D:(pb + 1) * D],
                    axis=AX, op=ALU.add), waits=[tsq], inc="dv", drain=True)
                sq_free[pb] = tss
                tvar = S.op("vector", lambda e, base=base: e.scalar_tensor_tensor(
                    out=stats[:, base + 4:base + 5], in0=stats[:, base + 1:base + 2],
                    scalar=1.0 / D, in1=stats[:, base + 3:base + 4],
                    op0=ALU.mult, op1=ALU.subtract), waits=[tss],
                    inc="dv", drain=True)
                S.op("scalar", lambda e, base=base: e.activation(
                    out=stats[:, base + 5:base + 6], in_=stats[:, base + 4:base + 5],
                    func=AF.Ln, bias=epsc[:, 0:1], scale=1.0),
                    waits=[tvar, t_eps], drain=True)
                trstd = S.op("scalar", lambda e, base=base: e.activation(
                    out=stats[:, base + 6:base + 7], in_=stats[:, base + 5:base + 6],
                    func=AF.Exp, bias=0.0, scale=-0.5), inc="ac", drain=True)
                tneg = S.op("vector", lambda e, base=base: e.scalar_tensor_tensor(
                    out=stats[:, base + 7:base + 8], in0=stats[:, base + 2:base + 3],
                    scalar=-1.0, in1=stats[:, base + 6:base + 7],
                    op0=ALU.mult, op1=ALU.mult), waits=[trstd], inc="dv", drain=True)
                txn = S.op("scalar", lambda e, pb=pb, base=base: e.activation(
                    out=xn_tm[:, pb * D:(pb + 1) * D],
                    in_=x_in[:, pb * D:(pb + 1) * D], func=AF.Identity,
                    bias=stats[:, base + 7:base + 8],
                    scale=stats[:, base + 6:base + 7]),
                    waits=[tneg, xntm_free[pb]], inc="ac", drain=True)
                x_free[pb] = txn
                ev_last = None
                for c in range(DC):
                    ev = S.dma("sync", lambda e, pb=pb, c=c, tl_=tl_, sf=sf:
                               e.dma_start(
                        out=xn_fm[:, sf + c * D + tl_ * 128:
                                  sf + c * D + (tl_ + 1) * 128],
                        in_=xn_tm[:, pb * D + c * 128: pb * D + (c + 1) * 128],
                        transpose=True),
                        waits=[txn, xnfm_rd[s % 2]], sem="tr")
                    ev_last = ev
                xntm_free[pb] = ev_last
                xnfm_toks_by_s[s].append(ev_last)

            xnfm_toks_by_s[0] = []
            for tl_ in range(8):
                emit_ln(0, tl_)

            for s in range(NSA):
                sf = (s % 2) * DC * D
                xnfm_toks = xnfm_toks_by_s[s]
                if s + 1 < NSA:
                    xnfm_toks_by_s[s + 1] = []
                ln_state = {"next": 0, "ctr": 0}

                def hook(s=s, ln_state=ln_state):
                    ln_state["ctr"] += 1
                    if (s + 1 < NSA and ln_state["next"] < 8
                            and ln_state["ctr"] % 4 == 0):
                        emit_ln(s + 1, ln_state["next"])
                        ln_state["next"] += 1

                tok0 = s * 1024
                vfm_toks = []
                for oc in range(DC):
                    hook()
                    pg = pr_ct % 2
                    pr_ct += 1
                    psu = (ps_a, ps_b)[pg]
                    mm_last = None
                    for ic in range(DC):
                        for h in range(2):
                            def mmf(e, psu=psu, ic=ic, oc=oc, h=h, sf=sf,
                                    st=(ic == 0), sp=(ic == DC - 1)):
                                return e.matmul(
                                    out=psu[:, h * 512:(h + 1) * 512],
                                    lhsT=wvT_b[:, ic * D + oc * 128:
                                               ic * D + oc * 128 + 128],
                                    rhs=xn_fm[:, sf + ic * D + h * 512:
                                              sf + ic * D + h * 512 + 512],
                                    start=st, stop=sp)
                            w = ([psAB_free[pg], t_wcast] + xnfm_toks
                                 if (ic == 0 and h == 0) else [])
                            mm_last = S.op("tensor", mmf, waits=w,
                                           inc="pe" if (ic == DC - 1 and h == 1)
                                           else None,
                                           drain=(ic == DC - 1 and h == 1))
                    xnfm_rd[s % 2] = mm_last
                    tev = S.op("vector", lambda e, psu=psu, oc=oc: e.tensor_copy(
                        out=V_fm[:, oc * D:(oc + 1) * D], in_=psu[:]),
                        waits=[mm_last, vfm_rd], inc="dv", drain=True)
                    psAB_free[pg] = tev
                    vfm_toks.append(tev)
                for tt in range(8):
                    hook()
                    pg = pr_ct % 2
                    pst_i = pr_ct % 4
                    pr_ct += 1
                    psu = (ps_a, ps_b)[pg]
                    mm_last = None
                    for oc in range(DC):
                        for h in range(2):
                            def mmf(e, psu=psu, oc=oc, tt=tt, h=h,
                                    st=(oc == 0), sp=(oc == DC - 1)):
                                return e.matmul(
                                    out=psu[:, h * 512:(h + 1) * 512],
                                    lhsT=V_fm[:, oc * D + tt * 128:
                                              oc * D + tt * 128 + 128],
                                    rhs=woT_b[:, oc * D + h * 512:
                                              oc * D + h * 512 + 512],
                                    start=st, stop=sp)
                            w = ([psAB_free[pg]] + vfm_toks
                                 if (oc == 0 and h == 0) else [])
                            mm_last = S.op("tensor", mmf, waits=w,
                                           inc="pe" if (oc == DC - 1 and h == 1)
                                           else None,
                                           drain=(oc == DC - 1 and h == 1))
                    vfm_rd = mm_last
                    tev = S.op("vector", lambda e, psu=psu, pst_i=pst_i:
                               e.tensor_copy(
                        out=pstg[:, pst_i * D:(pst_i + 1) * D], in_=psu[:]),
                        waits=[mm_last, pstg_free[pst_i]], inc="dv", drain=True)
                    psAB_free[pg] = tev
                    tst = S.dma("gpsimd",
                                lambda e, tt=tt, tok0=tok0, pst_i=pst_i:
                                e.dma_start(
                                    out=Vs[tok0 + tt * 128:
                                           tok0 + (tt + 1) * 128, :],
                                    in_=pstg[:, pst_i * D:(pst_i + 1) * D]),
                                waits=[tev])
                    pstg_free[pst_i] = tst
                for (wb, dstT) in ((wkT_b, Ks), (wqT_b, Qs)):
                    for oc in range(DC):
                        hook()
                        pg = pr_ct % 2
                        pst_i = pr_ct % 4
                        pr_ct += 1
                        psu = (ps_a, ps_b)[pg]
                        mm_last = None
                        for ic in range(DC):
                            for h in range(2):
                                def mmf(e, wb=wb, psu=psu, ic=ic, oc=oc, h=h,
                                        sf=sf, st=(ic == 0), sp=(ic == DC - 1)):
                                    return e.matmul(
                                        out=psu[:, h * 512:(h + 1) * 512],
                                        lhsT=wb[:, ic * D + oc * 128:
                                                ic * D + oc * 128 + 128],
                                        rhs=xn_fm[:, sf + ic * D + h * 512:
                                                  sf + ic * D + h * 512 + 512],
                                        start=st, stop=sp)
                                w = ([psAB_free[pg], t_wcast] + xnfm_toks
                                     if (ic == 0 and h == 0) else [])
                                mm_last = S.op("tensor", mmf, waits=w,
                                               inc="pe" if (ic == DC - 1 and h == 1)
                                               else None,
                                               drain=(ic == DC - 1 and h == 1))
                        xnfm_rd[s % 2] = mm_last
                        tev = S.op("vector", lambda e, psu=psu, pst_i=pst_i:
                                   e.tensor_copy(
                            out=pstg[:, pst_i * D:(pst_i + 1) * D], in_=psu[:]),
                            waits=[mm_last, pstg_free[pst_i]], inc="dv", drain=True)
                        psAB_free[pg] = tev
                        tst = S.dma("gpsimd",
                                    lambda e, dstT=dstT, oc=oc, tok0=tok0,
                                    pst_i=pst_i: e.dma_start(
                                        out=dstT[oc * 128:(oc + 1) * 128,
                                                 tok0:tok0 + 1024],
                                        in_=pstg[:, pst_i * D:(pst_i + 1) * D]),
                                    waits=[tev])
                        pstg_free[pst_i] = tst

            S.op("tensor", lambda e: e.nop(), inc="bar", drain=True)
            S.op("vector", lambda e: e.nop(), inc="bar", drain=True)
            S.op("scalar", lambda e: e.nop(), inc="bar", drain=True)
            S.op("gpsimd", lambda e: e.nop(),
                 waits=S.queue_drained("gpsimd"), inc="bar")
            bar_a = counts["bar"]
            psAB_last = dict(psAB_free)

            blockA = PA.enter_context(nc.Block("phaseA"))
            S.emit(blockA)

        with ExitStack() as PB:
            qbuf = sb("qbuf", [128, 2 * DC * 256], BF16, PB)
            kst = sb("kst", [128, 2 * DC * 512], BF16, PB)
            vst = sb("vst", [128, 2 * 4 * D], BF16, PB)
            pt = sb("pt", [128, 4 * 256], BF16, PB)
            xres = sb("xres", [128, 2 * D], F32, PB)
            x1st = sb("x1st", [128, 4 * D], F32, PB)
            den = sb("den", [128, 32], F32, PB)
            bst = sb("bst", [128, 32], F32, PB)
            sqb = sb("sqb", [128, 2 * D], F32, PB)
            xhb = sb("xhb", [128, 2 * D], F32, PB)
            xnb = sb("xnb", [128, 2 * D], BF16, PB)
            xnf = sb("xnf", [128, 2 * D], BF16, PB)

            S = Sched(sems, counts, waited, qrot)
            barw = [("sync", "bar", bar_a, True)]

            qb_free = [None, None]
            kst_free = [None, None]
            vst_free = [None, None]
            pt_free = [None] * 4
            psS_free = [None, None]
            psC_free = {0: psAB_last[0], 1: psAB_last[1]}
            psD_free = [None, None]
            xres_free = [None, None]
            x1_free = [[], [], [], []]
            deferred = []
            bst_free = [None] * 4
            sqb_free = [None, None]
            xhb_free = [None, None]
            xnb_free = [None, None]
            xnf_free = [None, None]
            sc = 0
            ptn = 0
            for jq in range(NSB):
                qs_ = jq % 2
                tq = S.dma("sync", lambda e, jq=jq, qs_=qs_: e.dma_start(
                    out=qbuf[:, qs_ * DC * 256:(qs_ + 1) * DC * 256]
                        .rearrange("p (c n) -> p c n", n=256),
                    in_=Qs.rearrange("(c p) m -> p c m", p=128)[
                        :, :, jq * 256:(jq + 1) * 256]),
                    waits=barw + [qb_free[qs_]])
                txr = [None, None]
                for qt in (0, 1):
                    g = 2 * jq + qt
                    txr[qt] = S.dma("sync", lambda e, g=g, qt=qt: e.dma_start(
                        out=xres[:, qt * D:(qt + 1) * D],
                        in_=x_h[g * 128:(g + 1) * 128, :]),
                        waits=barw + [xres_free[qt]])
                nkb = 2 * jq + 2
                nch = (nkb + 3) // 4
                pv_last = {0: None, 1: None}
                dn_last = {0: None, 1: None}
                pend = None
                last_sc = None

                def issue_pv(info):
                    rd = None
                    for (qt, lo) in info["cols"]:
                        kg = info["kg"]
                        psu = (ps_a, ps_b)[qt]
                        first = (kg == 0)
                        last = (kg == 2 * jq + qt)
                        mm2 = None
                        for h in range(2):
                            def mmf(e, psu=psu, lo=lo, ksv=info["ks"],
                                    kb=info["kb"], h=h, st=first, sp=last):
                                return e.matmul(
                                    out=psu[:, h * 512:(h + 1) * 512],
                                    lhsT=pt[:, lo:lo + 128],
                                    rhs=vst[:, ksv * 4096 + kb * 1024 + h * 512:
                                            ksv * 4096 + kb * 1024 + h * 512 + 512],
                                    start=st, stop=sp)
                            w_ = (([info["ttok"], info["tv"]]
                                   + ([psC_free[qt]] if first else []))
                                  if h == 0 else [])
                            mm2 = S.op("tensor", mmf, waits=w_,
                                       inc="pe" if h == 1 else None,
                                       drain=(h == 1 and last))
                        pv_last[qt] = mm2
                        vst_free[info["ks"]] = mm2

                        def mmd(e, lo=lo, qt=qt, st=first, sp=last):
                            return e.matmul(out=ps_dd[qt][:, 0:8],
                                            lhsT=pt[:, lo:lo + 128],
                                            rhs=ones8[:], start=st, stop=sp)
                        tdm = S.op("tensor", mmd,
                                   waits=([t_ones, psD_free[qt]]
                                          if first else []),
                                   inc="pe" if last else None, drain=last)
                        dn_last[qt] = tdm
                        rd = tdm
                    pt_free[info["p"]] = rd

                for ch in range(nch):
                    ks_ = ch % 2
                    tk = S.dma("sync", lambda e, ch=ch, ks_=ks_: e.dma_start(
                        out=kst[:, ks_ * DC * 512:(ks_ + 1) * DC * 512]
                            .rearrange("p (c n) -> p c n", n=512),
                        in_=Ks.rearrange("(c p) m -> p c m", p=128)[
                            :, :, ch * 512:(ch + 1) * 512]),
                        waits=barw + [kst_free[ks_]])
                    tv = S.dma("gpsimd", lambda e, ch=ch, ks_=ks_: e.dma_start(
                        out=vst[:, ks_ * 4096:(ks_ + 1) * 4096]
                            .rearrange("p (t n) -> p t n", n=1024),
                        in_=Vs.rearrange("(t p) n -> p t n", p=128)[
                            :, ch * 4:(ch + 1) * 4, :]),
                        waits=barw + [vst_free[ks_]])
                    if ch >= 1 and deferred:
                        deferred.pop(0)()
                    for kb in range(4):
                        kg = ch * 4 + kb
                        if kg >= nkb:
                            break
                        narrow = (kg == nkb - 1)
                        c0, w = (128, 128) if narrow else (0, 256)
                        sb_ = sc % 2
                        sc += 1
                        mml = None
                        for dc in range(DC):
                            def mmf(e, dc=dc, sb_=sb_, ks_=ks_, kb=kb, qs_=qs_,
                                    c0=c0, w=w, st=(dc == 0), sp=(dc == DC - 1)):
                                return e.matmul(
                                    out=ps_s[sb_][:, 0:w],
                                    lhsT=kst[:, ks_ * DC * 512 + dc * 512 + kb * 128:
                                             ks_ * DC * 512 + dc * 512 + kb * 128 + 128],
                                    rhs=qbuf[:, qs_ * DC * 256 + dc * 256 + c0:
                                             qs_ * DC * 256 + dc * 256 + c0 + w],
                                    start=st, stop=sp)
                            w_ = [tk, tq, psS_free[sb_]] if dc == 0 else []
                            mml = S.op("tensor", mmf, waits=w_,
                                       inc="pe" if dc == DC - 1 else None,
                                       drain=(dc == DC - 1))
                        last_sc = mml
                        kst_free[ks_] = mml
                        p_ = ptn % 4
                        ptn += 1
                        texp = S.op("scalar", lambda e, p_=p_, sb_=sb_, w=w:
                                    e.activation(
                                        out=pt[:, p_ * 256:p_ * 256 + w],
                                        in_=ps_s[sb_][:, 0:w],
                                        func=AF.Exp, bias=0.0, scale=1.0),
                                    waits=[mml, pt_free[p_]], inc="ac", drain=True)
                        psS_free[sb_] = texp
                        ttok = texp
                        if kg >= nkb - 2:
                            mc0 = p_ * 256
                            ttok = S.op("vector", lambda e, mc0=mc0:
                                        e.tensor_tensor(
                                            out=pt[:, mc0:mc0 + 128],
                                            in0=pt[:, mc0:mc0 + 128],
                                            in1=maskT[:], op=ALU.mult),
                                        waits=[texp, t_mask], inc="dv", drain=True)
                        if narrow:
                            cols = [(1, p_ * 256)]
                        else:
                            cols = [(0, p_ * 256), (1, p_ * 256 + 128)]
                        cur = {"kg": kg, "kb": kb, "ks": ks_, "p": p_,
                               "ttok": ttok, "tv": tv, "cols": cols}
                        if pend is not None:
                            issue_pv(pend)
                        pend = cur
                if pend is not None:
                    issue_pv(pend)
                    pend = None
                qb_free[qs_] = last_sc
                while deferred:
                    deferred.pop(0)()

                S.op("vector", lambda e: e.tensor_copy(
                    out=den[:, 0:8], in_=ps_d0[:, 0:8]),
                    waits=[dn_last[0], dn_last[1]], drain=True)
                tde = S.op("vector", lambda e: e.tensor_copy(
                    out=den[:, 8:16], in_=ps_d1[:, 0:8]), inc="dv", drain=True)
                trc = S.op("vector", lambda e: e.reciprocal(
                    out=den[:, 16:32], in_=den[:, 0:16]), inc="dv", drain=True)
                psD_free[0] = tde
                psD_free[1] = tde
                for qt in (0, 1):
                    g = 2 * jq + qt
                    sl = g % 4
                    tep = S.op("vector", lambda e, qt=qt, sl=sl:
                               e.scalar_tensor_tensor(
                        out=x1st[:, sl * D:(sl + 1) * D],
                        in0=(ps_a, ps_b)[qt][:],
                        scalar=den[:, 16 + qt * 8:17 + qt * 8],
                        in1=xres[:, qt * D:(qt + 1) * D],
                        op0=ALU.mult, op1=ALU.add),
                        waits=[pv_last[qt], trc, txr[qt]] + x1_free[sl],
                        inc="dv", drain=True)
                    psC_free[qt] = tep
                    xres_free[qt] = tep
                    tsx = S.dma("gpsimd", lambda e, g=g, sl=sl: e.dma_start(
                        out=X1s[g * 128:(g + 1) * 128, :],
                        in_=x1st[:, sl * D:(sl + 1) * D]), waits=[tep])

                    def emit_stats(g=g, sl=sl, tep=tep, tsx=tsx):
                        sg = g % 2
                        base = (g % 4) * 8
                        S.op("vector", lambda e, sl=sl, base=base: e.tensor_reduce(
                            out=bst[:, base:base + 1],
                            in_=x1st[:, sl * D:(sl + 1) * D],
                            axis=AX, op=ALU.add), waits=[bst_free[g % 4]],
                            drain=True)
                        tsq = S.op("scalar", lambda e, sl=sl, sg=sg: e.activation(
                            out=sqb[:, sg * D:(sg + 1) * D],
                            in_=x1st[:, sl * D:(sl + 1) * D], func=AF.Square),
                            waits=[tep, sqb_free[sg]], inc="ac", drain=True)
                        S.op("vector", lambda e, base=base: e.tensor_scalar(
                            out=bst[:, base + 2:base + 3],
                            in0=bst[:, base:base + 1],
                            scalar1=1.0 / D, scalar2=None, op0=ALU.mult),
                            drain=True)
                        S.op("vector", lambda e, base=base: e.tensor_tensor(
                            out=bst[:, base + 3:base + 4],
                            in0=bst[:, base + 2:base + 3],
                            in1=bst[:, base + 2:base + 3], op=ALU.mult),
                            drain=True)
                        S.op("vector", lambda e, sg=sg, base=base: e.tensor_reduce(
                            out=bst[:, base + 1:base + 2],
                            in_=sqb[:, sg * D:(sg + 1) * D],
                            axis=AX, op=ALU.add), waits=[tsq], drain=True)
                        tvar = S.op("vector", lambda e, base=base:
                                    e.scalar_tensor_tensor(
                            out=bst[:, base + 4:base + 5],
                            in0=bst[:, base + 1:base + 2],
                            scalar=1.0 / D, in1=bst[:, base + 3:base + 4],
                            op0=ALU.mult, op1=ALU.subtract), inc="dv", drain=True)
                        S.op("scalar", lambda e, base=base: e.activation(
                            out=bst[:, base + 5:base + 6],
                            in_=bst[:, base + 4:base + 5],
                            func=AF.Ln, bias=epsc[:, 1:2], scale=1.001),
                            waits=[tvar], drain=True)
                        tse = S.op("scalar", lambda e, base=base: e.activation(
                            out=bst[:, base + 6:base + 7],
                            in_=bst[:, base + 5:base + 6],
                            func=AF.Exp, bias=0.0, scale=-0.5), inc="ac",
                            drain=True)
                        tng = S.op("vector", lambda e, base=base:
                                   e.scalar_tensor_tensor(
                            out=bst[:, base + 7:base + 8],
                            in0=bst[:, base + 2:base + 3],
                            scalar=-1.0, in1=bst[:, base + 6:base + 7],
                            op0=ALU.mult, op1=ALU.mult), waits=[tse],
                            inc="dv", drain=True)
                        txh = S.op("scalar", lambda e, sl=sl, sg=sg, base=base:
                                   e.activation(
                            out=xhb[:, sg * D:(sg + 1) * D],
                            in_=x1st[:, sl * D:(sl + 1) * D], func=AF.Identity,
                            bias=bst[:, base + 7:base + 8],
                            scale=bst[:, base + 6:base + 7]),
                            waits=[tng, xhb_free[sg]], inc="ac", drain=True)
                        tg1 = S.op("vector", lambda e, sg=sg: e.tensor_tensor(
                            out=sqb[:, sg * D:(sg + 1) * D],
                            in0=xhb[:, sg * D:(sg + 1) * D], in1=gam_bc[:],
                            op=ALU.mult), waits=[txh], drain=True)
                        txn_ = S.op("vector", lambda e, sg=sg: e.tensor_tensor(
                            out=xnb[:, sg * D:(sg + 1) * D],
                            in0=sqb[:, sg * D:(sg + 1) * D], in1=bet_bc[:],
                            op=ALU.add), waits=[xnb_free[sg]], inc="dv",
                            drain=True)
                        S.op("vector", lambda e, sg=sg, g=g: e.tensor_reduce(
                            out=absm[:, g:g + 1], in_=xnb[:, sg * D:(sg + 1) * D],
                            axis=AX, op=ALU.max, apply_absolute_value=True),
                            drain=True)
                        trl = None
                        for c in range(DC):
                            trl = S.dma("sync", lambda e, sg=sg, c=c: e.dma_start(
                                out=xnf[:, sg * D + c * 128:
                                        sg * D + (c + 1) * 128],
                                in_=xnb[:, sg * D + c * 128:
                                        sg * D + (c + 1) * 128],
                                transpose=True),
                                waits=[txn_, xnf_free[sg]] if c == 0 else [])
                        tsn = S.dma("gpsimd", lambda e, g=g, sg=sg: e.dma_start(
                            out=Xns.rearrange("(c p) t -> p c t", p=128)[
                                :, :, g * 128:(g + 1) * 128],
                            in_=xnf[:, sg * D:(sg + 1) * D]
                                .rearrange("p (c t) -> p c t", t=128)),
                            waits=[trl])
                        bst_free[g % 4] = txh
                        sqb_free[sg] = txn_
                        xhb_free[sg] = tg1
                        xnb_free[sg] = trl
                        xnf_free[sg] = tsn
                        x1_free[sl] = [tsx, txh]

                    deferred.append(emit_stats)
            while deferred:
                deferred.pop(0)()

            S.op("tensor", lambda e: e.nop(), inc="bar", drain=True)
            S.op("vector", lambda e: e.nop(), inc="bar", drain=True)
            S.op("scalar", lambda e: e.nop(), inc="bar", drain=True)
            S.op("gpsimd", lambda e: e.nop(),
                 waits=S.queue_drained("gpsimd"), inc="bar")
            bar_b = counts["bar"]

            blockB = PB.enter_context(nc.Block("phaseB"))
            S.emit(blockB)

        with ExitStack() as PC:
            x1_in = sb("x1_in", [128, 2 * D], F32, PC)
            xnld = sb("xnld", [128, 2 * D], BF16, PC)
            xq_bf = sb("xq_bf", [128, 2 * D], BF16, PC)
            y_out = sb("y_out", [128, 2 * D], F32, PC)

            S = Sched(sems, counts, waited, qrot)
            barw = [("sync", "bar", bar_b, True)]

            S.op("vector", lambda e: e.memset(cc_sb[:], 0.0), waits=barw,
                 inc="dv", drain=True)

            x1_free = [None, None]

            tam = S.op("vector", lambda e: e.tensor_reduce(
                out=absm[:, 64:65], in_=absm[:, 0:64], axis=AX, op=ALU.max),
                inc="dv", drain=True)
            t_hb2 = S.dma("gpsimd", lambda e: e.dma_start(
                out=hbounce, in_=absm[:, 64:65], allow_slow_non_contiguous=True),
                waits=[tam], sem="st")
            t_rr2 = S.dma("sync", lambda e: e.dma_start(out=redrow[:], in_=hbounce),
                          waits=[t_hb2])
            tred = S.op("vector", lambda e: e.tensor_reduce(
                out=cc_sb[0:1, 0:1], in_=redrow[:], axis=AX, op=ALU.max),
                waits=[t_rr2], inc="dv", drain=True)
            tcc1 = S.dma("gpsimd", lambda e: e.dma_start(out=cc_in[:, :], in_=cc_sb[:]),
                         waits=[tred], sem="st")
            tcol = S.op("gpsimd", lambda e: e.collective_compute(
                "AllReduce", ALU.max, replica_groups=[list(range(8))],
                ins=[cc_in.ap().opt()], outs=[cc_out.ap().opt()]),
                waits=[tcc1], inc="cc", asynchronous=True)
            t_gbc = S.dma("sync", lambda e: e.dma_start(
                out=scal[:, 2:3], in_=cc_out.ap()[0:1, 0:1].partition_broadcast(128),
                allow_slow_non_contiguous=True), waits=[tcol])
            S.op("vector", lambda e: e.reciprocal(out=scal[:, 5:6], in_=scal[:, 2:3]),
                 waits=[t_gbc], drain=True)
            ts1 = S.op("vector", lambda e: e.tensor_scalar(
                out=scal[:, 3:4], in0=scal[:, 5:6], scalar1=128.0, scalar2=None,
                op0=ALU.mult), inc="dv", drain=True)
            ts2 = S.op("vector", lambda e: e.scalar_tensor_tensor(
                out=scal[:, 4:5], in0=scal[:, 2:3], scalar=1.0 / 128.0,
                in1=scal[:, 1:2], op0=ALU.mult, op1=ALU.mult), inc="dv", drain=True)

            xq_free = [None, None]
            yout_free = [None, None]
            psC_free = {0: None, 1: None}
            xnld_free = [None, None]
            for t in range(NT):
                pb = t % 2
                tx = S.dma("sync", lambda e, t=t, pb=pb: e.dma_start(
                    out=x1_in[:, pb * D:(pb + 1) * D],
                    in_=X1s[t * 128:(t + 1) * 128, :]),
                    waits=barw + [x1_free[pb]])
                txnl = S.dma("sync", lambda e, t=t, pb=pb: e.dma_start(
                    out=xnld[:, pb * D:(pb + 1) * D]
                        .rearrange("p (c m) -> p c m", m=128),
                    in_=Xns.rearrange("(c p) m -> p c m", p=128)[
                        :, :, t * 128:(t + 1) * 128]),
                    waits=barw + [xnld_free[pb]])
                tq1 = S.op("vector", lambda e, pb=pb: e.tensor_scalar(
                    out=xq_bf[:, pb * D:(pb + 1) * D],
                    in0=xnld[:, pb * D:(pb + 1) * D],
                    scalar1=scal[:, 3:4], scalar2=CLIP, op0=ALU.mult, op1=ALU.min),
                    waits=[ts1, txnl, xq_free[pb]])
                tq2 = S.op("vector", lambda e, pb=pb: e.tensor_scalar(
                    out=xq_bf[:, pb * D:(pb + 1) * D],
                    in0=xq_bf[:, pb * D:(pb + 1) * D],
                    scalar1=-CLIP, scalar2=None,
                    op0=ALU.max), inc="dv", drain=True)
                pg = t % 2
                psu = (ps_a, ps_b)[pg]
                mml = None
                for ic in range(DC):
                    for h in range(2):
                        def mmf(e, psu=psu, pb=pb, ic=ic, h=h,
                                st=(ic == 0), sp=(ic == DC - 1)):
                            return e.matmul(
                                out=psu[:, h * 512:(h + 1) * 512],
                                lhsT=xq_bf[:, pb * D + ic * 128:
                                           pb * D + (ic + 1) * 128],
                                rhs=w_tern[:, ic * D + h * 512: ic * D + h * 512 + 512],
                                start=st, stop=sp)
                        w = [tq2, psC_free[pg]] if ic == 0 and h == 0 else []
                        mml = S.op("tensor", mmf, waits=w,
                                   inc="pe" if (ic == DC - 1 and h == 1) else None,
                                   drain=(ic == DC - 1 and h == 1))
                xq_free[pb] = mml
                xnld_free[pb] = tq2
                tep = S.op("vector", lambda e, psu=psu, pb=pb: e.scalar_tensor_tensor(
                    out=y_out[:, pb * D:(pb + 1) * D], in0=psu[:],
                    scalar=scal[:, 4:5], in1=x1_in[:, pb * D:(pb + 1) * D],
                    op0=ALU.mult, op1=ALU.add),
                    waits=[mml, tx, ts2, yout_free[pb]], inc="dv", drain=True)
                psC_free[pg] = tep
                x1_free[pb] = tep
                yout_free[pb] = S.dma("gpsimd", lambda e, t=t, pb=pb: e.dma_start(
                    out=out_h[t * 128:(t + 1) * 128, :],
                    in_=y_out[:, pb * D:(pb + 1) * D]), waits=[tep], sem="st")

            S.op("gpsimd", lambda e: e.nop(), waits=S.queue_drained("gpsimd"))

            blockC = PC.enter_context(nc.Block("phaseC"))
            S.emit(blockC)

    return nc


_CONSTS = None


def _consts():
    global _CONSTS
    if _CONSTS is None:
        mt = np.triu(np.ones((128, 128), dtype=np.float32), 0)
        _CONSTS = (mt.astype(ml_dtypes.bfloat16),
                   np.ones((128, 8), dtype=ml_dtypes.bfloat16))
    return _CONSTS


def kernel(L0_attn_in_features, L0_attn_w_q, L0_attn_w_k, L0_attn_w_v,
           L0_attn_w_o, L0_ffn_weight, L0_ffn_ln_gamma, L0_ffn_ln_beta):
    x = np.ascontiguousarray(np.asarray(L0_attn_in_features, dtype=np.float32))
    wqT = np.ascontiguousarray(np.asarray(L0_attn_w_q, dtype=np.float32).T)
    wkT = np.ascontiguousarray(np.asarray(L0_attn_w_k, dtype=np.float32).T)
    wvT = np.ascontiguousarray(np.asarray(L0_attn_w_v, dtype=np.float32).T)
    woT = np.ascontiguousarray(np.asarray(L0_attn_w_o, dtype=np.float32).T)
    wf = np.ascontiguousarray(np.asarray(L0_ffn_weight, dtype=np.float32))
    gam = np.ascontiguousarray(np.asarray(L0_ffn_ln_gamma, dtype=np.float32).reshape(1, D))
    bet = np.ascontiguousarray(np.asarray(L0_ffn_ln_beta, dtype=np.float32).reshape(1, D))
    maskT, ones8 = _consts()

    nc = build_program()
    in_maps = []
    for c in range(8):
        in_maps.append({
            "x": np.ascontiguousarray(x[c]),
            "wqT": wqT, "wkT": wkT, "wvT": wvT, "woT": woT, "wf": wf,
            "gam": gam, "bet": bet, "maskT": maskT, "ones8": ones8,
        })
    res = run_bass_kernel_spmd(nc, in_maps, core_ids=list(range(8)))
    return np.stack([res.results[c]["out"] for c in range(8)], axis=0)
